# version 24
# baseline (speedup 1.0000x reference)
import contextlib
import math

import numpy as np
import ml_dtypes

import concourse.bass as bass
import concourse.mybir as mybir
from concourse.alu_op_type import AluOpType as ALU
from concourse.bass_utils import run_bass_kernel_spmd

IOA = bass.IndirectOffsetOnAxis

F32 = mybir.dt.float32
BF16 = mybir.dt.bfloat16
U32 = mybir.dt.uint32
AF = mybir.ActivationFunctionType
AX = mybir.AxisListType

D = 1024
NH = 16
DH = 64
I_FF = 1408
NE = 8
NCORES = 8

ENGINES = ("pe", "act", "dve", "pool", "sp")


def cdiv(a, b):
    return (a + b - 1) // b


class Prog:

    def __init__(self, nc):
        self.nc = nc
        self.per_eng = {e: [] for e in ENGINES}
        self.q_counts = {}
        self.writes = {}
        self.reads = {}
        self.waited = {e: {} for e in ENGINES}
        self.sems = {}
        self.es = None
        self.n_ops = 0

    def make_sems(self, es):
        self.es = es
        for e in ENGINES:
            self.sems[e] = es.enter_context(self.nc.semaphore(f"sem_{e}"))

    def _channel(self, q):
        if q not in self.q_counts:
            self.q_counts[q] = 0
            self.sems[q] = self.es.enter_context(self.nc.semaphore(f"sem_{q}"))

    @staticmethod
    def _foot(ap):
        t = ap.tensor
        name = t.name
        shape = list(t.shape)
        pitch = 1
        for s in shape[1:]:
            pitch *= s
        if len(shape) == 1:
            pitch = shape[0]
        off = ap.offset
        p0 = off // pitch
        c0 = off % pitch
        pn = 1
        cspan = 0
        for step, count in list(ap.ap):
            if count <= 1:
                continue
            if step > 0 and step % pitch == 0:
                pn += (count - 1) * (step // pitch)
            else:
                cspan += (count - 1) * abs(step)
        p1 = p0 + pn
        c1 = c0 + cspan + 1
        if "PSum" in type(t).__name__:
            esz = 2 if t.dtype == BF16 else 4
            bank = 2048 // esz
            c0 = (c0 // bank) * bank
            c1 = cdiv(c1, bank) * bank
        return (name, p0, p1, c0, c1)

    @staticmethod
    def _overlap(a, b):
        return a[1] < b[2] and b[1] < a[2] and a[3] < b[4] and b[3] < a[4]

    @staticmethod
    def _covers(a, b):
        return a[1] <= b[1] and a[2] >= b[2] and a[3] <= b[3] and a[4] >= b[4]

    def op(self, eng, fn, reads=(), writes=(), queue=None, tag=""):
        rf = [self._foot(a) for a in reads]
        wf = [self._foot(a) for a in writes]
        owner = queue if queue else eng
        edges = {}

        def add_edge(src, idx):
            if src == "pe" and owner == "pe":
                return
            if self.waited[eng].get(src, -1) >= idx:
                return
            if edges.get(src, -1) >= idx:
                return
            edges[src] = idx

        for f in rf:
            for w in self.writes.get(f[0], ()):
                if self._overlap(f, w["r"]):
                    add_edge(w["eng"], w["idx"])
        for f in wf:
            for w in self.writes.get(f[0], ()):
                if self._overlap(f, w["r"]):
                    add_edge(w["eng"], w["idx"])
            for r in self.reads.get(f[0], ()):
                if self._overlap(f, r["r"]):
                    add_edge(r["eng"], r["idx"])

        if queue:
            self._channel(queue)
            idx = self.q_counts[queue]
            self.q_counts[queue] += 1
        else:
            idx = self.n_ops
        self.n_ops += 1

        rec = {"fn": fn, "edges": edges, "queue": queue, "idx": idx,
               "inc": queue is not None, "tag": tag}
        self.per_eng[eng].append(rec)

        wd = self.waited[eng]
        for sKey, i in edges.items():
            if wd.get(sKey, -1) < i:
                wd[sKey] = i

        rec_eng = queue if queue else eng
        for f in wf:
            lst = self.writes.setdefault(f[0], [])
            lst[:] = [w for w in lst if not self._covers(f, w["r"])]
            lst.append({"r": f, "eng": rec_eng, "idx": idx})
            rl = self.reads.get(f[0])
            if rl:
                rl[:] = [r for r in rl if not self._covers(f, r["r"])]
        for f in rf:
            self.reads.setdefault(f[0], []).append(
                {"r": f, "eng": rec_eng, "idx": idx})
        return rec

    def barrier(self):
        lasts = {}
        for e in ENGINES:
            for rec in reversed(self.per_eng[e]):
                if rec["queue"] is None and rec["fn"] is not None:
                    lasts[e] = rec["idx"]
                    break
        for q, n in self.q_counts.items():
            if n:
                lasts[q] = n - 1
        for e in ENGINES:
            edges = {}
            for sKey, i in lasts.items():
                if sKey == e:
                    continue
                if self.waited[e].get(sKey, -1) >= i:
                    continue
                edges[sKey] = i
            if edges:
                self.per_eng[e].append({"fn": None, "edges": edges,
                                        "queue": None, "idx": None,
                                        "inc": False, "tag": "barrier"})
                for sKey, i in edges.items():
                    self.waited[e][sKey] = i

    def finalize(self):
        needed = {e: set() for e in ENGINES}
        for e in ENGINES:
            for rec in self.per_eng[e]:
                for sKey, i in rec["edges"].items():
                    if sKey in ENGINES:
                        needed[sKey].add(i)
        vals = {}
        for e in ENGINES:
            acc = 0
            v = {}
            for rec in self.per_eng[e]:
                if rec["queue"] is None and rec["fn"] is not None:
                    if rec["idx"] in needed[e]:
                        rec["inc"] = True
                        acc += 1
                    v[rec["idx"]] = acc
            vals[e] = v
        prog = self

        def emit(handle, eng):
            for rec in prog.per_eng[eng]:
                for sKey in sorted(rec["edges"], key=str):
                    i = rec["edges"][sKey]
                    if sKey in ENGINES:
                        handle.wait_ge(prog.sems[sKey], vals[sKey][i])
                    else:
                        handle.wait_ge(prog.sems[sKey], 16 * (i + 1))
                if rec["fn"] is None:
                    continue
                ins = rec["fn"](handle)
                if rec["queue"]:
                    ins.then_inc(prog.sems[rec["queue"]], 16)
                elif rec["inc"]:
                    if ins is None:
                        ins = handle.engine_nop() if hasattr(handle, "engine_nop") else handle.nop()
                    ins.then_inc(prog.sems[eng], 1)
            if eng == "sp":
                for q, n in sorted(prog.q_counts.items()):
                    if n:
                        handle.wait_ge(prog.sems[q], 16 * n)

        with prog.nc.Block() as block:
            @block.tensor
            def _(h):
                emit(h, "pe")

            @block.scalar
            def _(h):
                emit(h, "act")

            @block.vector
            def _(h):
                emit(h, "dve")

            @block.gpsimd
            def _(h):
                emit(h, "pool")

            @block.sync
            def _(h):
                emit(h, "sp")



def build_core(n_seqs=4, S=2048, n_experts=NE):
    KC = S // 128
    CAP = 768
    NJT = CAP // 128
    QHW = min(1024, S)
    NQH = S // QHW
    QGW = min(512, QHW)
    NQG = QHW // QGW
    NT = S // 128
    SKW = 2 * S - 128

    nc = bass.Bass("TRN2", target_bir_lowering=False)

    x_d = nc.dram_tensor("x", [n_seqs, S, D], F32, kind="ExternalInput").ap()
    wq_d = nc.dram_tensor("wq", [D, D], F32, kind="ExternalInput").ap()
    wk_d = nc.dram_tensor("wk", [D, D], F32, kind="ExternalInput").ap()
    wv_d = nc.dram_tensor("wv", [D, D], F32, kind="ExternalInput").ap()
    wo_d = nc.dram_tensor("wo", [D, D], F32, kind="ExternalInput").ap()
    wgt_d = nc.dram_tensor("wgt", [D, NE], F32, kind="ExternalInput").ap()
    w1t_d = nc.dram_tensor("w1t", [n_experts, D, I_FF], F32, kind="ExternalInput").ap()
    w2t_d = nc.dram_tensor("w2t", [n_experts, D, I_FF], F32, kind="ExternalInput").ap()
    w3t_d = nc.dram_tensor("w3t", [n_experts, I_FF, D], F32, kind="ExternalInput").ap()
    ws1t_d = nc.dram_tensor("ws1t", [D, I_FF], F32, kind="ExternalInput").ap()
    ws2t_d = nc.dram_tensor("ws2t", [D, I_FF], F32, kind="ExternalInput").ap()
    ws3t_d = nc.dram_tensor("ws3t", [I_FF, D], F32, kind="ExternalInput").ap()
    bsk_d = nc.dram_tensor("bsk", [NH, 128, SKW], F32, kind="ExternalInput").ap()
    triu_d = nc.dram_tensor("triu", [128, 128], BF16, kind="ExternalInput").ap()
    lane8_d = nc.dram_tensor("lane8", [128, NE], F32, kind="ExternalInput").ap()
    oobi_d = nc.dram_tensor("oobi", [NE * CAP], U32, kind="ExternalInput").ap()
    rowid_d = nc.dram_tensor("rowid", [128, 1], F32, kind="ExternalInput").ap()
    idb_d = nc.dram_tensor("idb", [128, 128], BF16, kind="ExternalInput").ap()
    idf_d = nc.dram_tensor("idf", [128, 128], F32, kind="ExternalInput").ap()
    out_d = nc.dram_tensor("out", [n_seqs, S, D], F32, kind="ExternalOutput").ap()
    xnd_d = nc.dram_tensor("xnd", [n_seqs, S, D], BF16).ap()
    cval_d = nc.dram_tensor("cvald", [n_seqs, NE * CAP], F32).ap()
    idx_d = nc.dram_tensor("idxd", [n_seqs, NE * CAP], U32).ap()
    ysc_d = nc.dram_tensor("yscd", [n_seqs, S + 128, D], F32).ap()

    es = contextlib.ExitStack()
    p = Prog(nc)
    p.make_sems(es)

    def sb(name, shape, dt, side=None):
        if side:
            return nc.sbuf_tensor(name, shape, dt, side=side)
        return nc.sbuf_tensor(name, shape, dt)

    idb = es.enter_context(sb("idb_s", [128, 128], BF16, side="right"))
    idf = es.enter_context(sb("idf_s", [128, 128], F32, side="right"))
    ones_f = es.enter_context(sb("ones_f", [1, 128], F32, side="right"))
    wgt_s = es.enter_context(sb("wgt_s", [128, NE * 8], BF16, side="right"))
    xn = es.enter_context(sb("xn", [128, 8 * S], BF16, side="right"))
    psA = es.enter_context(nc.psum_tensor("psA", [128, 1024], F32))
    psB = es.enter_context(nc.psum_tensor("psB", [128, 1024], F32))
    psC2 = es.enter_context(nc.psum_tensor("psC2", [128, 1024], F32))
    psE = es.enter_context(nc.psum_tensor("psE", [128, 512], F32))
    psT = es.enter_context(nc.psum_tensor("psT", [128, 1024], BF16))

    class BankView:

        def __init__(self, t, c0):
            self.t, self.c0 = t, c0

        def __getitem__(self, idx):
            r, c = idx
            cs = self.c0 + (c.start or 0)
            ce = self.c0 + c.stop
            return self.t[r, cs:ce]

    psC = BankView(psC2, 0)
    psD = BankView(psC2, 512)
    psA0 = BankView(psA, 0)
    psA1 = BankView(psA, 512)
    psB0 = BankView(psB, 0)
    psB1 = BankView(psB, 512)

    p.op("sp", lambda h: h.dma_start(idb[:, :], idb_d), writes=[idb[:, :]], queue="q_idb")
    p.op("sp", lambda h: h.dma_start(idf[:, :], idf_d), writes=[idf[:, :]], queue="q_idf")
    p.op("dve", lambda h: h.memset(ones_f[:, :], 1.0), writes=[ones_f[:, :]])

    def chunk_pair(dst, src_ap, base_off, n_ch, inner, src_row_pitch):
        t = src_ap.tensor
        d = bass.AP(dst, 0, [[n_ch * inner, 128], [inner, n_ch], [1, inner]])
        s_ = bass.AP(t, base_off,
                     [[src_row_pitch, 128], [128 * src_row_pitch, n_ch], [1, inner]])
        return d, s_

    def mm(out, lhsT, rhs, start, stop, tag=""):
        p.op("pe", lambda h: h.matmul(out, lhsT, rhs, start=start, stop=stop),
             reads=[lhsT, rhs], writes=[out], tag=tag)

    def transpose(out, in_, ident, tag=""):
        p.op("pe", lambda h: h.transpose(out, in_, ident),
             reads=[in_, ident], writes=[out], tag=tag)

    def act(out, in_, func, bias=None, scale=None, accum=None, tag=""):
        kw = {}
        if bias is not None:
            kw["bias"] = bias
        if scale is not None:
            kw["scale"] = scale
        if accum is not None:
            kw["accum_out"] = accum
        reads = [in_] + ([bias] if isinstance(bias, bass.AP) else [])
        wr = [out] + ([accum] if accum is not None else [])
        p.op("act", lambda h: h.activation(out, in_, func, **kw),
             reads=reads, writes=wr, tag=tag)

    def acopy(out, in_, tag=""):
        p.op("act", lambda h: h.copy(out, in_), reads=[in_], writes=[out], tag=tag)

    def tt(out, a, b, alu, tag=""):
        p.op("dve", lambda h: h.tensor_tensor(out, a, b, alu),
             reads=[a, b], writes=[out], tag=tag)

    def tsc(out, in_, s1, s2, op0, op1=None, tag=""):
        reads = [in_] + [q for q in (s1, s2) if isinstance(q, bass.AP)]
        if op1 is None:
            p.op("dve", lambda h: h.tensor_scalar(out, in_, s1, None, op0),
                 reads=reads, writes=[out], tag=tag)
        else:
            p.op("dve", lambda h: h.tensor_scalar(out, in_, s1, s2, op0, op1),
                 reads=reads, writes=[out], tag=tag)

    def stt(out, in0, scalar, in1, op0, op1, tag=""):
        p.op("dve", lambda h: h.scalar_tensor_tensor(out, in0, scalar, in1, op0, op1),
             reads=[in0, in1], writes=[out], tag=tag)

    def rmax(out, in_, tag=""):
        p.op("dve", lambda h: h.reduce_max(out, in_, axis=AX.X),
             reads=[in_], writes=[out], tag=tag)

    def recipo(out, in_, tag=""):
        p.op("dve", lambda h: h.reciprocal(out, in_),
             reads=[in_], writes=[out], tag=tag)

    _wgt_dst, _wgt_src = chunk_pair(wgt_s, wgt_d, 0, 8, NE, NE)
    p.op("pool", lambda h: h.dma_start(_wgt_dst, _wgt_src),
        writes=[wgt_s[:, :]], queue="q_wgt")
    triu = es.enter_context(sb("triu_s", [128, 128], BF16, side="right"))
    lane8 = es.enter_context(sb("lane8_s", [128, NE], F32, side="right"))
    rowid = es.enter_context(sb("rowid_s", [128, 1], F32, side="right"))
    zrow = es.enter_context(sb("zrow_s", [128, D], F32, side="right"))
    p.op("sp", lambda h: h.dma_start(triu[:, :], triu_d), writes=[triu[:, :]], queue="q_triu")
    p.op("sp", lambda h: h.dma_start(lane8[:, :], lane8_d), writes=[lane8[:, :]], queue="q_lane8")
    p.op("sp", lambda h: h.dma_start(rowid[:, :], rowid_d),
         writes=[rowid[:, :]], queue="q_rowid")
    onesc_bf = es.enter_context(sb("onesc_bf", [128, 1], BF16, side="right"))
    padid = es.enter_context(sb("padid_s", [128, 1], F32, side="right"))
    p.op("dve", lambda h: h.memset(zrow[:, :], 0.0), writes=[zrow[:, :]])
    p.op("dve", lambda h: h.memset(onesc_bf[:, :], 1.0), writes=[onesc_bf[:, :]])
    tsc(padid[:, 0:1], rowid[:, 0:1], float(S), None, ALU.add)
    rbS = nc.gpsimd.alloc_register("rbS")
    rbC = nc.gpsimd.alloc_register("rbC")
    p.op("pool", lambda h: h.reg_mov(rbS, S - 1))
    p.op("pool", lambda h: h.reg_mov(rbC, NE * CAP - 1))

    for s in range(n_seqs):
        with contextlib.ExitStack() as s_es:
            with contextlib.ExitStack() as ln_es:
                xst = [ln_es.enter_context(sb(f"xst{s}_{i}", [128, D], F32))
                       for i in range(2)]
                xnt = [ln_es.enter_context(sb(f"xnt{s}_{i}", [128, D], BF16))
                       for i in range(2)]
                bn6g = [ln_es.enter_context(sb(f"bn6_{s}_{i}", [128, 12], F32))
                        for i in range(2)]
                bn2g = [ln_es.enter_context(sb(f"bn2_{s}_{i}", [128, 2], F32))
                        for i in range(2)]
                nmg = [ln_es.enter_context(sb(f"nmean_{s}_{i}", [128, 1], F32))
                       for i in range(2)]
                rsg = [ln_es.enter_context(sb(f"rstd_{s}_{i}", [128, 1], F32))
                       for i in range(2)]
                for t in range(NT):
                    xb = xst[t % 2]
                    nb = xnt[t % 2]
                    bn6 = bn6g[t % 2]
                    bn2 = bn2g[t % 2]
                    nmean = nmg[t % 2]
                    rstd = rsg[t % 2]
                    p.op("sp", lambda h, t=t, xb=xb, s=s: h.dma_start(
                        xb[:, :], x_d[s, t * 128:(t + 1) * 128, :]),
                        writes=[xb[:, :]], queue=f"q_xst{t % 2}")
                    p.op("dve", lambda h, xb=xb, bn6=bn6: h.bn_stats(bn6[:, 0:6], xb[:, 0:512]),
                         reads=[xb[:, 0:512]], writes=[bn6[:, 0:6]])
                    p.op("dve", lambda h, xb=xb, bn6=bn6: h.bn_stats(bn6[:, 6:12], xb[:, 512:1024]),
                         reads=[xb[:, 512:1024]], writes=[bn6[:, 6:12]])
                    p.op("dve", lambda h, bn2=bn2, bn6=bn6: h.bn_aggr(bn2[:, :], bn6[:, :]),
                         reads=[bn6[:, :]], writes=[bn2[:, :]])
                    tsc(nmean[:, :], bn2[:, 0:1], -1.0, None, ALU.mult)
                    tsc(rstd[:, :], bn2[:, 1:2], 1e-5, None, ALU.add)
                    act(rstd[:, :], rstd[:, :], AF.Ln)
                    tsc(rstd[:, :], rstd[:, :], -0.5, None, ALU.mult)
                    act(rstd[:, :], rstd[:, :], AF.Exp)
                    tsc(nb[:, :], xb[:, :], nmean[:, 0:1], rstd[:, 0:1],
                        ALU.add, ALU.mult)
                    p.op("sp", lambda h, t=t, nb=nb, s=s: h.dma_start(
                        xnd_d[s, t * 128:(t + 1) * 128, :], nb[:, :]),
                        reads=[nb[:, :]], writes=[xnd_d[s, t * 128:(t + 1) * 128, :]],
                        queue=f"q_xnd{t % 2}")
                    for cc in range(8):
                        transpose(psT[:, cc * 128:(cc + 1) * 128],
                                  nb[:, cc * 128:(cc + 1) * 128], idb[:, :])
                    p.op("act", lambda h, t=t: h.copy(
                        bass.AP(xn, t * 128, [[8 * S, 128], [S, 8], [1, 128]]),
                        psT[:, :]),
                        reads=[psT[:, :]],
                        writes=[bass.AP(xn, t * 128, [[8 * S, 128], [S, 8], [1, 128]])],
                        tag="xn_evac")
                p.barrier()

            ab_es = s_es.enter_context(contextlib.ExitStack())
            ao = ab_es.enter_context(sb(f"ao_{s}", [128, 8 * S], BF16))
            with contextlib.ExitStack() as at_es:
                qf = at_es.enter_context(sb(f"qf_{s}", [128, 4 * S], BF16))
                kf = at_es.enter_context(sb(f"kf_{s}", [128, 4 * S], BF16))
                vt = at_es.enter_context(sb(f"vt_{s}", [128, KC * 520], BF16))
                prb = at_es.enter_context(sb(f"prb_{s}", [128, KC * QHW], BF16))
                skw = [at_es.enter_context(sb(f"skw_{s}_{i}", [128, SKW], BF16))
                       for i in range(2)]
                wpr = [at_es.enter_context(sb(f"wpr_{s}_{i}", [128, 8 * 512], BF16))
                       for i in range(2)]
                recip = at_es.enter_context(sb(f"recip_{s}", [1, 1024], F32))
                rbc = at_es.enter_context(sb(f"rbc_{s}", [64, 1024], F32))
                aosum = at_es.enter_context(sb(f"aosum_{s}", [128, 1024], F32))

                p.op("dve", lambda h, vt=vt: h.memset(vt[:, :], 1.0), writes=[vt[:, :]])

                pending = []
                for g in range(2):
                    for wi, (wd, dst) in enumerate(((wk_d, kf), (wq_d, qf))):
                        wb = wpr[wi]
                        _d, _s = chunk_pair(wb, wd, g * 512, 8, 512, D)
                        p.op("pool", lambda h, _d=_d, _s=_s: h.dma_start(_d, _s),
                            writes=[wb[:, :]], queue=f"q_wpr{wi}")
                        for fl in range(4):
                            for n0 in range(0, S, 512):
                                nw = min(512, S - n0)
                                pst = psC if (fl % 2 == 0) else psD
                                for cc in range(8):
                                    mm(pst[0:128, 0:nw],
                                       wb[:, cc * 512 + fl * 128: cc * 512 + fl * 128 + 128],
                                       xn[:, cc * S + n0: cc * S + n0 + nw],
                                       start=(cc == 0), stop=(cc == 7))
                                acopy(dst[:, fl * S + n0: fl * S + n0 + nw],
                                      pst[0:128, 0:nw])
                    wb = wpr[0]
                    _d, _s = chunk_pair(wb, wv_d, g * 512, 8, 512, D)
                    p.op("pool", lambda h, _d=_d, _s=_s: h.dma_start(_d, _s),
                        writes=[wb[:, :]], queue="q_wpr0")
                    for t in range(NT):
                        pst = psC if (t % 2 == 0) else psD
                        for cc in range(8):
                            mm(pst[:, 0:512],
                               xn[:, cc * S + t * 128: cc * S + (t + 1) * 128],
                               wb[:, cc * 512:(cc + 1) * 512],
                               start=(cc == 0), stop=(cc == 7))
                        vap = bass.AP(vt, t * 520, [[KC * 520, 128], [65, 8], [1, 64]])
                        p.op("act", lambda h, pst=pst, vap=vap: h.copy(vap, pst[:, 0:512]),
                             reads=[pst[:, 0:512]],
                             writes=[vt[:, t * 520:(t + 1) * 520]])

                    for hl in range(8):
                        hh = g * 8 + hl
                        sk = skw[hh % 2]
                        p.op("pool", lambda h, hh=hh, sk=sk: h.dma_start(
                            sk[:, :], bsk_d[hh, :, :]),
                            writes=[sk[:, :]], queue=f"q_skw{hh % 2}")
                        pbase = 64 * (hl % 2)
                        fcol = (hl // 2) * S
                        aocol = (hh // 2) * S
                        for qh in range(NQH):
                            q0 = qh * QHW

                            PVG = KC

                            def pv_ap(qg, r0, r1, gw=QGW):
                                if qg == 0:
                                    return psE[r0:r1, 0:gw]
                                return psT[r0:r1, 0:2 * gw].bitcast(F32)

                            def pv_evac(qg, first):
                                dst = aosum[0:65, qg * QGW:(qg + 1) * QGW]
                                srcp = pv_ap(qg, 0, 65)
                                if first:
                                    p.op("dve", lambda h, dst=dst, srcp=srcp:
                                         h.tensor_copy(dst, srcp),
                                         reads=[srcp], writes=[dst], tag="pvev")
                                else:
                                    tt(dst, dst, srcp, ALU.add, tag="pvev")

                            def pv_step(kc):
                                for qg in range(NQG):
                                    mm(pv_ap(qg, 0, 65),
                                       vt[:, kc * 520 + hl * 65: kc * 520 + hl * 65 + 65],
                                       prb[:, kc * QHW + qg * QGW: kc * QHW + (qg + 1) * QGW],
                                       start=(kc % PVG == 0),
                                       stop=(kc % PVG == PVG - 1 or kc == KC - 1),
                                       tag="pv")
                                if kc % PVG == PVG - 1 or kc == KC - 1:
                                    for qg in range(NQG):
                                        pv_evac(qg, kc < PVG)

                            flush_at = min(4, KC - 1)
                            for kc in range(KC):
                                if kc == flush_at:
                                    for f_ in pending:
                                        f_()
                                    pending.clear()
                                sc = (psA, psB, psC2)[kc % 3]
                                boff = q0 + (S - 128) - kc * 128
                                for j in range(0, QHW, 512):
                                    nw = min(512, QHW - j)
                                    mm(sc[:, j:j + nw], idb[:, :],
                                       sk[:, boff + j: boff + j + nw],
                                       start=True, stop=False, tag="biasmm")
                                for j in range(0, QHW, 512):
                                    nw = min(512, QHW - j)
                                    mm(sc[:, j:j + nw],
                                       kf[pbase:pbase + 64,
                                          fcol + kc * 128: fcol + (kc + 1) * 128],
                                       qf[pbase:pbase + 64,
                                          fcol + q0 + j: fcol + q0 + j + nw],
                                       start=False, stop=True, tag="scores")
                                act(prb[:, kc * QHW: (kc + 1) * QHW],
                                    sc[:, 0:QHW], AF.Exp, tag="exp")
                                if kc > 0:
                                    pv_step(kc - 1)
                            pv_step(KC - 1)

                            for qg in range(NQG):
                                qs = qg * QGW
                                p.op("dve", lambda h, qs=qs, recip=recip, aosum=aosum: h.reciprocal(
                                    recip[0:1, qs:qs + QGW],
                                    aosum[64:65, qs:qs + QGW]),
                                    reads=[aosum[64:65, qs:qs + QGW]],
                                    writes=[recip[0:1, qs:qs + QGW]], tag="recip")

                                def norm_part2(qs=qs, pbase=pbase, aocol=aocol,
                                               q0=q0, recip=recip, aosum=aosum,
                                               rbc=rbc):
                                    bk = psC2[0:64, qs:qs + QGW]
                                    mm(bk, ones_f[0:1, 0:64],
                                       recip[0:1, qs:qs + QGW],
                                       start=True, stop=True, tag="bcast")
                                    acopy(rbc[0:64, qs:qs + QGW], bk)
                                    tt(ao[pbase:pbase + 64,
                                          aocol + q0 + qs: aocol + q0 + qs + QGW],
                                       aosum[0:64, qs:qs + QGW],
                                       rbc[0:64, qs:qs + QGW],
                                       ALU.mult, tag="ao")
                                pending.append(norm_part2)
                for f_ in pending:
                    f_()
                pending.clear()
                p.barrier()

            yf_es = s_es.enter_context(contextlib.ExitStack())
            yf = yf_es.enter_context(sb(f"yf_{s}", [128, 8 * S], F32, side="right"))
            with contextlib.ExitStack() as op_es:
                wob = op_es.enter_context(sb(f"wob_{s}", [128, 8 * 1024], BF16))
                _d, _s = chunk_pair(wob, wo_d, 0, 8, 1024, D)
                p.op("pool", lambda h, _d=_d, _s=_s: h.dma_start(_d, _s),
                    writes=[wob[:, :]], queue="q_wob")
                for fo in range(8):
                    for n0 in range(0, S, 512):
                        nw = min(512, S - n0)
                        pst = psC if (fo % 2 == 0) else psD
                        for cc in range(8):
                            mm(pst[:, 0:nw],
                               wob[:, cc * 1024 + fo * 128: cc * 1024 + fo * 128 + 128],
                               ao[:, cc * S + n0: cc * S + n0 + nw],
                               start=(cc == 0), stop=(cc == 7))
                        acopy(yf[:, fo * S + n0: fo * S + n0 + nw], pst[:, 0:nw])
                p.barrier()
            ab_es.close()

            with contextlib.ExitStack() as b_es:
                hw = min(1024, S // 2)
                NHALF = S // hw
                w1b = b_es.enter_context(sb(f"w1b_{s}", [128, 8 * I_FF], BF16))
                w2b = b_es.enter_context(sb(f"w2b_{s}", [128, 8 * I_FF], BF16))
                w3b = b_es.enter_context(sb(f"w3b_{s}", [128, 11 * 1024], BF16))
                shpairs = (chunk_pair(w1b, ws1t_d, 0, 8, I_FF, I_FF),
                           chunk_pair(w2b, ws2t_d, 0, 8, I_FF, I_FF),
                           chunk_pair(w3b, ws3t_d, 0, 11, 1024, D))
                for wbn, (wb_, (_d, _s)) in enumerate(zip((w1b, w2b, w3b), shpairs)):
                    p.op("pool", lambda h, _d=_d, _s=_s: h.dma_start(_d, _s),
                         writes=[wb_[:, :]], queue=f"q_w{wbn}b")
                with contextlib.ExitStack() as gt_es:
                    gl = gt_es.enter_context(sb(f"gl_{s}", [128, NE], F32))
                    gm = gt_es.enter_context(sb(f"gm_{s}", [128, 1], F32))
                    gs = gt_es.enter_context(sb(f"gs_{s}", [128, 1], F32))
                    gp = gt_es.enter_context(sb(f"gp_{s}", [128, NE], F32))
                    m1k = gt_es.enter_context(sb(f"m1k_{s}", [128, NE], F32))
                    m2k = gt_es.enter_context(sb(f"m2k_{s}", [128, NE], F32))
                    pex = gt_es.enter_context(sb(f"pex_{s}", [128, NE], F32))
                    gm2 = gt_es.enter_context(sb(f"gm2_{s}", [128, 1], F32))
                    comb = [gt_es.enter_context(sb(f"comb_{s}_{i}", [128, NE], F32))
                            for i in range(2)]
                    maskb = gt_es.enter_context(sb(f"maskb_{s}", [128, NE], F32))
                    mbf = gt_es.enter_context(sb(f"mbf_{s}", [128, NE], BF16))
                    rnk = gt_es.enter_context(sb(f"rnk_{s}", [128, NE], F32))
                    crep = gt_es.enter_context(sb(f"crep_{s}", [128, NE], F32))
                    carry = gt_es.enter_context(sb(f"carry_{s}", [1, NE], F32))
                    tmp8 = gt_es.enter_context(sb(f"tmp8_{s}", [128, NE], F32))
                    rsel = gt_es.enter_context(sb(f"rsel_{s}", [128, 2], F32))
                    esel = gt_es.enter_context(sb(f"esel_{s}", [128, 2], F32))
                    flf = gt_es.enter_context(sb(f"flf_{s}", [128, 2], F32))
                    flu = gt_es.enter_context(sb(f"flu_{s}", [128, 2], U32))
                    vsel = gt_es.enter_context(sb(f"vsel_{s}", [128, 2], F32))
                    tokid = gt_es.enter_context(sb(f"tokid_{s}", [128, 1], U32))
                    tokf = gt_es.enter_context(sb(f"tokf_{s}", [128, 1], F32))
                    ovf = gt_es.enter_context(sb(f"ovf_{s}", [128, 2], F32))

                    p.op("sp", lambda h, s=s: h.dma_start(
                        idx_d[s, :], oobi_d),
                        writes=[idx_d[s, :]], queue="q_oobi")
                    cvout = bass.AP(cval_d.tensor, s * NE * CAP,
                                    [[48, 128], [1, 48]])
                    p.op("sp", lambda h, cvout=cvout: h.dma_start(
                        cvout, zrow[:, 0:48]),
                        reads=[zrow[:, 0:48]], writes=[cval_d[s, :]],
                        queue="q_cvz")
                    p.op("dve", lambda h, carry=carry: h.memset(carry[:, :], 0.0),
                         writes=[carry[:, :]])

                    def rsum(out, in_, tag=""):
                        p.op("dve", lambda h, out=out, in_=in_: h.reduce_sum(
                            out, in_, axis=AX.X),
                            reads=[in_], writes=[out], tag=tag)

                    for t in range(NT):
                        cmb = comb[t % 2]
                        for cc in range(8):
                            mm(psD[0:128, 0:NE],
                               xn[:, cc * S + t * 128: cc * S + (t + 1) * 128],
                               wgt_s[:, cc * 8: cc * 8 + NE],
                               start=(cc == 0), stop=(cc == 7))
                        rmax(gm[:, :], psD[0:128, 0:NE])
                        tsc(gm[:, :], gm[:, :], -1.0, None, ALU.mult)
                        act(gl[:, :], psD[0:128, 0:NE], AF.Exp, bias=gm[:, 0:1],
                            accum=gs[:, 0:1])
                        recipo(gs[:, :], gs[:, :])
                        tsc(gp[:, :], gl[:, :], gs[:, 0:1], None, ALU.mult)
                        rmax(gm[:, :], gp[:, :])
                        tsc(m1k[:, :], gp[:, :], gm[:, 0:1], None, ALU.is_ge)
                        stt(pex[:, :], m1k[:, :], -1e30, gp[:, :], ALU.mult, ALU.add)
                        rmax(gm2[:, :], pex[:, :])
                        tsc(cmb[:, :], gp[:, :], gm2[:, 0:1], None, ALU.is_ge)
                        tt(cmb[:, :], cmb[:, :], gp[:, :], ALU.mult)
                        tsc(maskb[:, :], cmb[:, :], 0.0, None, ALU.is_gt)
                        tt(m2k[:, :], maskb[:, :], m1k[:, :], ALU.subtract)
                        p.op("dve", lambda h, mbf=mbf, maskb=maskb: h.tensor_copy(mbf[:, :], maskb[:, :]),
                             reads=[maskb[:, :]], writes=[mbf[:, :]])
                        mm(psC[0:128, 0:NE], triu[:, :], mbf[:, :],
                           start=True, stop=True, tag="rank")
                        mm(psE[0:128, 0:NE], ones_f[0:1, 0:128], carry[0:1, :],
                           start=True, stop=True, tag="carrybc")
                        acopy(crep[:, :], psE[0:128, 0:NE])
                        tt(rnk[:, :], psC[0:128, 0:NE], crep[:, :], ALU.add)
                        mm(psE[0:1, 0:NE], onesc_bf[:, 0:1], mbf[:, :],
                           start=True, stop=True, tag="tot")
                        tt(carry[0:1, :], psE[0:1, 0:NE], carry[0:1, :], ALU.add)
                        for k, mk in ((0, m1k), (1, m2k)):
                            tt(tmp8[:, :], rnk[:, :], mk[:, :], ALU.mult)
                            rsum(rsel[:, k:k + 1], tmp8[:, :])
                            tt(tmp8[:, :], lane8[:, :], mk[:, :], ALU.mult)
                            rsum(esel[:, k:k + 1], tmp8[:, :])
                            tt(tmp8[:, :], cmb[:, :], mk[:, :], ALU.mult)
                            rsum(vsel[:, k:k + 1], tmp8[:, :])
                        tsc(ovf[:, :], rsel[:, :], float(CAP), None, ALU.is_ge)
                        tsc(flf[:, :], esel[:, :], float(CAP), None, ALU.mult)
                        tt(flf[:, :], flf[:, :], rsel[:, :], ALU.add)
                        stt(flf[:, :], ovf[:, :], 1e9, flf[:, :], ALU.mult, ALU.add)
                        p.op("dve", lambda h, flu=flu, flf=flf: h.tensor_copy(flu[:, :], flf[:, :]),
                             reads=[flf[:, :]], writes=[flu[:, :]])
                        tsc(tokf[:, 0:1], rowid[:, 0:1], float(t * 128), None,
                            ALU.add)
                        p.op("dve", lambda h, tokid=tokid, tokf=tokf:
                             h.tensor_copy(tokid[:, 0:1], tokf[:, 0:1]),
                             reads=[tokf[:, :]], writes=[tokid[:, :]])
                        iout = bass.AP(idx_d.tensor, 0,
                                       [[1, n_seqs * NE * CAP], [1, 1]])
                        ieoff = s * NE * CAP
                        cvo = bass.AP(cval_d.tensor, 0,
                                      [[1, n_seqs * NE * CAP], [1, 1]])
                        for k in range(2):
                            p.op("pool", lambda h, k=k, iout=iout, ieoff=ieoff, flu=flu, tokid=tokid: h.indirect_dma_start(
                                iout, IOA(ap=flu[:, k:k + 1], axis=0),
                                tokid[:, 0:1], None, element_offset=ieoff,
                                bounds_check=rbC, oob_is_err=False),
                                reads=[flu[:, k:k + 1], tokid[:, :]],
                                writes=[idx_d[s, :]], queue=f"q_isc{k}")
                            p.op("pool", lambda h, k=k, cvo=cvo, ieoff=ieoff, flu=flu, vsel=vsel: h.indirect_dma_start(
                                cvo, IOA(ap=flu[:, k:k + 1], axis=0),
                                vsel[:, k:k + 1], None, element_offset=ieoff,
                                bounds_check=rbC, oob_is_err=False),
                                reads=[flu[:, k:k + 1], vsel[:, k:k + 1]],
                                writes=[cval_d[s, :]], queue=f"q_vsc{k}")
                    p.barrier()

                hsb = b_es.enter_context(sb(f"hsb_{s}", [128, 11 * 1024], BF16))
                sgb = b_es.enter_context(sb(f"sgb_{s}", [128, 512], BF16))
                usb = b_es.enter_context(sb(f"usb_{s}", [128, 512], BF16))
                sgb2 = sgb
                usb2 = usb

                sgbs = [sgb, sgb2]
                usbs = [usb, usb2]
                for hf in range(NHALF):
                    h0 = hf * hw
                    for ic in range(11):
                        for j0 in range(0, hw, 512):
                            nw = min(512, hw - j0)
                            gb = (psC, psA0)[ic % 2]
                            ub = (psD, psA1)[ic % 2]
                            sg_ = sgbs[ic % 2]
                            us_ = usbs[ic % 2]
                            for cc in range(8):
                                mm(gb[0:128, 0:nw],
                                   w1b[:, cc * I_FF + ic * 128: cc * I_FF + ic * 128 + 128],
                                   xn[:, cc * S + h0 + j0: cc * S + h0 + j0 + nw],
                                   start=(cc == 0), stop=(cc == 7), tag="g")
                            for cc in range(8):
                                mm(ub[0:128, 0:nw],
                                   w2b[:, cc * I_FF + ic * 128: cc * I_FF + ic * 128 + 128],
                                   xn[:, cc * S + h0 + j0: cc * S + h0 + j0 + nw],
                                   start=(cc == 0), stop=(cc == 7), tag="u")
                            act(sg_[:, 0:nw], gb[0:128, 0:nw], AF.Silu)
                            acopy(us_[:, 0:nw], ub[0:128, 0:nw])
                            tt(hsb[:, ic * hw + j0: ic * hw + j0 + nw],
                               sg_[:, 0:nw], us_[:, 0:nw], ALU.mult, tag="h")
                    for cc in range(8):
                        for j0 in range(0, hw, 512):
                            nw = min(512, hw - j0)
                            pst = psB0 if (cc % 2 == 0) else psB1
                            for ic in range(11):
                                mm(pst[0:128, 0:nw],
                                   w3b[:, ic * 1024 + cc * 128: ic * 1024 + cc * 128 + 128],
                                   hsb[:, ic * hw + j0: ic * hw + j0 + nw],
                                   start=(ic == 0), stop=(ic == 10), tag="o")
                            tt(yf[:, cc * S + h0 + j0: cc * S + h0 + j0 + nw],
                               yf[:, cc * S + h0 + j0: cc * S + h0 + j0 + nw],
                               pst[0:128, 0:nw], ALU.add, tag="yacc")
                p.barrier()

                ost = [b_es.enter_context(sb(f"ost_{s}_{i}", [128, D], F32))
                       for i in range(2)]
                with contextlib.ExitStack() as fl_es:
                    xt2 = [fl_es.enter_context(sb(f"xt2_{s}_{i}", [128, D], F32))
                           for i in range(2)]
                    ob = ost
                    p.op("sp", lambda h, s=s: h.dma_start(
                        ysc_d[s, S:S + 128, :], zrow[:, :]),
                        reads=[zrow[:, :]], writes=[ysc_d[s, S:S + 128, :]],
                        queue="q_zr")
                    for t in range(NT):
                        xb = xt2[t % 2]
                        o = ob[t % 2]
                        p.op("sp", lambda h, t=t, xb=xb, s=s: h.dma_start(
                            xb[:, :], x_d[s, t * 128:(t + 1) * 128, :]),
                            writes=[xb[:, :]], queue=f"q_xt2{t % 2}")
                        for fo in range(8):
                            dst = (psA[:, fo * 128:(fo + 1) * 128] if fo < 4
                                   else psB[:, (fo - 4) * 128:(fo - 3) * 128])
                            transpose(dst,
                                      yf[:, fo * S + t * 128: fo * S + (t + 1) * 128],
                                      idf[:, :])
                        tt(o[:, 0:512], psA[:, 0:512], xb[:, 0:512], ALU.add)
                        tt(o[:, 512:1024], psB[:, 0:512], xb[:, 512:1024], ALU.add)
                        p.op("sp", lambda h, t=t, o=o, s=s: h.dma_start(
                            ysc_d[s, t * 128:(t + 1) * 128, :], o[:, :]),
                            reads=[o[:, :]], writes=[ysc_d[s, t * 128:(t + 1) * 128, :]],
                            queue=f"q_ob{t % 2}")
                    p.barrier()
                yf_es.close()

                xngt = [b_es.enter_context(sb(f"xngt_{s}_{i}", [128, D], BF16))
                        for i in range(2)]
                xngf = b_es.enter_context(sb(f"xngf_{s}", [128, 8 * CAP], BF16))
                ofm = b_es.enter_context(sb(f"ofm_{s}", [128, 8 * CAP], BF16))
                idxe = b_es.enter_context(sb(f"idxe_{s}", [128, NJT], U32))
                idxf = b_es.enter_context(sb(f"idxf_{s}", [128, NJT], F32))
                vld = b_es.enter_context(sb(f"vld_{s}", [128, NJT], F32))
                cscg = [b_es.enter_context(sb(f"csc_{s}_{i}", [128, NJT], F32))
                        for i in range(2)]
                sxf = b_es.enter_context(sb(f"sxf_{s}", [128, NJT], F32))
                sxug = [b_es.enter_context(sb(f"sxu_{s}_{i}", [128, NJT], U32))
                        for i in range(2)]
                pend_out = []
                t9 = b_es.enter_context(sb(f"t9_{s}", [128, NJT], F32))

                def gather_stage(e):
                    csc = cscg[e % 2]
                    sxu = sxug[e % 2]
                    iap = bass.AP(idx_d.tensor, (s * NE + e) * CAP,
                                  [[NJT, 128], [1, NJT]])
                    p.op("sp", lambda h, iap=iap, idxe=idxe: h.dma_start(
                        idxe[:, :], iap),
                        reads=[idx_d[s, :]], writes=[idxe[:, :]], queue="q_idxe")
                    cap_ = bass.AP(cval_d.tensor, (s * NE + e) * CAP,
                                   [[NJT, 128], [1, NJT]])
                    p.op("sp", lambda h, cap_=cap_, csc=csc: h.dma_start(
                        csc[:, :], cap_),
                        reads=[cval_d[s, :]], writes=[csc[:, :]], queue=f"q_csc{e % 2}")
                    for j in range(NJT):
                        gt_ = xngt[j % 2]
                        xnd_flat = bass.AP(xnd_d.tensor, 0,
                                           [[D, n_seqs * S], [1, D]])
                        p.op("pool", lambda h, gt_=gt_, s=s, j=j, idxe=idxe,
                             xnd_flat=xnd_flat: h.indirect_dma_start(
                                 gt_[:, :], None, xnd_flat,
                                 IOA(ap=idxe[:, j:j + 1], axis=0),
                                 element_offset=s * S * D,
                                 bounds_check=rbS, oob_is_err=False),
                             reads=[idxe[:, j:j + 1], xnd_d[s]],
                             writes=[gt_[:, :]], queue=f"q_xg{j % 2}")
                        p.op("dve", lambda h, j=j, idxe=idxe, idxf=idxf:
                             h.tensor_copy(idxf[:, j:j + 1], idxe[:, j:j + 1]),
                             reads=[idxe[:, j:j + 1]], writes=[idxf[:, j:j + 1]])
                        tsc(vld[:, j:j + 1], idxf[:, j:j + 1], float(S - 1), None,
                            ALU.is_le)
                        tsc(t9[:, j:j + 1], vld[:, j:j + 1], -1.0, 1.0,
                            ALU.mult, ALU.add)
                        tt(t9[:, j:j + 1], t9[:, j:j + 1], padid[:, 0:1], ALU.mult)
                        tt(sxf[:, j:j + 1], idxf[:, j:j + 1], vld[:, j:j + 1],
                           ALU.mult)
                        tt(sxf[:, j:j + 1], sxf[:, j:j + 1], t9[:, j:j + 1],
                           ALU.add)
                        p.op("dve", lambda h, j=j, sxu=sxu, sxf=sxf:
                             h.tensor_copy(sxu[:, j:j + 1], sxf[:, j:j + 1]),
                             reads=[sxf[:, j:j + 1]], writes=[sxu[:, j:j + 1]])
                        for cc in range(8):
                            transpose(psT[:, cc * 128:(cc + 1) * 128],
                                      gt_[:, cc * 128:(cc + 1) * 128], idb[:, :])
                        p.op("act", lambda h, j=j, xngf=xngf: h.copy(
                            bass.AP(xngf, j * 128, [[8 * CAP, 128], [CAP, 8], [1, 128]]),
                            psT[:, :]),
                            reads=[psT[:, :]],
                            writes=[bass.AP(xngf, j * 128,
                                            [[8 * CAP, 128], [CAP, 8], [1, 128]])],
                            tag="xg_evac")

                gather_stage(0)
                for e in range(n_experts):
                    csc = cscg[e % 2]
                    sxu = sxug[e % 2]
                    pairs = (chunk_pair(w1b, w1t_d, e * D * I_FF, 8, I_FF, I_FF),
                             chunk_pair(w2b, w2t_d, e * D * I_FF, 8, I_FF, I_FF),
                             chunk_pair(w3b, w3t_d, e * I_FF * D, 11, 1024, D))
                    for wbn, (wb_, (_d, _s)) in enumerate(zip((w1b, w2b, w3b), pairs)):
                        p.op("pool", lambda h, _d=_d, _s=_s: h.dma_start(_d, _s),
                             writes=[wb_[:, :]], queue=f"q_w{wbn}b")
                    if e > 0:
                        gather_stage(e)
                    for f_ in pend_out:
                        f_()
                    pend_out.clear()
                    for ic in range(11):
                        for j0 in range(0, CAP, 512):
                            nw = min(512, CAP - j0)
                            gb = (psC, psA0)[ic % 2]
                            ub = (psD, psA1)[ic % 2]
                            sg_ = sgbs[ic % 2]
                            us_ = usbs[ic % 2]
                            for cc in range(8):
                                mm(gb[0:128, 0:nw],
                                   w1b[:, cc * I_FF + ic * 128: cc * I_FF + ic * 128 + 128],
                                   xngf[:, cc * CAP + j0: cc * CAP + j0 + nw],
                                   start=(cc == 0), stop=(cc == 7), tag="g")
                            for cc in range(8):
                                mm(ub[0:128, 0:nw],
                                   w2b[:, cc * I_FF + ic * 128: cc * I_FF + ic * 128 + 128],
                                   xngf[:, cc * CAP + j0: cc * CAP + j0 + nw],
                                   start=(cc == 0), stop=(cc == 7), tag="u")
                            act(sg_[:, 0:nw], gb[0:128, 0:nw], AF.Silu)
                            acopy(us_[:, 0:nw], ub[0:128, 0:nw])
                            tt(hsb[:, ic * CAP + j0: ic * CAP + j0 + nw],
                               sg_[:, 0:nw], us_[:, 0:nw], ALU.mult, tag="h")
                    for cc in range(8):
                        for j0 in range(0, CAP, 512):
                            nw = min(512, CAP - j0)
                            pst = psB0 if (cc % 2 == 0) else psB1
                            for ic in range(11):
                                mm(pst[0:128, 0:nw],
                                   w3b[:, ic * 1024 + cc * 128: ic * 1024 + cc * 128 + 128],
                                   hsb[:, ic * CAP + j0: ic * CAP + j0 + nw],
                                   start=(ic == 0), stop=(ic == 10), tag="o")
                            acopy(ofm[:, cc * CAP + j0: cc * CAP + j0 + nw],
                                  pst[0:128, 0:nw])
                    def out_stage(s=s, e=e, csc2=csc, sxu2=sxu):
                        for j in range(NJT):
                            for cc in range(8):
                                transpose(psT[:, cc * 128:(cc + 1) * 128],
                                          ofm[:, cc * CAP + j * 128: cc * CAP + (j + 1) * 128],
                                          idb[:, :])
                            o_ = ost[j % 2]
                            p.op("dve", lambda h, o_=o_, j=j: h.tensor_scalar(
                                o_[:, :], psT[:, :], csc2[:, j:j + 1], None, ALU.mult),
                                reads=[psT[:, :], csc2[:, j:j + 1]], writes=[o_[:, :]],
                                tag="oscale")
                            ysc_flat = bass.AP(ysc_d.tensor, 0,
                                               [[D, n_seqs * (S + 128)], [1, D]])
                            p.op("pool", lambda h, o_=o_, s=s, j=j:
                                 h.indirect_dma_start(
                                     ysc_flat, IOA(ap=sxu2[:, j:j + 1], axis=0),
                                     o_[:, :], None,
                                     element_offset=s * (S + 128) * D,
                                     compute_op=ALU.add),
                                 reads=[sxu2[:, j:j + 1], o_[:, :]],
                                 writes=[ysc_d[s]], queue="q_ysc")
                    pend_out.append(out_stage)
                for f_ in pend_out:
                    f_()
                pend_out.clear()
                p.barrier()

            for q4 in range(4):
                r0 = q4 * (S // 4)
                r1 = (q4 + 1) * (S // 4)
                p.op("sp", lambda h, s=s, r0=r0, r1=r1: h.dma_start(
                    out_d[s, r0:r1, :], ysc_d[s, r0:r1, :]),
                    reads=[ysc_d[s, r0:r1, :]],
                    writes=[out_d[s, r0:r1, :]], queue="q_fin")
            p.barrier()

    p.finalize()
    es.close()
    return nc



def _rel_bucket(rel_pos, num_buckets=32, max_distance=128):
    nb = num_buckets // 2
    buckets = (rel_pos > 0).astype(np.int32) * nb
    rel = np.abs(rel_pos)
    max_exact = nb // 2
    is_small = rel < max_exact
    rel_if_large = max_exact + (
        np.log(np.maximum(rel, 1).astype(np.float32) / max_exact)
        / math.log(max_distance / max_exact) * (nb - max_exact)
    ).astype(np.int32)
    rel_if_large = np.minimum(rel_if_large, nb - 1)
    return buckets + np.where(is_small, rel, rel_if_large)


def make_bias_skew(rel_emb, S):
    d = np.arange(-(S - 1), S)
    buckets = _rel_bucket(d)
    master = np.asarray(rel_emb)[buckets, :]
    SKW = 2 * S - 128
    j = np.arange(SKW)
    pp = np.arange(128)
    idx = pp[:, None] + (2 * S - 129) - j[None, :]
    skw = master[idx, :]
    return np.ascontiguousarray(skw.transpose(2, 0, 1)).astype(np.float32)


def make_inputs(x_shard, Wq, Wk, Wv, Wo, rel_emb, Wg, W1, W2, W3,
                Ws1, Ws2, Ws3, S):
    c = np.ascontiguousarray
    f = np.float32
    return {
        "x": c(np.asarray(x_shard), dtype=f),
        "wq": c(np.asarray(Wq), dtype=f),
        "wk": c(np.asarray(Wk), dtype=f),
        "wv": c(np.asarray(Wv), dtype=f),
        "wo": c(np.asarray(Wo), dtype=f),
        "wgt": c(np.asarray(Wg).T, dtype=f),
        "w1t": c(np.asarray(W1).transpose(0, 2, 1), dtype=f),
        "w2t": c(np.asarray(W2).transpose(0, 2, 1), dtype=f),
        "w3t": c(np.asarray(W3).transpose(0, 2, 1), dtype=f),
        "ws1t": c(np.asarray(Ws1).T, dtype=f),
        "ws2t": c(np.asarray(Ws2).T, dtype=f),
        "ws3t": c(np.asarray(Ws3).T, dtype=f),
        "bsk": make_bias_skew(rel_emb, S),
        "idb": np.eye(128, dtype=ml_dtypes.bfloat16),
        "idf": np.eye(128, dtype=np.float32),
        "triu": np.triu(np.ones((128, 128)), 1).astype(ml_dtypes.bfloat16),
        "lane8": np.broadcast_to(np.arange(8, dtype=np.float32), (128, 8)).copy(),
        "oobi": np.full(8 * 768, 10 ** 9, dtype=np.uint32),
        "rowid": np.arange(128, dtype=np.float32)[:, None].copy(),
    }


def kernel(L0_attn_hidden_states, L0_attn_Wq, L0_attn_Wk, L0_attn_Wv,
           L0_attn_Wo, L0_attn_rel_emb, L0_ffn_Wg, L0_ffn_W1, L0_ffn_W2,
           L0_ffn_W3, L0_ffn_Ws1, L0_ffn_Ws2, L0_ffn_Ws3):
    x = np.asarray(L0_attn_hidden_states)
    B, S, _ = x.shape
    per = B // NCORES
    nc = build_core(n_seqs=per, S=S)
    in_maps = []
    for c in range(NCORES):
        in_maps.append(make_inputs(
            x[c * per:(c + 1) * per], L0_attn_Wq, L0_attn_Wk, L0_attn_Wv,
            L0_attn_Wo, L0_attn_rel_emb, L0_ffn_Wg, L0_ffn_W1, L0_ffn_W2,
            L0_ffn_W3, L0_ffn_Ws1, L0_ffn_Ws2, L0_ffn_Ws3, S))
    res = run_bass_kernel_spmd(nc, in_maps, core_ids=list(range(NCORES)))
    return np.concatenate([r["out"] for r in res.results], axis=0)


# version 27 (final)
# speedup vs baseline: 1.0087x; 1.0087x over previous
import contextlib
import math

import numpy as np
import ml_dtypes

import concourse.bass as bass
import concourse.mybir as mybir
from concourse.alu_op_type import AluOpType as ALU
from concourse.bass_utils import run_bass_kernel_spmd

IOA = bass.IndirectOffsetOnAxis

F32 = mybir.dt.float32
BF16 = mybir.dt.bfloat16
U32 = mybir.dt.uint32
AF = mybir.ActivationFunctionType
AX = mybir.AxisListType

D = 1024
NH = 16
DH = 64
I_FF = 1408
NE = 8
NCORES = 8

ENGINES = ("pe", "act", "dve", "pool", "sp")


def cdiv(a, b):
    return (a + b - 1) // b


class Prog:

    def __init__(self, nc):
        self.nc = nc
        self.per_eng = {e: [] for e in ENGINES}
        self.q_counts = {}
        self.writes = {}
        self.reads = {}
        self.waited = {e: {} for e in ENGINES}
        self.sems = {}
        self.es = None
        self.n_ops = 0

    def make_sems(self, es):
        self.es = es
        for e in ENGINES:
            self.sems[e] = es.enter_context(self.nc.semaphore(f"sem_{e}"))

    def _channel(self, q):
        if q not in self.q_counts:
            self.q_counts[q] = 0
            self.sems[q] = self.es.enter_context(self.nc.semaphore(f"sem_{q}"))

    @staticmethod
    def _foot(ap):
        t = ap.tensor
        name = t.name
        shape = list(t.shape)
        pitch = 1
        for s in shape[1:]:
            pitch *= s
        if len(shape) == 1:
            pitch = shape[0]
        off = ap.offset
        p0 = off // pitch
        c0 = off % pitch
        pn = 1
        cspan = 0
        for step, count in list(ap.ap):
            if count <= 1:
                continue
            if step > 0 and step % pitch == 0:
                pn += (count - 1) * (step // pitch)
            else:
                cspan += (count - 1) * abs(step)
        p1 = p0 + pn
        c1 = c0 + cspan + 1
        if "PSum" in type(t).__name__:
            esz = 2 if t.dtype == BF16 else 4
            bank = 2048 // esz
            c0 = (c0 // bank) * bank
            c1 = cdiv(c1, bank) * bank
        return (name, p0, p1, c0, c1)

    @staticmethod
    def _overlap(a, b):
        return a[1] < b[2] and b[1] < a[2] and a[3] < b[4] and b[3] < a[4]

    @staticmethod
    def _covers(a, b):
        return a[1] <= b[1] and a[2] >= b[2] and a[3] <= b[3] and a[4] >= b[4]

    def op(self, eng, fn, reads=(), writes=(), queue=None, tag=""):
        rf = [self._foot(a) for a in reads]
        wf = [self._foot(a) for a in writes]
        owner = queue if queue else eng
        edges = {}

        def add_edge(src, idx):
            if src == "pe" and owner == "pe":
                return
            if self.waited[eng].get(src, -1) >= idx:
                return
            if edges.get(src, -1) >= idx:
                return
            edges[src] = idx

        for f in rf:
            for w in self.writes.get(f[0], ()):
                if self._overlap(f, w["r"]):
                    add_edge(w["eng"], w["idx"])
        for f in wf:
            for w in self.writes.get(f[0], ()):
                if self._overlap(f, w["r"]):
                    add_edge(w["eng"], w["idx"])
            for r in self.reads.get(f[0], ()):
                if self._overlap(f, r["r"]):
                    add_edge(r["eng"], r["idx"])

        if queue:
            self._channel(queue)
            idx = self.q_counts[queue]
            self.q_counts[queue] += 1
        else:
            idx = self.n_ops
        self.n_ops += 1

        rec = {"fn": fn, "edges": edges, "queue": queue, "idx": idx,
               "inc": queue is not None, "tag": tag}
        self.per_eng[eng].append(rec)

        wd = self.waited[eng]
        for sKey, i in edges.items():
            if wd.get(sKey, -1) < i:
                wd[sKey] = i

        rec_eng = queue if queue else eng
        for f in wf:
            lst = self.writes.setdefault(f[0], [])
            lst[:] = [w for w in lst if not self._covers(f, w["r"])]
            lst.append({"r": f, "eng": rec_eng, "idx": idx})
            rl = self.reads.get(f[0])
            if rl:
                rl[:] = [r for r in rl if not self._covers(f, r["r"])]
        for f in rf:
            self.reads.setdefault(f[0], []).append(
                {"r": f, "eng": rec_eng, "idx": idx})
        return rec

    def barrier(self):
        lasts = {}
        for e in ENGINES:
            for rec in reversed(self.per_eng[e]):
                if rec["queue"] is None and rec["fn"] is not None:
                    lasts[e] = rec["idx"]
                    break
        for q, n in self.q_counts.items():
            if n:
                lasts[q] = n - 1
        for e in ENGINES:
            edges = {}
            for sKey, i in lasts.items():
                if sKey == e:
                    continue
                if self.waited[e].get(sKey, -1) >= i:
                    continue
                edges[sKey] = i
            if edges:
                self.per_eng[e].append({"fn": None, "edges": edges,
                                        "queue": None, "idx": None,
                                        "inc": False, "tag": "barrier"})
                for sKey, i in edges.items():
                    self.waited[e][sKey] = i

    def finalize(self):
        needed = {e: set() for e in ENGINES}
        for e in ENGINES:
            for rec in self.per_eng[e]:
                for sKey, i in rec["edges"].items():
                    if sKey in ENGINES:
                        needed[sKey].add(i)
        vals = {}
        for e in ENGINES:
            acc = 0
            v = {}
            for rec in self.per_eng[e]:
                if rec["queue"] is None and rec["fn"] is not None:
                    if rec["idx"] in needed[e]:
                        rec["inc"] = True
                        acc += 1
                    v[rec["idx"]] = acc
            vals[e] = v
        prog = self

        def emit(handle, eng):
            for rec in prog.per_eng[eng]:
                for sKey in sorted(rec["edges"], key=str):
                    i = rec["edges"][sKey]
                    if sKey in ENGINES:
                        handle.wait_ge(prog.sems[sKey], vals[sKey][i])
                    else:
                        handle.wait_ge(prog.sems[sKey], 16 * (i + 1))
                if rec["fn"] is None:
                    continue
                ins = rec["fn"](handle)
                if rec["queue"]:
                    ins.then_inc(prog.sems[rec["queue"]], 16)
                elif rec["inc"]:
                    if ins is None:
                        ins = handle.engine_nop() if hasattr(handle, "engine_nop") else handle.nop()
                    ins.then_inc(prog.sems[eng], 1)
            if eng == "sp":
                for q, n in sorted(prog.q_counts.items()):
                    if n:
                        handle.wait_ge(prog.sems[q], 16 * n)

        with prog.nc.Block() as block:
            @block.tensor
            def _(h):
                emit(h, "pe")

            @block.scalar
            def _(h):
                emit(h, "act")

            @block.vector
            def _(h):
                emit(h, "dve")

            @block.gpsimd
            def _(h):
                emit(h, "pool")

            @block.sync
            def _(h):
                emit(h, "sp")



def build_core(n_seqs=4, S=2048, n_experts=NE):
    KC = S // 128
    CAP = 768
    NJT = CAP // 128
    QHW = min(1024, S)
    NQH = S // QHW
    QGW = min(512, QHW)
    NQG = QHW // QGW
    NT = S // 128
    SKW = 2 * S - 128

    nc = bass.Bass("TRN2", target_bir_lowering=False)

    x_d = nc.dram_tensor("x", [n_seqs, S, D], F32, kind="ExternalInput").ap()
    wq_d = nc.dram_tensor("wq", [D, D], F32, kind="ExternalInput").ap()
    wk_d = nc.dram_tensor("wk", [D, D], F32, kind="ExternalInput").ap()
    wv_d = nc.dram_tensor("wv", [D, D], F32, kind="ExternalInput").ap()
    wo_d = nc.dram_tensor("wo", [D, D], F32, kind="ExternalInput").ap()
    wgt_d = nc.dram_tensor("wgt", [D, NE], F32, kind="ExternalInput").ap()
    w1t_d = nc.dram_tensor("w1t", [n_experts, D, I_FF], F32, kind="ExternalInput").ap()
    w2t_d = nc.dram_tensor("w2t", [n_experts, D, I_FF], F32, kind="ExternalInput").ap()
    w3t_d = nc.dram_tensor("w3t", [n_experts, I_FF, D], F32, kind="ExternalInput").ap()
    ws1t_d = nc.dram_tensor("ws1t", [D, I_FF], F32, kind="ExternalInput").ap()
    ws2t_d = nc.dram_tensor("ws2t", [D, I_FF], F32, kind="ExternalInput").ap()
    ws3t_d = nc.dram_tensor("ws3t", [I_FF, D], F32, kind="ExternalInput").ap()
    bsk_d = nc.dram_tensor("bsk", [NH, 128, SKW], F32, kind="ExternalInput").ap()
    triu_d = nc.dram_tensor("triu", [128, 128], BF16, kind="ExternalInput").ap()
    lane8_d = nc.dram_tensor("lane8", [128, 128], F32, kind="ExternalInput").ap()
    tokid_d = nc.dram_tensor("tokids", [128, 16], U32, kind="ExternalInput").ap()
    oobi_d = nc.dram_tensor("oobi", [NE * CAP], U32, kind="ExternalInput").ap()
    rowid_d = nc.dram_tensor("rowid", [128, 1], F32, kind="ExternalInput").ap()
    idb_d = nc.dram_tensor("idb", [128, 128], BF16, kind="ExternalInput").ap()
    idf_d = nc.dram_tensor("idf", [128, 128], F32, kind="ExternalInput").ap()
    out_d = nc.dram_tensor("out", [n_seqs, S, D], F32, kind="ExternalOutput").ap()
    xnd_d = nc.dram_tensor("xnd", [n_seqs, S, D], BF16).ap()
    cval_d = nc.dram_tensor("cvald", [n_seqs, NE * CAP], F32).ap()
    idx_d = nc.dram_tensor("idxd", [n_seqs, NE * CAP], U32).ap()
    ysc_d = nc.dram_tensor("yscd", [n_seqs, S + 128, D], F32).ap()

    es = contextlib.ExitStack()
    p = Prog(nc)
    p.make_sems(es)

    def sb(name, shape, dt, side=None):
        if side:
            return nc.sbuf_tensor(name, shape, dt, side=side)
        return nc.sbuf_tensor(name, shape, dt)

    idb = es.enter_context(sb("idb_s", [128, 128], BF16, side="right"))
    idf = es.enter_context(sb("idf_s", [128, 128], F32, side="right"))
    ones_f = es.enter_context(sb("ones_f", [1, 128], F32, side="right"))
    wgt_s = es.enter_context(sb("wgt_s", [128, NE * 8], BF16, side="right"))
    xn = es.enter_context(sb("xn", [128, 8 * S], BF16, side="right"))
    psA = es.enter_context(nc.psum_tensor("psA", [128, 1024], F32))
    psB = es.enter_context(nc.psum_tensor("psB", [128, 1024], F32))
    psC2 = es.enter_context(nc.psum_tensor("psC2", [128, 1024], F32))
    psE = es.enter_context(nc.psum_tensor("psE", [128, 512], F32))
    psT = es.enter_context(nc.psum_tensor("psT", [128, 1024], BF16))

    class BankView:

        def __init__(self, t, c0):
            self.t, self.c0 = t, c0

        def __getitem__(self, idx):
            r, c = idx
            cs = self.c0 + (c.start or 0)
            ce = self.c0 + c.stop
            return self.t[r, cs:ce]

    psC = BankView(psC2, 0)
    psD = BankView(psC2, 512)
    psA0 = BankView(psA, 0)
    psA1 = BankView(psA, 512)
    psB0 = BankView(psB, 0)
    psB1 = BankView(psB, 512)

    p.op("sp", lambda h: h.dma_start(idb[:, :], idb_d), writes=[idb[:, :]], queue="q_idb")
    p.op("sp", lambda h: h.dma_start(idf[:, :], idf_d), writes=[idf[:, :]], queue="q_idf")
    p.op("dve", lambda h: h.memset(ones_f[:, :], 1.0), writes=[ones_f[:, :]])

    def chunk_pair(dst, src_ap, base_off, n_ch, inner, src_row_pitch):
        t = src_ap.tensor
        d = bass.AP(dst, 0, [[n_ch * inner, 128], [inner, n_ch], [1, inner]])
        s_ = bass.AP(t, base_off,
                     [[src_row_pitch, 128], [128 * src_row_pitch, n_ch], [1, inner]])
        return d, s_

    def mm(out, lhsT, rhs, start, stop, tag=""):
        p.op("pe", lambda h: h.matmul(out, lhsT, rhs, start=start, stop=stop),
             reads=[lhsT, rhs], writes=[out], tag=tag)

    def transpose(out, in_, ident, tag=""):
        p.op("pe", lambda h: h.transpose(out, in_, ident),
             reads=[in_, ident], writes=[out], tag=tag)

    def act(out, in_, func, bias=None, scale=None, accum=None, tag=""):
        kw = {}
        if bias is not None:
            kw["bias"] = bias
        if scale is not None:
            kw["scale"] = scale
        if accum is not None:
            kw["accum_out"] = accum
        reads = [in_] + ([bias] if isinstance(bias, bass.AP) else [])
        wr = [out] + ([accum] if accum is not None else [])
        p.op("act", lambda h: h.activation(out, in_, func, **kw),
             reads=reads, writes=wr, tag=tag)

    def acopy(out, in_, tag=""):
        p.op("act", lambda h: h.copy(out, in_), reads=[in_], writes=[out], tag=tag)

    def tt(out, a, b, alu, tag=""):
        p.op("dve", lambda h: h.tensor_tensor(out, a, b, alu),
             reads=[a, b], writes=[out], tag=tag)

    def tsc(out, in_, s1, s2, op0, op1=None, tag=""):
        reads = [in_] + [q for q in (s1, s2) if isinstance(q, bass.AP)]
        if op1 is None:
            p.op("dve", lambda h: h.tensor_scalar(out, in_, s1, None, op0),
                 reads=reads, writes=[out], tag=tag)
        else:
            p.op("dve", lambda h: h.tensor_scalar(out, in_, s1, s2, op0, op1),
                 reads=reads, writes=[out], tag=tag)

    def stt(out, in0, scalar, in1, op0, op1, tag=""):
        p.op("dve", lambda h: h.scalar_tensor_tensor(out, in0, scalar, in1, op0, op1),
             reads=[in0, in1], writes=[out], tag=tag)

    def rmax(out, in_, tag=""):
        p.op("dve", lambda h: h.reduce_max(out, in_, axis=AX.X),
             reads=[in_], writes=[out], tag=tag)

    def recipo(out, in_, tag=""):
        p.op("dve", lambda h: h.reciprocal(out, in_),
             reads=[in_], writes=[out], tag=tag)

    _wgt_dst, _wgt_src = chunk_pair(wgt_s, wgt_d, 0, 8, NE, NE)
    p.op("pool", lambda h: h.dma_start(_wgt_dst, _wgt_src),
        writes=[wgt_s[:, :]], queue="q_wgt")
    triu = es.enter_context(sb("triu_s", [128, 128], BF16, side="right"))
    tokids = es.enter_context(sb("tokids_s", [128, 16], U32, side="right"))
    rowid = es.enter_context(sb("rowid_s", [128, 1], F32, side="right"))
    zrow = es.enter_context(sb("zrow_s", [128, D], F32, side="right"))
    p.op("sp", lambda h: h.dma_start(triu[:, :], triu_d), writes=[triu[:, :]], queue="q_triu")
    p.op("sp", lambda h: h.dma_start(tokids[:, :], tokid_d), writes=[tokids[:, :]], queue="q_tokids")
    p.op("sp", lambda h: h.dma_start(rowid[:, :], rowid_d),
         writes=[rowid[:, :]], queue="q_rowid")
    onesc_bf = es.enter_context(sb("onesc_bf", [128, 1], BF16, side="right"))
    padid = es.enter_context(sb("padid_s", [128, 1], F32, side="right"))
    p.op("dve", lambda h: h.memset(zrow[:, :], 0.0), writes=[zrow[:, :]])
    p.op("dve", lambda h: h.memset(onesc_bf[:, :], 1.0), writes=[onesc_bf[:, :]])
    tsc(padid[:, 0:1], rowid[:, 0:1], float(S), None, ALU.add)
    rbS = nc.gpsimd.alloc_register("rbS")
    rbC = nc.gpsimd.alloc_register("rbC")
    p.op("pool", lambda h: h.reg_mov(rbS, S - 1))
    p.op("pool", lambda h: h.reg_mov(rbC, NE * CAP - 1))

    for s in range(n_seqs):
        with contextlib.ExitStack() as s_es:
            with contextlib.ExitStack() as ln_es:
                xst = [ln_es.enter_context(sb(f"xst{s}_{i}", [128, D], F32))
                       for i in range(2)]
                xnt = [ln_es.enter_context(sb(f"xnt{s}_{i}", [128, D], BF16))
                       for i in range(2)]
                bn6 = ln_es.enter_context(sb(f"bn6_{s}", [128, 12], F32))
                bn2 = ln_es.enter_context(sb(f"bn2_{s}", [128, 2], F32))
                nmean = ln_es.enter_context(sb(f"nmean_{s}", [128, 1], F32))
                rstd = ln_es.enter_context(sb(f"rstd_{s}", [128, 1], F32))
                for t in range(NT):
                    xb = xst[t % 2]
                    nb = xnt[t % 2]
                    p.op("sp", lambda h, t=t, xb=xb, s=s: h.dma_start(
                        xb[:, :], x_d[s, t * 128:(t + 1) * 128, :]),
                        writes=[xb[:, :]], queue=f"q_xst{t % 2}")
                    p.op("dve", lambda h, xb=xb, bn6=bn6: h.bn_stats(bn6[:, 0:6], xb[:, 0:512]),
                         reads=[xb[:, 0:512]], writes=[bn6[:, 0:6]])
                    p.op("dve", lambda h, xb=xb, bn6=bn6: h.bn_stats(bn6[:, 6:12], xb[:, 512:1024]),
                         reads=[xb[:, 512:1024]], writes=[bn6[:, 6:12]])
                    p.op("dve", lambda h, bn2=bn2, bn6=bn6: h.bn_aggr(bn2[:, :], bn6[:, :]),
                         reads=[bn6[:, :]], writes=[bn2[:, :]])
                    tsc(nmean[:, :], bn2[:, 0:1], -1.0, None, ALU.mult)
                    tsc(rstd[:, :], bn2[:, 1:2], 1e-5, None, ALU.add)
                    act(rstd[:, :], rstd[:, :], AF.Ln)
                    tsc(rstd[:, :], rstd[:, :], -0.5, None, ALU.mult)
                    act(rstd[:, :], rstd[:, :], AF.Exp)
                    tsc(nb[:, :], xb[:, :], nmean[:, 0:1], rstd[:, 0:1],
                        ALU.add, ALU.mult)
                    p.op("sp", lambda h, t=t, nb=nb, s=s: h.dma_start(
                        xnd_d[s, t * 128:(t + 1) * 128, :], nb[:, :]),
                        reads=[nb[:, :]], writes=[xnd_d[s, t * 128:(t + 1) * 128, :]],
                        queue=f"q_xnd{t % 2}")
                    for cc in range(8):
                        transpose(psT[:, cc * 128:(cc + 1) * 128],
                                  nb[:, cc * 128:(cc + 1) * 128], idb[:, :])
                    p.op("act", lambda h, t=t: h.copy(
                        bass.AP(xn, t * 128, [[8 * S, 128], [S, 8], [1, 128]]),
                        psT[:, :]),
                        reads=[psT[:, :]],
                        writes=[bass.AP(xn, t * 128, [[8 * S, 128], [S, 8], [1, 128]])],
                        tag="xn_evac")
                p.barrier()

            ab_es = s_es.enter_context(contextlib.ExitStack())
            ao = ab_es.enter_context(sb(f"ao_{s}", [128, 8 * S], BF16))
            with contextlib.ExitStack() as at_es:
                qf = at_es.enter_context(sb(f"qf_{s}", [128, 4 * S], BF16))
                kf = at_es.enter_context(sb(f"kf_{s}", [128, 4 * S], BF16))
                vt = at_es.enter_context(sb(f"vt_{s}", [128, KC * 520], BF16))
                prb = at_es.enter_context(sb(f"prb_{s}", [128, KC * QHW], BF16))
                skw = [at_es.enter_context(sb(f"skw_{s}_{i}", [128, SKW], BF16))
                       for i in range(2)]
                wpr = [at_es.enter_context(sb(f"wpr_{s}_{i}", [128, 8 * 512], BF16))
                       for i in range(2)]
                recip = at_es.enter_context(sb(f"recip_{s}", [1, 1024], F32))
                rbc = at_es.enter_context(sb(f"rbc_{s}", [64, 1024], F32))
                aosum = at_es.enter_context(sb(f"aosum_{s}", [128, 1024], F32))

                p.op("dve", lambda h, vt=vt: h.memset(vt[:, :], 1.0), writes=[vt[:, :]])

                pending = []
                for g in range(2):
                    for wi, (wd, dst) in enumerate(((wk_d, kf), (wq_d, qf))):
                        wb = wpr[wi]
                        _d, _s = chunk_pair(wb, wd, g * 512, 8, 512, D)
                        p.op("pool", lambda h, _d=_d, _s=_s: h.dma_start(_d, _s),
                            writes=[wb[:, :]], queue=f"q_wpr{wi}")
                        for fl in range(4):
                            for n0 in range(0, S, 512):
                                nw = min(512, S - n0)
                                pst = psC if (fl % 2 == 0) else psD
                                for cc in range(8):
                                    mm(pst[0:128, 0:nw],
                                       wb[:, cc * 512 + fl * 128: cc * 512 + fl * 128 + 128],
                                       xn[:, cc * S + n0: cc * S + n0 + nw],
                                       start=(cc == 0), stop=(cc == 7))
                                acopy(dst[:, fl * S + n0: fl * S + n0 + nw],
                                      pst[0:128, 0:nw])
                    wb = wpr[0]
                    _d, _s = chunk_pair(wb, wv_d, g * 512, 8, 512, D)
                    p.op("pool", lambda h, _d=_d, _s=_s: h.dma_start(_d, _s),
                        writes=[wb[:, :]], queue="q_wpr0")
                    for t in range(NT):
                        pst = psC if (t % 2 == 0) else psD
                        for cc in range(8):
                            mm(pst[:, 0:512],
                               xn[:, cc * S + t * 128: cc * S + (t + 1) * 128],
                               wb[:, cc * 512:(cc + 1) * 512],
                               start=(cc == 0), stop=(cc == 7))
                        vap = bass.AP(vt, t * 520, [[KC * 520, 128], [65, 8], [1, 64]])
                        p.op("act", lambda h, pst=pst, vap=vap: h.copy(vap, pst[:, 0:512]),
                             reads=[pst[:, 0:512]],
                             writes=[vt[:, t * 520:(t + 1) * 520]])

                    for hl in range(8):
                        hh = g * 8 + hl
                        sk = skw[hh % 2]
                        p.op("pool", lambda h, hh=hh, sk=sk: h.dma_start(
                            sk[:, :], bsk_d[hh, :, :]),
                            writes=[sk[:, :]], queue=f"q_skw{hh % 2}")
                        pbase = 64 * (hl % 2)
                        fcol = (hl // 2) * S
                        aocol = (hh // 2) * S
                        for qh in range(NQH):
                            q0 = qh * QHW

                            PVG = KC

                            def pv_ap(qg, r0, r1, gw=QGW):
                                if qg == 0:
                                    return psE[r0:r1, 0:gw]
                                return psT[r0:r1, 0:2 * gw].bitcast(F32)

                            def pv_evac(qg, first):
                                dst = aosum[0:65, qg * QGW:(qg + 1) * QGW]
                                srcp = pv_ap(qg, 0, 65)
                                if first:
                                    p.op("dve", lambda h, dst=dst, srcp=srcp:
                                         h.tensor_copy(dst, srcp),
                                         reads=[srcp], writes=[dst], tag="pvev")
                                else:
                                    tt(dst, dst, srcp, ALU.add, tag="pvev")

                            def pv_step(kc):
                                for qg in range(NQG):
                                    mm(pv_ap(qg, 0, 65),
                                       vt[:, kc * 520 + hl * 65: kc * 520 + hl * 65 + 65],
                                       prb[:, kc * QHW + qg * QGW: kc * QHW + (qg + 1) * QGW],
                                       start=(kc % PVG == 0),
                                       stop=(kc % PVG == PVG - 1 or kc == KC - 1),
                                       tag="pv")
                                if kc % PVG == PVG - 1 or kc == KC - 1:
                                    for qg in range(NQG):
                                        pv_evac(qg, kc < PVG)

                            flush_at = min(4, KC - 1)
                            for kc in range(KC):
                                if kc == flush_at:
                                    for f_ in pending:
                                        f_()
                                    pending.clear()
                                sc = (psA, psB, psC2)[kc % 3]
                                boff = q0 + (S - 128) - kc * 128
                                for j in range(0, QHW, 512):
                                    nw = min(512, QHW - j)
                                    mm(sc[:, j:j + nw], idb[:, :],
                                       sk[:, boff + j: boff + j + nw],
                                       start=True, stop=False, tag="biasmm")
                                for j in range(0, QHW, 512):
                                    nw = min(512, QHW - j)
                                    mm(sc[:, j:j + nw],
                                       kf[pbase:pbase + 64,
                                          fcol + kc * 128: fcol + (kc + 1) * 128],
                                       qf[pbase:pbase + 64,
                                          fcol + q0 + j: fcol + q0 + j + nw],
                                       start=False, stop=True, tag="scores")
                                act(prb[:, kc * QHW: (kc + 1) * QHW],
                                    sc[:, 0:QHW], AF.Exp, tag="exp")
                                if kc > 0:
                                    pv_step(kc - 1)
                            pv_step(KC - 1)

                            for qg in range(NQG):
                                qs = qg * QGW
                                p.op("dve", lambda h, qs=qs, recip=recip, aosum=aosum: h.reciprocal(
                                    recip[0:1, qs:qs + QGW],
                                    aosum[64:65, qs:qs + QGW]),
                                    reads=[aosum[64:65, qs:qs + QGW]],
                                    writes=[recip[0:1, qs:qs + QGW]], tag="recip")

                                def norm_part2(qs=qs, pbase=pbase, aocol=aocol,
                                               q0=q0, recip=recip, aosum=aosum,
                                               rbc=rbc):
                                    bk = psC2[0:64, qs:qs + QGW]
                                    mm(bk, ones_f[0:1, 0:64],
                                       recip[0:1, qs:qs + QGW],
                                       start=True, stop=True, tag="bcast")
                                    acopy(rbc[0:64, qs:qs + QGW], bk)
                                    tt(ao[pbase:pbase + 64,
                                          aocol + q0 + qs: aocol + q0 + qs + QGW],
                                       aosum[0:64, qs:qs + QGW],
                                       rbc[0:64, qs:qs + QGW],
                                       ALU.mult, tag="ao")
                                pending.append(norm_part2)
                for f_ in pending:
                    f_()
                pending.clear()
                p.barrier()

            yf_es = s_es.enter_context(contextlib.ExitStack())
            yf = yf_es.enter_context(sb(f"yf_{s}", [128, 8 * S], F32, side="right"))
            with contextlib.ExitStack() as op_es:
                wob = op_es.enter_context(sb(f"wob_{s}", [128, 8 * 1024], BF16))
                _d, _s = chunk_pair(wob, wo_d, 0, 8, 1024, D)
                p.op("pool", lambda h, _d=_d, _s=_s: h.dma_start(_d, _s),
                    writes=[wob[:, :]], queue="q_wob")
                for fo in range(8):
                    for n0 in range(0, S, 512):
                        nw = min(512, S - n0)
                        pst = psC if (fo % 2 == 0) else psD
                        for cc in range(8):
                            mm(pst[:, 0:nw],
                               wob[:, cc * 1024 + fo * 128: cc * 1024 + fo * 128 + 128],
                               ao[:, cc * S + n0: cc * S + n0 + nw],
                               start=(cc == 0), stop=(cc == 7))
                        acopy(yf[:, fo * S + n0: fo * S + n0 + nw], pst[:, 0:nw])
                p.barrier()
            ab_es.close()

            with contextlib.ExitStack() as b_es:
                hw = min(1024, S // 2)
                NHALF = S // hw
                w1b = b_es.enter_context(sb(f"w1b_{s}", [128, 8 * I_FF], BF16))
                w2b = b_es.enter_context(sb(f"w2b_{s}", [128, 8 * I_FF], BF16))
                w3b = b_es.enter_context(sb(f"w3b_{s}", [128, 11 * 1024], BF16))
                shpairs = (chunk_pair(w1b, ws1t_d, 0, 8, I_FF, I_FF),
                           chunk_pair(w2b, ws2t_d, 0, 8, I_FF, I_FF),
                           chunk_pair(w3b, ws3t_d, 0, 11, 1024, D))
                for wbn, (wb_, (_d, _s)) in enumerate(zip((w1b, w2b, w3b), shpairs)):
                    p.op("pool", lambda h, _d=_d, _s=_s: h.dma_start(_d, _s),
                         writes=[wb_[:, :]], queue=f"q_w{wbn}b")
                with contextlib.ExitStack() as gt_es:
                    W = NT * NE
                    glA = gt_es.enter_context(sb(f"glA_{s}", [128, W], F32))
                    glE = gt_es.enter_context(sb(f"glE_{s}", [128, W], F32))
                    gpA = gt_es.enter_context(sb(f"gpA_{s}", [128, W], F32))
                    m1kA = gt_es.enter_context(sb(f"m1kA_{s}", [128, W], F32))
                    m2kA = gt_es.enter_context(sb(f"m2kA_{s}", [128, W], F32))
                    pexA = gt_es.enter_context(sb(f"pexA_{s}", [128, W], F32))
                    cmbA = gt_es.enter_context(sb(f"cmbA_{s}", [128, W], F32))
                    mskA = gt_es.enter_context(sb(f"mskA_{s}", [128, W], F32))
                    mbfA = gt_es.enter_context(sb(f"mbfA_{s}", [128, W], BF16))
                    tmpA = gt_es.enter_context(sb(f"tmpA_{s}", [128, W], F32))
                    rnkA = gt_es.enter_context(sb(f"rnkA_{s}", [128, W], F32))
                    rnkF = gt_es.enter_context(sb(f"rnkF_{s}", [128, W], F32))
                    gmax = gt_es.enter_context(sb(f"gmax_{s}", [128, NT], F32))
                    gsum = gt_es.enter_context(sb(f"gsum_{s}", [128, NT], F32))
                    gmx1 = gt_es.enter_context(sb(f"gmx1_{s}", [128, NT], F32))
                    gmx2 = gt_es.enter_context(sb(f"gmx2_{s}", [128, NT], F32))
                    totsb = gt_es.enter_context(sb(f"totsb_{s}", [1, W], F32))
                    cf = gt_es.enter_context(sb(f"cf_{s}", [1, W], F32))
                    rselA = gt_es.enter_context(sb(f"rselA_{s}", [128, 2 * NT], F32))
                    eselA = gt_es.enter_context(sb(f"eselA_{s}", [128, 2 * NT], F32))
                    vselA = gt_es.enter_context(sb(f"vselA_{s}", [128, 2 * NT], F32))
                    ovfA = gt_es.enter_context(sb(f"ovfA_{s}", [128, 2 * NT], F32))
                    flfA = gt_es.enter_context(sb(f"flfA_{s}", [128, 2 * NT], F32))
                    fluA = gt_es.enter_context(sb(f"fluA_{s}", [128, 2 * NT], U32))
                    lane8 = gt_es.enter_context(sb(f"lane8_{s}", [128, W], F32))
                    lsrc = bass.AP(lane8_d.tensor, 0, [[128, 128], [1, W]])
                    p.op("sp", lambda h, lane8=lane8, lsrc=lsrc: h.dma_start(
                        lane8[:, :], lsrc),
                        writes=[lane8[:, :]], queue="q_lane8")

                    p.op("sp", lambda h, s=s: h.dma_start(
                        idx_d[s, :], oobi_d),
                        writes=[idx_d[s, :]], queue="q_oobi")
                    cvout = bass.AP(cval_d.tensor, s * NE * CAP,
                                    [[48, 128], [1, 48]])
                    p.op("sp", lambda h, cvout=cvout: h.dma_start(
                        cvout, zrow[:, 0:48]),
                        reads=[zrow[:, 0:48]], writes=[cval_d[s, :]],
                        queue="q_cvz")

                    def rsum3(out, tns, tag=""):
                        in3 = bass.AP(tns, 0, [[W, 128], [NE, NT], [1, NE]])
                        p.op("dve", lambda h, out=out, in3=in3: h.reduce_sum(
                            out, in3, axis=AX.X),
                            reads=[tns[:, 0:W]], writes=[out], tag=tag)

                    def rmax3(out, tns, tag=""):
                        in3 = bass.AP(tns, 0, [[W, 128], [NE, NT], [1, NE]])
                        p.op("dve", lambda h, out=out, in3=in3: h.reduce_max(
                            out, in3, axis=AX.X),
                            reads=[tns[:, 0:W]], writes=[out], tag=tag)

                    def bc0(t16):
                        return bass.AP(t16, 0, [[NT, 128], [1, NT], [0, NE]])

                    def v3(tns):
                        return bass.AP(tns, 0, [[W, 128], [NE, NT], [1, NE]])

                    def ttb(out, a3, b3, alu, rd, tag=""):
                        p.op("dve", lambda h, out=out, a3=a3, b3=b3, alu=alu:
                             h.tensor_tensor(out, a3, b3, alu),
                             reads=rd, writes=[out[:, 0:W] if hasattr(out, "tensor") is False else out],
                             tag=tag)

                    for t in range(NT):
                        for cc in range(8):
                            mm(psD[0:128, t * NE:(t + 1) * NE],
                               xn[:, cc * S + t * 128: cc * S + (t + 1) * 128],
                               wgt_s[:, cc * 8: cc * 8 + NE],
                               start=(cc == 0), stop=(cc == 7))
                    acopy(glA[:, 0:W], psD[0:128, 0:W])
                    rmax3(gmax[:, 0:NT], glA)
                    _o, _a, _b = v3(glA), v3(glA), bc0(gmax)
                    p.op("dve", lambda h, _o=_o, _a=_a, _b=_b: h.tensor_tensor(
                        _o, _a, _b, ALU.subtract),
                        reads=[glA[:, 0:W], gmax[:, 0:NT]], writes=[glA[:, 0:W]])
                    act(glE[:, 0:W], glA[:, 0:W], AF.Exp, tag="gexp")
                    rsum3(gsum[:, 0:NT], glE)
                    recipo(gsum[:, 0:NT], gsum[:, 0:NT])
                    _o, _a, _b = v3(gpA), v3(glE), bc0(gsum)
                    p.op("dve", lambda h, _o=_o, _a=_a, _b=_b: h.tensor_tensor(
                        _o, _a, _b, ALU.mult),
                        reads=[glE[:, 0:W], gsum[:, 0:NT]], writes=[gpA[:, 0:W]])
                    rmax3(gmx1[:, 0:NT], gpA)
                    _o, _a, _b = v3(m1kA), v3(gpA), bc0(gmx1)
                    p.op("dve", lambda h, _o=_o, _a=_a, _b=_b: h.tensor_tensor(
                        _o, _a, _b, ALU.is_ge),
                        reads=[gpA[:, 0:W], gmx1[:, 0:NT]], writes=[m1kA[:, 0:W]])
                    stt(pexA[:, 0:W], m1kA[:, 0:W], -1e30, gpA[:, 0:W],
                        ALU.mult, ALU.add)
                    rmax3(gmx2[:, 0:NT], pexA)
                    _o, _a, _b = v3(cmbA), v3(gpA), bc0(gmx2)
                    p.op("dve", lambda h, _o=_o, _a=_a, _b=_b: h.tensor_tensor(
                        _o, _a, _b, ALU.is_ge),
                        reads=[gpA[:, 0:W], gmx2[:, 0:NT]], writes=[cmbA[:, 0:W]])
                    tt(cmbA[:, 0:W], cmbA[:, 0:W], gpA[:, 0:W], ALU.mult)
                    tsc(mskA[:, 0:W], cmbA[:, 0:W], 0.0, None, ALU.is_gt)
                    tt(m2kA[:, 0:W], mskA[:, 0:W], m1kA[:, 0:W], ALU.subtract)
                    p.op("dve", lambda h, mbfA=mbfA, mskA=mskA:
                         h.tensor_copy(mbfA[:, 0:W], mskA[:, 0:W]),
                         reads=[mskA[:, 0:W]], writes=[mbfA[:, 0:W]])
                    for t in range(NT):
                        mm(psC[0:128, t * NE:(t + 1) * NE], triu[:, :],
                           mbfA[:, t * NE:(t + 1) * NE],
                           start=True, stop=True, tag="rank")
                    mm(psC[0:1, 128:128 + W], onesc_bf[:, 0:1], mbfA[:, 0:W],
                       start=True, stop=True, tag="tot")
                    acopy(rnkA[:, 0:W], psC[0:128, 0:W])
                    acopy(totsb[0:1, 0:W], psC[0:1, 128:128 + W])
                    p.op("dve", lambda h, cf=cf: h.memset(cf[0:1, 0:NE], 0.0),
                         writes=[cf[0:1, 0:NE]])
                    for t in range(1, NT):
                        tt(cf[0:1, t * NE:(t + 1) * NE],
                           cf[0:1, (t - 1) * NE:t * NE],
                           totsb[0:1, (t - 1) * NE:t * NE], ALU.add)
                    mm(psC[0:128, 256:256 + W], ones_f[0:1, 0:128], cf[0:1, 0:W],
                       start=True, stop=True, tag="carrybc")
                    tt(rnkF[:, 0:W], rnkA[:, 0:W], psC[0:128, 256:256 + W], ALU.add)
                    for k, mk in ((0, m1kA), (1, m2kA)):
                        tt(tmpA[:, 0:W], rnkF[:, 0:W], mk[:, 0:W], ALU.mult)
                        rsum3(rselA[:, k * NT:(k + 1) * NT], tmpA)
                        tt(tmpA[:, 0:W], lane8[:, 0:W], mk[:, 0:W], ALU.mult)
                        rsum3(eselA[:, k * NT:(k + 1) * NT], tmpA)
                        tt(tmpA[:, 0:W], cmbA[:, 0:W], mk[:, 0:W], ALU.mult)
                        rsum3(vselA[:, k * NT:(k + 1) * NT], tmpA)
                    tsc(ovfA[:, 0:2 * NT], rselA[:, 0:2 * NT], float(CAP), None,
                        ALU.is_ge)
                    tsc(flfA[:, 0:2 * NT], eselA[:, 0:2 * NT], float(CAP), None,
                        ALU.mult)
                    tt(flfA[:, 0:2 * NT], flfA[:, 0:2 * NT], rselA[:, 0:2 * NT],
                       ALU.add)
                    stt(flfA[:, 0:2 * NT], ovfA[:, 0:2 * NT], 1e9,
                        flfA[:, 0:2 * NT], ALU.mult, ALU.add)
                    p.op("dve", lambda h, fluA=fluA, flfA=flfA:
                         h.tensor_copy(fluA[:, 0:2 * NT], flfA[:, 0:2 * NT]),
                         reads=[flfA[:, 0:2 * NT]], writes=[fluA[:, 0:2 * NT]])
                    iout = bass.AP(idx_d.tensor, 0,
                                   [[1, n_seqs * NE * CAP], [1, 1]])
                    cvo = bass.AP(cval_d.tensor, 0,
                                  [[1, n_seqs * NE * CAP], [1, 1]])
                    ieoff = s * NE * CAP
                    for t in range(NT):
                        for k in range(2):
                            c = k * NT + t
                            p.op("pool", lambda h, t=t, c=c, iout=iout,
                                 ieoff=ieoff, fluA=fluA: h.indirect_dma_start(
                                     iout, IOA(ap=fluA[:, c:c + 1], axis=0),
                                     tokids[:, t:t + 1], None,
                                     element_offset=ieoff,
                                     bounds_check=rbC, oob_is_err=False),
                                reads=[fluA[:, c:c + 1], tokids[:, t:t + 1]],
                                writes=[idx_d[s, :]], queue=f"q_isc{k}")
                            p.op("pool", lambda h, c=c, cvo=cvo, ieoff=ieoff,
                                 fluA=fluA, vselA=vselA: h.indirect_dma_start(
                                     cvo, IOA(ap=fluA[:, c:c + 1], axis=0),
                                     vselA[:, c:c + 1], None,
                                     element_offset=ieoff,
                                     bounds_check=rbC, oob_is_err=False),
                                reads=[fluA[:, c:c + 1], vselA[:, c:c + 1]],
                                writes=[cval_d[s, :]], queue=f"q_vsc{k}")
                    p.barrier()

                hsb = b_es.enter_context(sb(f"hsb_{s}", [128, 11 * 1024], BF16))
                sgb = b_es.enter_context(sb(f"sgb_{s}", [128, 512], BF16))
                usb = b_es.enter_context(sb(f"usb_{s}", [128, 512], BF16))
                sgb2 = sgb
                usb2 = usb

                sgbs = [sgb, sgb2]
                usbs = [usb, usb2]
                for hf in range(NHALF):
                    h0 = hf * hw
                    for ic in range(11):
                        for j0 in range(0, hw, 512):
                            nw = min(512, hw - j0)
                            gb = (psC, psA0)[ic % 2]
                            ub = (psD, psA1)[ic % 2]
                            sg_ = sgbs[ic % 2]
                            us_ = usbs[ic % 2]
                            for cc in range(8):
                                mm(gb[0:128, 0:nw],
                                   w1b[:, cc * I_FF + ic * 128: cc * I_FF + ic * 128 + 128],
                                   xn[:, cc * S + h0 + j0: cc * S + h0 + j0 + nw],
                                   start=(cc == 0), stop=(cc == 7), tag="g")
                            for cc in range(8):
                                mm(ub[0:128, 0:nw],
                                   w2b[:, cc * I_FF + ic * 128: cc * I_FF + ic * 128 + 128],
                                   xn[:, cc * S + h0 + j0: cc * S + h0 + j0 + nw],
                                   start=(cc == 0), stop=(cc == 7), tag="u")
                            act(sg_[:, 0:nw], gb[0:128, 0:nw], AF.Silu)
                            acopy(us_[:, 0:nw], ub[0:128, 0:nw])
                            tt(hsb[:, ic * hw + j0: ic * hw + j0 + nw],
                               sg_[:, 0:nw], us_[:, 0:nw], ALU.mult, tag="h")
                    for cc in range(8):
                        for j0 in range(0, hw, 512):
                            nw = min(512, hw - j0)
                            pst = psB0 if (cc % 2 == 0) else psB1
                            for ic in range(11):
                                mm(pst[0:128, 0:nw],
                                   w3b[:, ic * 1024 + cc * 128: ic * 1024 + cc * 128 + 128],
                                   hsb[:, ic * hw + j0: ic * hw + j0 + nw],
                                   start=(ic == 0), stop=(ic == 10), tag="o")
                            tt(yf[:, cc * S + h0 + j0: cc * S + h0 + j0 + nw],
                               yf[:, cc * S + h0 + j0: cc * S + h0 + j0 + nw],
                               pst[0:128, 0:nw], ALU.add, tag="yacc")
                p.barrier()

                ost = [b_es.enter_context(sb(f"ost_{s}_{i}", [128, D], F32))
                       for i in range(2)]
                with contextlib.ExitStack() as fl_es:
                    xt2 = [fl_es.enter_context(sb(f"xt2_{s}_{i}", [128, D], F32))
                           for i in range(2)]
                    ob = ost
                    p.op("sp", lambda h, s=s: h.dma_start(
                        ysc_d[s, S:S + 128, :], zrow[:, :]),
                        reads=[zrow[:, :]], writes=[ysc_d[s, S:S + 128, :]],
                        queue="q_zr")
                    for t in range(NT):
                        xb = xt2[t % 2]
                        o = ob[t % 2]
                        p.op("sp", lambda h, t=t, xb=xb, s=s: h.dma_start(
                            xb[:, :], x_d[s, t * 128:(t + 1) * 128, :]),
                            writes=[xb[:, :]], queue=f"q_xt2{t % 2}")
                        for fo in range(8):
                            dst = (psA[:, fo * 128:(fo + 1) * 128] if fo < 4
                                   else psB[:, (fo - 4) * 128:(fo - 3) * 128])
                            transpose(dst,
                                      yf[:, fo * S + t * 128: fo * S + (t + 1) * 128],
                                      idf[:, :])
                        tt(o[:, 0:512], psA[:, 0:512], xb[:, 0:512], ALU.add)
                        tt(o[:, 512:1024], psB[:, 0:512], xb[:, 512:1024], ALU.add)
                        p.op("sp", lambda h, t=t, o=o, s=s: h.dma_start(
                            ysc_d[s, t * 128:(t + 1) * 128, :], o[:, :]),
                            reads=[o[:, :]], writes=[ysc_d[s, t * 128:(t + 1) * 128, :]],
                            queue=f"q_ob{t % 2}")
                    p.barrier()
                yf_es.close()

                xngt = [b_es.enter_context(sb(f"xngt_{s}_{i}", [128, D], BF16))
                        for i in range(2)]
                xngf = b_es.enter_context(sb(f"xngf_{s}", [128, 8 * CAP], BF16))
                ofm = b_es.enter_context(sb(f"ofm_{s}", [128, 8 * CAP], BF16))
                idxe = b_es.enter_context(sb(f"idxe_{s}", [128, NJT], U32))
                idxf = b_es.enter_context(sb(f"idxf_{s}", [128, NJT], F32))
                vld = b_es.enter_context(sb(f"vld_{s}", [128, NJT], F32))
                cscg = [b_es.enter_context(sb(f"csc_{s}_{i}", [128, NJT], F32))
                        for i in range(2)]
                sxf = b_es.enter_context(sb(f"sxf_{s}", [128, NJT], F32))
                sxug = [b_es.enter_context(sb(f"sxu_{s}_{i}", [128, NJT], U32))
                        for i in range(2)]
                pend_out = []
                t9 = b_es.enter_context(sb(f"t9_{s}", [128, NJT], F32))

                def gather_stage(e):
                    csc = cscg[e % 2]
                    sxu = sxug[e % 2]
                    iap = bass.AP(idx_d.tensor, (s * NE + e) * CAP,
                                  [[NJT, 128], [1, NJT]])
                    p.op("sp", lambda h, iap=iap, idxe=idxe: h.dma_start(
                        idxe[:, :], iap),
                        reads=[idx_d[s, :]], writes=[idxe[:, :]], queue="q_idxe")
                    cap_ = bass.AP(cval_d.tensor, (s * NE + e) * CAP,
                                   [[NJT, 128], [1, NJT]])
                    p.op("sp", lambda h, cap_=cap_, csc=csc: h.dma_start(
                        csc[:, :], cap_),
                        reads=[cval_d[s, :]], writes=[csc[:, :]], queue=f"q_csc{e % 2}")
                    for j in range(NJT):
                        gt_ = xngt[j % 2]
                        xnd_flat = bass.AP(xnd_d.tensor, 0,
                                           [[D, n_seqs * S], [1, D]])
                        p.op("pool", lambda h, gt_=gt_, s=s, j=j, idxe=idxe,
                             xnd_flat=xnd_flat: h.indirect_dma_start(
                                 gt_[:, :], None, xnd_flat,
                                 IOA(ap=idxe[:, j:j + 1], axis=0),
                                 element_offset=s * S * D,
                                 bounds_check=rbS, oob_is_err=False),
                             reads=[idxe[:, j:j + 1], xnd_d[s]],
                             writes=[gt_[:, :]], queue=f"q_xg{j % 2}")
                        p.op("dve", lambda h, j=j, idxe=idxe, idxf=idxf:
                             h.tensor_copy(idxf[:, j:j + 1], idxe[:, j:j + 1]),
                             reads=[idxe[:, j:j + 1]], writes=[idxf[:, j:j + 1]])
                        tsc(vld[:, j:j + 1], idxf[:, j:j + 1], float(S - 1), None,
                            ALU.is_le)
                        tsc(t9[:, j:j + 1], vld[:, j:j + 1], -1.0, 1.0,
                            ALU.mult, ALU.add)
                        tt(t9[:, j:j + 1], t9[:, j:j + 1], padid[:, 0:1], ALU.mult)
                        tt(sxf[:, j:j + 1], idxf[:, j:j + 1], vld[:, j:j + 1],
                           ALU.mult)
                        tt(sxf[:, j:j + 1], sxf[:, j:j + 1], t9[:, j:j + 1],
                           ALU.add)
                        p.op("dve", lambda h, j=j, sxu=sxu, sxf=sxf:
                             h.tensor_copy(sxu[:, j:j + 1], sxf[:, j:j + 1]),
                             reads=[sxf[:, j:j + 1]], writes=[sxu[:, j:j + 1]])
                        for cc in range(8):
                            transpose(psT[:, cc * 128:(cc + 1) * 128],
                                      gt_[:, cc * 128:(cc + 1) * 128], idb[:, :])
                        p.op("act", lambda h, j=j, xngf=xngf: h.copy(
                            bass.AP(xngf, j * 128, [[8 * CAP, 128], [CAP, 8], [1, 128]]),
                            psT[:, :]),
                            reads=[psT[:, :]],
                            writes=[bass.AP(xngf, j * 128,
                                            [[8 * CAP, 128], [CAP, 8], [1, 128]])],
                            tag="xg_evac")

                gather_stage(0)
                for e in range(n_experts):
                    csc = cscg[e % 2]
                    sxu = sxug[e % 2]
                    pairs = (chunk_pair(w1b, w1t_d, e * D * I_FF, 8, I_FF, I_FF),
                             chunk_pair(w2b, w2t_d, e * D * I_FF, 8, I_FF, I_FF),
                             chunk_pair(w3b, w3t_d, e * I_FF * D, 11, 1024, D))
                    for wbn, (wb_, (_d, _s)) in enumerate(zip((w1b, w2b, w3b), pairs)):
                        p.op("pool", lambda h, _d=_d, _s=_s: h.dma_start(_d, _s),
                             writes=[wb_[:, :]], queue=f"q_w{wbn}b")
                    if e > 0:
                        gather_stage(e)
                    for f_ in pend_out:
                        f_()
                    pend_out.clear()
                    for ic in range(11):
                        for j0 in range(0, CAP, 512):
                            nw = min(512, CAP - j0)
                            gb = (psC, psA0)[ic % 2]
                            ub = (psD, psA1)[ic % 2]
                            sg_ = sgbs[ic % 2]
                            us_ = usbs[ic % 2]
                            for cc in range(8):
                                mm(gb[0:128, 0:nw],
                                   w1b[:, cc * I_FF + ic * 128: cc * I_FF + ic * 128 + 128],
                                   xngf[:, cc * CAP + j0: cc * CAP + j0 + nw],
                                   start=(cc == 0), stop=(cc == 7), tag="g")
                            for cc in range(8):
                                mm(ub[0:128, 0:nw],
                                   w2b[:, cc * I_FF + ic * 128: cc * I_FF + ic * 128 + 128],
                                   xngf[:, cc * CAP + j0: cc * CAP + j0 + nw],
                                   start=(cc == 0), stop=(cc == 7), tag="u")
                            act(sg_[:, 0:nw], gb[0:128, 0:nw], AF.Silu)
                            acopy(us_[:, 0:nw], ub[0:128, 0:nw])
                            tt(hsb[:, ic * CAP + j0: ic * CAP + j0 + nw],
                               sg_[:, 0:nw], us_[:, 0:nw], ALU.mult, tag="h")
                    for cc in range(8):
                        for j0 in range(0, CAP, 512):
                            nw = min(512, CAP - j0)
                            pst = psB0 if (cc % 2 == 0) else psB1
                            for ic in range(11):
                                mm(pst[0:128, 0:nw],
                                   w3b[:, ic * 1024 + cc * 128: ic * 1024 + cc * 128 + 128],
                                   hsb[:, ic * CAP + j0: ic * CAP + j0 + nw],
                                   start=(ic == 0), stop=(ic == 10), tag="o")
                            acopy(ofm[:, cc * CAP + j0: cc * CAP + j0 + nw],
                                  pst[0:128, 0:nw])
                    def out_stage(s=s, e=e, csc2=csc, sxu2=sxu):
                        for j in range(NJT):
                            for cc in range(8):
                                transpose(psT[:, cc * 128:(cc + 1) * 128],
                                          ofm[:, cc * CAP + j * 128: cc * CAP + (j + 1) * 128],
                                          idb[:, :])
                            o_ = ost[j % 2]
                            p.op("dve", lambda h, o_=o_, j=j: h.tensor_scalar(
                                o_[:, :], psT[:, :], csc2[:, j:j + 1], None, ALU.mult),
                                reads=[psT[:, :], csc2[:, j:j + 1]], writes=[o_[:, :]],
                                tag="oscale")
                            ysc_flat = bass.AP(ysc_d.tensor, 0,
                                               [[D, n_seqs * (S + 128)], [1, D]])
                            p.op("pool", lambda h, o_=o_, s=s, j=j:
                                 h.indirect_dma_start(
                                     ysc_flat, IOA(ap=sxu2[:, j:j + 1], axis=0),
                                     o_[:, :], None,
                                     element_offset=s * (S + 128) * D,
                                     compute_op=ALU.add),
                                 reads=[sxu2[:, j:j + 1], o_[:, :]],
                                 writes=[ysc_d[s]], queue="q_ysc")
                    pend_out.append(out_stage)
                for f_ in pend_out:
                    f_()
                pend_out.clear()
                p.barrier()

            for q4 in range(4):
                r0 = q4 * (S // 4)
                r1 = (q4 + 1) * (S // 4)
                p.op("sp", lambda h, s=s, r0=r0, r1=r1: h.dma_start(
                    out_d[s, r0:r1, :], ysc_d[s, r0:r1, :]),
                    reads=[ysc_d[s, r0:r1, :]],
                    writes=[out_d[s, r0:r1, :]], queue="q_fin")
            p.barrier()

    p.finalize()
    es.close()
    return nc



def _rel_bucket(rel_pos, num_buckets=32, max_distance=128):
    nb = num_buckets // 2
    buckets = (rel_pos > 0).astype(np.int32) * nb
    rel = np.abs(rel_pos)
    max_exact = nb // 2
    is_small = rel < max_exact
    rel_if_large = max_exact + (
        np.log(np.maximum(rel, 1).astype(np.float32) / max_exact)
        / math.log(max_distance / max_exact) * (nb - max_exact)
    ).astype(np.int32)
    rel_if_large = np.minimum(rel_if_large, nb - 1)
    return buckets + np.where(is_small, rel, rel_if_large)


def make_bias_skew(rel_emb, S):
    d = np.arange(-(S - 1), S)
    buckets = _rel_bucket(d)
    master = np.asarray(rel_emb)[buckets, :]
    SKW = 2 * S - 128
    j = np.arange(SKW)
    pp = np.arange(128)
    idx = pp[:, None] + (2 * S - 129) - j[None, :]
    skw = master[idx, :]
    return np.ascontiguousarray(skw.transpose(2, 0, 1)).astype(np.float32)


def make_inputs(x_shard, Wq, Wk, Wv, Wo, rel_emb, Wg, W1, W2, W3,
                Ws1, Ws2, Ws3, S):
    c = np.ascontiguousarray
    f = np.float32
    return {
        "x": c(np.asarray(x_shard), dtype=f),
        "wq": c(np.asarray(Wq), dtype=f),
        "wk": c(np.asarray(Wk), dtype=f),
        "wv": c(np.asarray(Wv), dtype=f),
        "wo": c(np.asarray(Wo), dtype=f),
        "wgt": c(np.asarray(Wg).T, dtype=f),
        "w1t": c(np.asarray(W1).transpose(0, 2, 1), dtype=f),
        "w2t": c(np.asarray(W2).transpose(0, 2, 1), dtype=f),
        "w3t": c(np.asarray(W3).transpose(0, 2, 1), dtype=f),
        "ws1t": c(np.asarray(Ws1).T, dtype=f),
        "ws2t": c(np.asarray(Ws2).T, dtype=f),
        "ws3t": c(np.asarray(Ws3).T, dtype=f),
        "bsk": make_bias_skew(rel_emb, S),
        "idb": np.eye(128, dtype=ml_dtypes.bfloat16),
        "idf": np.eye(128, dtype=np.float32),
        "triu": np.triu(np.ones((128, 128)), 1).astype(ml_dtypes.bfloat16),
        "lane8": np.tile(np.arange(8, dtype=np.float32), (128, 16)).copy(),
        "tokids": (np.arange(128, dtype=np.uint32)[:, None]
                   + 128 * np.arange(16, dtype=np.uint32)[None, :]).copy(),
        "oobi": np.full(8 * 768, 10 ** 9, dtype=np.uint32),
        "rowid": np.arange(128, dtype=np.float32)[:, None].copy(),
    }


def kernel(L0_attn_hidden_states, L0_attn_Wq, L0_attn_Wk, L0_attn_Wv,
           L0_attn_Wo, L0_attn_rel_emb, L0_ffn_Wg, L0_ffn_W1, L0_ffn_W2,
           L0_ffn_W3, L0_ffn_Ws1, L0_ffn_Ws2, L0_ffn_Ws3):
    x = np.asarray(L0_attn_hidden_states)
    B, S, _ = x.shape
    per = B // NCORES
    nc = build_core(n_seqs=per, S=S)
    in_maps = []
    for c in range(NCORES):
        in_maps.append(make_inputs(
            x[c * per:(c + 1) * per], L0_attn_Wq, L0_attn_Wk, L0_attn_Wv,
            L0_attn_Wo, L0_attn_rel_emb, L0_ffn_Wg, L0_ffn_W1, L0_ffn_W2,
            L0_ffn_W3, L0_ffn_Ws1, L0_ffn_Ws2, L0_ffn_Ws3, S))
    res = run_bass_kernel_spmd(nc, in_maps, core_ids=list(range(NCORES)))
    return np.concatenate([r["out"] for r in res.results], axis=0)


# version 28
# speedup vs baseline: 1.0205x; 1.0117x over previous
import contextlib
import math

import numpy as np
import ml_dtypes

import concourse.bass as bass
import concourse.mybir as mybir
from concourse.alu_op_type import AluOpType as ALU
from concourse.bass_utils import run_bass_kernel_spmd

IOA = bass.IndirectOffsetOnAxis

F32 = mybir.dt.float32
BF16 = mybir.dt.bfloat16
U32 = mybir.dt.uint32
AF = mybir.ActivationFunctionType
AX = mybir.AxisListType

D = 1024
NH = 16
DH = 64
I_FF = 1408
NE = 8
NCORES = 8

ENGINES = ("pe", "act", "dve", "pool", "sp")


def cdiv(a, b):
    return (a + b - 1) // b


class Prog:

    def __init__(self, nc):
        self.nc = nc
        self.per_eng = {e: [] for e in ENGINES}
        self.q_counts = {}
        self.writes = {}
        self.reads = {}
        self.waited = {e: {} for e in ENGINES}
        self.sems = {}
        self.es = None
        self.n_ops = 0

    def make_sems(self, es):
        self.es = es
        for e in ENGINES:
            self.sems[e] = es.enter_context(self.nc.semaphore(f"sem_{e}"))

    def _channel(self, q):
        if q not in self.q_counts:
            self.q_counts[q] = 0
            self.sems[q] = self.es.enter_context(self.nc.semaphore(f"sem_{q}"))

    @staticmethod
    def _foot(ap):
        t = ap.tensor
        name = t.name
        shape = list(t.shape)
        pitch = 1
        for s in shape[1:]:
            pitch *= s
        if len(shape) == 1:
            pitch = shape[0]
        off = ap.offset
        p0 = off // pitch
        c0 = off % pitch
        pn = 1
        cspan = 0
        for step, count in list(ap.ap):
            if count <= 1:
                continue
            if step > 0 and step % pitch == 0:
                pn += (count - 1) * (step // pitch)
            else:
                cspan += (count - 1) * abs(step)
        p1 = p0 + pn
        c1 = c0 + cspan + 1
        if "PSum" in type(t).__name__:
            esz = 2 if t.dtype == BF16 else 4
            bank = 2048 // esz
            c0 = (c0 // bank) * bank
            c1 = cdiv(c1, bank) * bank
        return (name, p0, p1, c0, c1)

    @staticmethod
    def _overlap(a, b):
        return a[1] < b[2] and b[1] < a[2] and a[3] < b[4] and b[3] < a[4]

    @staticmethod
    def _covers(a, b):
        return a[1] <= b[1] and a[2] >= b[2] and a[3] <= b[3] and a[4] >= b[4]

    def op(self, eng, fn, reads=(), writes=(), queue=None, tag=""):
        rf = [self._foot(a) for a in reads]
        wf = [self._foot(a) for a in writes]
        owner = queue if queue else eng
        edges = {}

        def add_edge(src, idx):
            if src == "pe" and owner == "pe":
                return
            if self.waited[eng].get(src, -1) >= idx:
                return
            if edges.get(src, -1) >= idx:
                return
            edges[src] = idx

        for f in rf:
            for w in self.writes.get(f[0], ()):
                if self._overlap(f, w["r"]):
                    add_edge(w["eng"], w["idx"])
        for f in wf:
            for w in self.writes.get(f[0], ()):
                if self._overlap(f, w["r"]):
                    add_edge(w["eng"], w["idx"])
            for r in self.reads.get(f[0], ()):
                if self._overlap(f, r["r"]):
                    add_edge(r["eng"], r["idx"])

        if queue:
            self._channel(queue)
            idx = self.q_counts[queue]
            self.q_counts[queue] += 1
        else:
            idx = self.n_ops
        self.n_ops += 1

        rec = {"fn": fn, "edges": edges, "queue": queue, "idx": idx,
               "inc": queue is not None, "tag": tag}
        self.per_eng[eng].append(rec)

        wd = self.waited[eng]
        for sKey, i in edges.items():
            if wd.get(sKey, -1) < i:
                wd[sKey] = i

        rec_eng = queue if queue else eng
        for f in wf:
            lst = self.writes.setdefault(f[0], [])
            lst[:] = [w for w in lst if not self._covers(f, w["r"])]
            lst.append({"r": f, "eng": rec_eng, "idx": idx})
            rl = self.reads.get(f[0])
            if rl:
                rl[:] = [r for r in rl if not self._covers(f, r["r"])]
        for f in rf:
            self.reads.setdefault(f[0], []).append(
                {"r": f, "eng": rec_eng, "idx": idx})
        return rec

    def barrier(self):
        lasts = {}
        for e in ENGINES:
            for rec in reversed(self.per_eng[e]):
                if rec["queue"] is None and rec["fn"] is not None:
                    lasts[e] = rec["idx"]
                    break
        for q, n in self.q_counts.items():
            if n:
                lasts[q] = n - 1
        for e in ENGINES:
            edges = {}
            for sKey, i in lasts.items():
                if sKey == e:
                    continue
                if self.waited[e].get(sKey, -1) >= i:
                    continue
                edges[sKey] = i
            if edges:
                self.per_eng[e].append({"fn": None, "edges": edges,
                                        "queue": None, "idx": None,
                                        "inc": False, "tag": "barrier"})
                for sKey, i in edges.items():
                    self.waited[e][sKey] = i

    def finalize(self):
        needed = {e: set() for e in ENGINES}
        for e in ENGINES:
            for rec in self.per_eng[e]:
                for sKey, i in rec["edges"].items():
                    if sKey in ENGINES:
                        needed[sKey].add(i)
        vals = {}
        for e in ENGINES:
            acc = 0
            v = {}
            for rec in self.per_eng[e]:
                if rec["queue"] is None and rec["fn"] is not None:
                    if rec["idx"] in needed[e]:
                        rec["inc"] = True
                        acc += 1
                    v[rec["idx"]] = acc
            vals[e] = v
        prog = self

        def emit(handle, eng):
            for rec in prog.per_eng[eng]:
                for sKey in sorted(rec["edges"], key=str):
                    i = rec["edges"][sKey]
                    if sKey in ENGINES:
                        handle.wait_ge(prog.sems[sKey], vals[sKey][i])
                    else:
                        handle.wait_ge(prog.sems[sKey], 16 * (i + 1))
                if rec["fn"] is None:
                    continue
                ins = rec["fn"](handle)
                if rec["queue"]:
                    ins.then_inc(prog.sems[rec["queue"]], 16)
                elif rec["inc"]:
                    if ins is None:
                        ins = handle.engine_nop() if hasattr(handle, "engine_nop") else handle.nop()
                    ins.then_inc(prog.sems[eng], 1)
            if eng == "sp":
                for q, n in sorted(prog.q_counts.items()):
                    if n:
                        handle.wait_ge(prog.sems[q], 16 * n)

        with prog.nc.Block() as block:
            @block.tensor
            def _(h):
                emit(h, "pe")

            @block.scalar
            def _(h):
                emit(h, "act")

            @block.vector
            def _(h):
                emit(h, "dve")

            @block.gpsimd
            def _(h):
                emit(h, "pool")

            @block.sync
            def _(h):
                emit(h, "sp")



def build_core(n_seqs=4, S=2048, n_experts=NE):
    KC = S // 128
    CAP = 768
    NJT = CAP // 128
    QHW = min(1024, S)
    NQH = S // QHW
    QGW = min(512, QHW)
    NQG = QHW // QGW
    NT = S // 128
    SKW = 2 * S - 128

    nc = bass.Bass("TRN2", target_bir_lowering=False)

    x_d = nc.dram_tensor("x", [n_seqs, S, D], F32, kind="ExternalInput").ap()
    wq_d = nc.dram_tensor("wq", [D, D], F32, kind="ExternalInput").ap()
    wk_d = nc.dram_tensor("wk", [D, D], F32, kind="ExternalInput").ap()
    wv_d = nc.dram_tensor("wv", [D, D], F32, kind="ExternalInput").ap()
    wo_d = nc.dram_tensor("wo", [D, D], F32, kind="ExternalInput").ap()
    wgt_d = nc.dram_tensor("wgt", [D, NE], F32, kind="ExternalInput").ap()
    w1t_d = nc.dram_tensor("w1t", [n_experts, D, I_FF], F32, kind="ExternalInput").ap()
    w2t_d = nc.dram_tensor("w2t", [n_experts, D, I_FF], F32, kind="ExternalInput").ap()
    w3t_d = nc.dram_tensor("w3t", [n_experts, I_FF, D], F32, kind="ExternalInput").ap()
    ws1t_d = nc.dram_tensor("ws1t", [D, I_FF], F32, kind="ExternalInput").ap()
    ws2t_d = nc.dram_tensor("ws2t", [D, I_FF], F32, kind="ExternalInput").ap()
    ws3t_d = nc.dram_tensor("ws3t", [I_FF, D], F32, kind="ExternalInput").ap()
    bsk_d = nc.dram_tensor("bsk", [NH, 128, SKW], F32, kind="ExternalInput").ap()
    triu_d = nc.dram_tensor("triu", [128, 128], BF16, kind="ExternalInput").ap()
    lane8_d = nc.dram_tensor("lane8", [128, 128], F32, kind="ExternalInput").ap()
    tokid_d = nc.dram_tensor("tokids", [128, 16], U32, kind="ExternalInput").ap()
    oobi_d = nc.dram_tensor("oobi", [NE * CAP], U32, kind="ExternalInput").ap()
    rowid_d = nc.dram_tensor("rowid", [128, 1], F32, kind="ExternalInput").ap()
    idb_d = nc.dram_tensor("idb", [128, 128], BF16, kind="ExternalInput").ap()
    idf_d = nc.dram_tensor("idf", [128, 128], F32, kind="ExternalInput").ap()
    out_d = nc.dram_tensor("out", [n_seqs, S, D], F32, kind="ExternalOutput").ap()
    xnd_d = nc.dram_tensor("xnd", [n_seqs, S, D], BF16).ap()
    cval_d = nc.dram_tensor("cvald", [n_seqs, NE * CAP], F32).ap()
    idx_d = nc.dram_tensor("idxd", [n_seqs, NE * CAP], U32).ap()
    ysc_d = nc.dram_tensor("yscd", [n_seqs, S + 128, D], F32).ap()

    es = contextlib.ExitStack()
    p = Prog(nc)
    p.make_sems(es)

    def sb(name, shape, dt, side=None):
        if side:
            return nc.sbuf_tensor(name, shape, dt, side=side)
        return nc.sbuf_tensor(name, shape, dt)

    idb = es.enter_context(sb("idb_s", [128, 128], BF16, side="right"))
    idf = es.enter_context(sb("idf_s", [128, 128], F32, side="right"))
    ones_f = es.enter_context(sb("ones_f", [1, 128], F32, side="right"))
    wgt_s = es.enter_context(sb("wgt_s", [128, NE * 8], BF16, side="right"))
    xn = es.enter_context(sb("xn", [128, 8 * S], BF16, side="right"))
    psA = es.enter_context(nc.psum_tensor("psA", [128, 1024], F32))
    psB = es.enter_context(nc.psum_tensor("psB", [128, 1024], F32))
    psC2 = es.enter_context(nc.psum_tensor("psC2", [128, 1024], F32))
    psE = es.enter_context(nc.psum_tensor("psE", [128, 512], F32))
    psT = es.enter_context(nc.psum_tensor("psT", [128, 1024], BF16))

    class BankView:

        def __init__(self, t, c0):
            self.t, self.c0 = t, c0

        def __getitem__(self, idx):
            r, c = idx
            cs = self.c0 + (c.start or 0)
            ce = self.c0 + c.stop
            return self.t[r, cs:ce]

    psC = BankView(psC2, 0)
    psD = BankView(psC2, 512)
    psA0 = BankView(psA, 0)
    psA1 = BankView(psA, 512)
    psB0 = BankView(psB, 0)
    psB1 = BankView(psB, 512)

    p.op("sp", lambda h: h.dma_start(idb[:, :], idb_d), writes=[idb[:, :]], queue="q_idb")
    p.op("sp", lambda h: h.dma_start(idf[:, :], idf_d), writes=[idf[:, :]], queue="q_idf")
    p.op("dve", lambda h: h.memset(ones_f[:, :], 1.0), writes=[ones_f[:, :]])

    def chunk_pair(dst, src_ap, base_off, n_ch, inner, src_row_pitch):
        t = src_ap.tensor
        d = bass.AP(dst, 0, [[n_ch * inner, 128], [inner, n_ch], [1, inner]])
        s_ = bass.AP(t, base_off,
                     [[src_row_pitch, 128], [128 * src_row_pitch, n_ch], [1, inner]])
        return d, s_

    def mm(out, lhsT, rhs, start, stop, tag=""):
        p.op("pe", lambda h: h.matmul(out, lhsT, rhs, start=start, stop=stop),
             reads=[lhsT, rhs], writes=[out], tag=tag)

    def transpose(out, in_, ident, tag=""):
        p.op("pe", lambda h: h.transpose(out, in_, ident),
             reads=[in_, ident], writes=[out], tag=tag)

    def act(out, in_, func, bias=None, scale=None, accum=None, tag=""):
        kw = {}
        if bias is not None:
            kw["bias"] = bias
        if scale is not None:
            kw["scale"] = scale
        if accum is not None:
            kw["accum_out"] = accum
        reads = [in_] + ([bias] if isinstance(bias, bass.AP) else [])
        wr = [out] + ([accum] if accum is not None else [])
        p.op("act", lambda h: h.activation(out, in_, func, **kw),
             reads=reads, writes=wr, tag=tag)

    def acopy(out, in_, tag=""):
        p.op("act", lambda h: h.copy(out, in_), reads=[in_], writes=[out], tag=tag)

    def tt(out, a, b, alu, tag=""):
        p.op("dve", lambda h: h.tensor_tensor(out, a, b, alu),
             reads=[a, b], writes=[out], tag=tag)

    def tsc(out, in_, s1, s2, op0, op1=None, tag=""):
        reads = [in_] + [q for q in (s1, s2) if isinstance(q, bass.AP)]
        if op1 is None:
            p.op("dve", lambda h: h.tensor_scalar(out, in_, s1, None, op0),
                 reads=reads, writes=[out], tag=tag)
        else:
            p.op("dve", lambda h: h.tensor_scalar(out, in_, s1, s2, op0, op1),
                 reads=reads, writes=[out], tag=tag)

    def stt(out, in0, scalar, in1, op0, op1, tag=""):
        p.op("dve", lambda h: h.scalar_tensor_tensor(out, in0, scalar, in1, op0, op1),
             reads=[in0, in1], writes=[out], tag=tag)

    def rmax(out, in_, tag=""):
        p.op("dve", lambda h: h.reduce_max(out, in_, axis=AX.X),
             reads=[in_], writes=[out], tag=tag)

    def recipo(out, in_, tag=""):
        p.op("dve", lambda h: h.reciprocal(out, in_),
             reads=[in_], writes=[out], tag=tag)

    _wgt_dst, _wgt_src = chunk_pair(wgt_s, wgt_d, 0, 8, NE, NE)
    p.op("pool", lambda h: h.dma_start(_wgt_dst, _wgt_src),
        writes=[wgt_s[:, :]], queue="q_wgt")
    triu = es.enter_context(sb("triu_s", [128, 128], BF16, side="right"))
    tokids = es.enter_context(sb("tokids_s", [128, 16], U32, side="right"))
    rowid = es.enter_context(sb("rowid_s", [128, 1], F32, side="right"))
    zrow = es.enter_context(sb("zrow_s", [128, D], F32, side="right"))
    p.op("sp", lambda h: h.dma_start(triu[:, :], triu_d), writes=[triu[:, :]], queue="q_triu")
    p.op("sp", lambda h: h.dma_start(tokids[:, :], tokid_d), writes=[tokids[:, :]], queue="q_tokids")
    p.op("sp", lambda h: h.dma_start(rowid[:, :], rowid_d),
         writes=[rowid[:, :]], queue="q_rowid")
    onesc_bf = es.enter_context(sb("onesc_bf", [128, 1], BF16, side="right"))
    padid = es.enter_context(sb("padid_s", [128, 1], F32, side="right"))
    p.op("dve", lambda h: h.memset(zrow[:, :], 0.0), writes=[zrow[:, :]])
    p.op("dve", lambda h: h.memset(onesc_bf[:, :], 1.0), writes=[onesc_bf[:, :]])
    tsc(padid[:, 0:1], rowid[:, 0:1], float(S), None, ALU.add)
    rbS = nc.gpsimd.alloc_register("rbS")
    rbC = nc.gpsimd.alloc_register("rbC")
    p.op("pool", lambda h: h.reg_mov(rbS, S - 1))
    p.op("pool", lambda h: h.reg_mov(rbC, NE * CAP - 1))

    for s in range(n_seqs):
        with contextlib.ExitStack() as s_es:
            with contextlib.ExitStack() as ln_es:
                xst = [ln_es.enter_context(sb(f"xst{s}_{i}", [128, D], F32))
                       for i in range(2)]
                xnt = [ln_es.enter_context(sb(f"xnt{s}_{i}", [128, D], BF16))
                       for i in range(2)]
                bn6 = ln_es.enter_context(sb(f"bn6_{s}", [128, 12], F32))
                bn2 = ln_es.enter_context(sb(f"bn2_{s}", [128, 2], F32))
                nmean = ln_es.enter_context(sb(f"nmean_{s}", [128, 1], F32))
                rstd = ln_es.enter_context(sb(f"rstd_{s}", [128, 1], F32))
                for t in range(NT):
                    xb = xst[t % 2]
                    nb = xnt[t % 2]
                    p.op("sp", lambda h, t=t, xb=xb, s=s: h.dma_start(
                        xb[:, :], x_d[s, t * 128:(t + 1) * 128, :]),
                        writes=[xb[:, :]], queue=f"q_xst{t % 2}")
                    p.op("dve", lambda h, xb=xb, bn6=bn6: h.bn_stats(bn6[:, 0:6], xb[:, 0:512]),
                         reads=[xb[:, 0:512]], writes=[bn6[:, 0:6]])
                    p.op("dve", lambda h, xb=xb, bn6=bn6: h.bn_stats(bn6[:, 6:12], xb[:, 512:1024]),
                         reads=[xb[:, 512:1024]], writes=[bn6[:, 6:12]])
                    p.op("dve", lambda h, bn2=bn2, bn6=bn6: h.bn_aggr(bn2[:, :], bn6[:, :]),
                         reads=[bn6[:, :]], writes=[bn2[:, :]])
                    tsc(nmean[:, :], bn2[:, 0:1], -1.0, None, ALU.mult)
                    tsc(rstd[:, :], bn2[:, 1:2], 1e-5, None, ALU.add)
                    act(rstd[:, :], rstd[:, :], AF.Ln)
                    tsc(rstd[:, :], rstd[:, :], -0.5, None, ALU.mult)
                    act(rstd[:, :], rstd[:, :], AF.Exp)
                    tsc(nb[:, :], xb[:, :], nmean[:, 0:1], rstd[:, 0:1],
                        ALU.add, ALU.mult)
                    p.op("sp", lambda h, t=t, nb=nb, s=s: h.dma_start(
                        xnd_d[s, t * 128:(t + 1) * 128, :], nb[:, :]),
                        reads=[nb[:, :]], writes=[xnd_d[s, t * 128:(t + 1) * 128, :]],
                        queue=f"q_xnd{t % 2}")
                    for cc in range(8):
                        transpose(psT[:, cc * 128:(cc + 1) * 128],
                                  nb[:, cc * 128:(cc + 1) * 128], idb[:, :])
                    p.op("act", lambda h, t=t: h.copy(
                        bass.AP(xn, t * 128, [[8 * S, 128], [S, 8], [1, 128]]),
                        psT[:, :]),
                        reads=[psT[:, :]],
                        writes=[bass.AP(xn, t * 128, [[8 * S, 128], [S, 8], [1, 128]])],
                        tag="xn_evac")
                p.barrier()

            ab_es = s_es.enter_context(contextlib.ExitStack())
            ao = ab_es.enter_context(sb(f"ao_{s}", [128, 8 * S], BF16))
            with contextlib.ExitStack() as at_es:
                qf = at_es.enter_context(sb(f"qf_{s}", [128, 4 * S], BF16))
                kf = at_es.enter_context(sb(f"kf_{s}", [128, 4 * S], BF16))
                vt = at_es.enter_context(sb(f"vt_{s}", [128, KC * 520], BF16))
                prb = at_es.enter_context(sb(f"prb_{s}", [128, KC * QHW], BF16))
                skw = [at_es.enter_context(sb(f"skw_{s}_{i}", [128, SKW], BF16))
                       for i in range(2)]
                wpr = [at_es.enter_context(sb(f"wpr_{s}_{i}", [128, 8 * 512], BF16))
                       for i in range(2)]
                recip = at_es.enter_context(sb(f"recip_{s}", [1, 1024], F32))
                rbc = at_es.enter_context(sb(f"rbc_{s}", [64, 1024], F32))
                aosum = at_es.enter_context(sb(f"aosum_{s}", [128, 1024], F32))

                p.op("dve", lambda h, vt=vt: h.memset(vt[:, :], 1.0), writes=[vt[:, :]])

                pending = []
                for g in range(2):
                    for wi, (wd, dst) in enumerate(((wk_d, kf), (wq_d, qf))):
                        wb = wpr[wi]
                        _d, _s = chunk_pair(wb, wd, g * 512, 8, 512, D)
                        p.op("pool", lambda h, _d=_d, _s=_s: h.dma_start(_d, _s),
                            writes=[wb[:, :]], queue=f"q_wpr{wi}")
                        for fl in range(4):
                            for n0 in range(0, S, 512):
                                nw = min(512, S - n0)
                                pst = psC if (fl % 2 == 0) else psD
                                for cc in range(8):
                                    mm(pst[0:128, 0:nw],
                                       wb[:, cc * 512 + fl * 128: cc * 512 + fl * 128 + 128],
                                       xn[:, cc * S + n0: cc * S + n0 + nw],
                                       start=(cc == 0), stop=(cc == 7))
                                acopy(dst[:, fl * S + n0: fl * S + n0 + nw],
                                      pst[0:128, 0:nw])
                    wb = wpr[0]
                    _d, _s = chunk_pair(wb, wv_d, g * 512, 8, 512, D)
                    p.op("pool", lambda h, _d=_d, _s=_s: h.dma_start(_d, _s),
                        writes=[wb[:, :]], queue="q_wpr0")
                    for t in range(NT):
                        pst = psC if (t % 2 == 0) else psD
                        for cc in range(8):
                            mm(pst[:, 0:512],
                               xn[:, cc * S + t * 128: cc * S + (t + 1) * 128],
                               wb[:, cc * 512:(cc + 1) * 512],
                               start=(cc == 0), stop=(cc == 7))
                        vap = bass.AP(vt, t * 520, [[KC * 520, 128], [65, 8], [1, 64]])
                        p.op("act", lambda h, pst=pst, vap=vap: h.copy(vap, pst[:, 0:512]),
                             reads=[pst[:, 0:512]],
                             writes=[vt[:, t * 520:(t + 1) * 520]])

                    for hl in range(8):
                        hh = g * 8 + hl
                        sk = skw[hh % 2]
                        p.op("pool", lambda h, hh=hh, sk=sk: h.dma_start(
                            sk[:, :], bsk_d[hh, :, :]),
                            writes=[sk[:, :]], queue=f"q_skw{hh % 2}")
                        pbase = 64 * (hl % 2)
                        fcol = (hl // 2) * S
                        aocol = (hh // 2) * S
                        for qh in range(NQH):
                            q0 = qh * QHW

                            PVG = KC

                            def pv_ap(qg, r0, r1, gw=QGW):
                                if qg == 0:
                                    return psE[r0:r1, 0:gw]
                                return psT[r0:r1, 0:2 * gw].bitcast(F32)

                            def pv_evac(qg, first):
                                dst = aosum[0:65, qg * QGW:(qg + 1) * QGW]
                                srcp = pv_ap(qg, 0, 65)
                                if first:
                                    p.op("dve", lambda h, dst=dst, srcp=srcp:
                                         h.tensor_copy(dst, srcp),
                                         reads=[srcp], writes=[dst], tag="pvev")
                                else:
                                    tt(dst, dst, srcp, ALU.add, tag="pvev")

                            def pv_step(kc):
                                for qg in range(NQG):
                                    mm(pv_ap(qg, 0, 65),
                                       vt[:, kc * 520 + hl * 65: kc * 520 + hl * 65 + 65],
                                       prb[:, kc * QHW + qg * QGW: kc * QHW + (qg + 1) * QGW],
                                       start=(kc % PVG == 0),
                                       stop=(kc % PVG == PVG - 1 or kc == KC - 1),
                                       tag="pv")
                                if kc % PVG == PVG - 1 or kc == KC - 1:
                                    for qg in range(NQG):
                                        pv_evac(qg, kc < PVG)

                            flush_at = min(4, KC - 1)
                            for kc in range(KC):
                                if kc == flush_at:
                                    for f_ in pending:
                                        f_()
                                    pending.clear()
                                sc = (psA, psB, psC2)[kc % 3]
                                boff = q0 + (S - 128) - kc * 128
                                for j in range(0, QHW, 512):
                                    nw = min(512, QHW - j)
                                    mm(sc[:, j:j + nw], idb[:, :],
                                       sk[:, boff + j: boff + j + nw],
                                       start=True, stop=False, tag="biasmm")
                                for j in range(0, QHW, 512):
                                    nw = min(512, QHW - j)
                                    mm(sc[:, j:j + nw],
                                       kf[pbase:pbase + 64,
                                          fcol + kc * 128: fcol + (kc + 1) * 128],
                                       qf[pbase:pbase + 64,
                                          fcol + q0 + j: fcol + q0 + j + nw],
                                       start=False, stop=True, tag="scores")
                                act(prb[:, kc * QHW: (kc + 1) * QHW],
                                    sc[:, 0:QHW], AF.Exp, tag="exp")
                                if kc > 0:
                                    pv_step(kc - 1)
                            pv_step(KC - 1)

                            for qg in range(NQG):
                                qs = qg * QGW
                                p.op("dve", lambda h, qs=qs, recip=recip, aosum=aosum: h.reciprocal(
                                    recip[0:1, qs:qs + QGW],
                                    aosum[64:65, qs:qs + QGW]),
                                    reads=[aosum[64:65, qs:qs + QGW]],
                                    writes=[recip[0:1, qs:qs + QGW]], tag="recip")

                                def norm_part2(qs=qs, pbase=pbase, aocol=aocol,
                                               q0=q0, recip=recip, aosum=aosum,
                                               rbc=rbc):
                                    bk = psC2[0:64, qs:qs + QGW]
                                    mm(bk, ones_f[0:1, 0:64],
                                       recip[0:1, qs:qs + QGW],
                                       start=True, stop=True, tag="bcast")
                                    acopy(rbc[0:64, qs:qs + QGW], bk)
                                    tt(ao[pbase:pbase + 64,
                                          aocol + q0 + qs: aocol + q0 + qs + QGW],
                                       aosum[0:64, qs:qs + QGW],
                                       rbc[0:64, qs:qs + QGW],
                                       ALU.mult, tag="ao")
                                pending.append(norm_part2)
                for f_ in pending:
                    f_()
                pending.clear()
                p.barrier()

            yf_es = s_es.enter_context(contextlib.ExitStack())
            yf = yf_es.enter_context(sb(f"yf_{s}", [128, 8 * S], F32, side="right"))
            with contextlib.ExitStack() as op_es:
                wob = op_es.enter_context(sb(f"wob_{s}", [128, 8 * 1024], BF16))
                _d, _s = chunk_pair(wob, wo_d, 0, 8, 1024, D)
                p.op("pool", lambda h, _d=_d, _s=_s: h.dma_start(_d, _s),
                    writes=[wob[:, :]], queue="q_wob")
                for fo in range(8):
                    for n0 in range(0, S, 512):
                        nw = min(512, S - n0)
                        pst = psC if (fo % 2 == 0) else psD
                        for cc in range(8):
                            mm(pst[:, 0:nw],
                               wob[:, cc * 1024 + fo * 128: cc * 1024 + fo * 128 + 128],
                               ao[:, cc * S + n0: cc * S + n0 + nw],
                               start=(cc == 0), stop=(cc == 7))
                        acopy(yf[:, fo * S + n0: fo * S + n0 + nw], pst[:, 0:nw])
                p.barrier()
            ab_es.close()

            with contextlib.ExitStack() as b_es:
                hw = min(1024, S // 2)
                NHALF = S // hw
                w1b = b_es.enter_context(sb(f"w1b_{s}", [128, 8 * I_FF], BF16))
                w2b = b_es.enter_context(sb(f"w2b_{s}", [128, 8 * I_FF], BF16))
                w3b = b_es.enter_context(sb(f"w3b_{s}", [128, 11 * 1024], BF16))
                NIC = I_FF // 128
                for j in range(NIC):
                    for wbn, (wb_, wd_) in enumerate(((w1b, ws1t_d),
                                                      (w2b, ws2t_d))):
                        _d = bass.AP(wb_, j * 128,
                                     [[8 * I_FF, 128], [I_FF, 8], [1, 128]])
                        _s = bass.AP(wd_.tensor, j * 128,
                                     [[I_FF, 128], [128 * I_FF, 8], [1, 128]])
                        wr = [wb_[:, cc * I_FF + j * 128:
                                  cc * I_FF + j * 128 + 128]
                              for cc in range(8)]
                        p.op("pool", lambda h, _d=_d, _s=_s: h.dma_start(_d, _s),
                             writes=wr, queue=f"q_w{wbn}b")
                _d3, _s3 = chunk_pair(w3b, ws3t_d, 0, 11, 1024, D)
                p.op("pool", lambda h, _d=_d3, _s=_s3: h.dma_start(_d, _s),
                     writes=[w3b[:, :]], queue="q_w2b")
                with contextlib.ExitStack() as gt_es:
                    W = NT * NE
                    glA = gt_es.enter_context(sb(f"glA_{s}", [128, W], F32))
                    glE = gt_es.enter_context(sb(f"glE_{s}", [128, W], F32))
                    gpA = gt_es.enter_context(sb(f"gpA_{s}", [128, W], F32))
                    m1kA = gt_es.enter_context(sb(f"m1kA_{s}", [128, W], F32))
                    m2kA = gt_es.enter_context(sb(f"m2kA_{s}", [128, W], F32))
                    pexA = gt_es.enter_context(sb(f"pexA_{s}", [128, W], F32))
                    cmbA = gt_es.enter_context(sb(f"cmbA_{s}", [128, W], F32))
                    mskA = gt_es.enter_context(sb(f"mskA_{s}", [128, W], F32))
                    mbfA = gt_es.enter_context(sb(f"mbfA_{s}", [128, W], BF16))
                    tmpA = gt_es.enter_context(sb(f"tmpA_{s}", [128, W], F32))
                    rnkA = gt_es.enter_context(sb(f"rnkA_{s}", [128, W], F32))
                    rnkF = gt_es.enter_context(sb(f"rnkF_{s}", [128, W], F32))
                    gmax = gt_es.enter_context(sb(f"gmax_{s}", [128, NT], F32))
                    gsum = gt_es.enter_context(sb(f"gsum_{s}", [128, NT], F32))
                    gmx1 = gt_es.enter_context(sb(f"gmx1_{s}", [128, NT], F32))
                    gmx2 = gt_es.enter_context(sb(f"gmx2_{s}", [128, NT], F32))
                    totsb = gt_es.enter_context(sb(f"totsb_{s}", [1, W], F32))
                    cf = gt_es.enter_context(sb(f"cf_{s}", [1, W], F32))
                    rselA = gt_es.enter_context(sb(f"rselA_{s}", [128, 2 * NT], F32))
                    eselA = gt_es.enter_context(sb(f"eselA_{s}", [128, 2 * NT], F32))
                    vselA = gt_es.enter_context(sb(f"vselA_{s}", [128, 2 * NT], F32))
                    ovfA = gt_es.enter_context(sb(f"ovfA_{s}", [128, 2 * NT], F32))
                    flfA = gt_es.enter_context(sb(f"flfA_{s}", [128, 2 * NT], F32))
                    fluA = gt_es.enter_context(sb(f"fluA_{s}", [128, 2 * NT], U32))
                    lane8 = gt_es.enter_context(sb(f"lane8_{s}", [128, W], F32))
                    lsrc = bass.AP(lane8_d.tensor, 0, [[128, 128], [1, W]])
                    p.op("sp", lambda h, lane8=lane8, lsrc=lsrc: h.dma_start(
                        lane8[:, :], lsrc),
                        writes=[lane8[:, :]], queue="q_lane8")

                    p.op("sp", lambda h, s=s: h.dma_start(
                        idx_d[s, :], oobi_d),
                        writes=[idx_d[s, :]], queue="q_oobi")
                    cvout = bass.AP(cval_d.tensor, s * NE * CAP,
                                    [[48, 128], [1, 48]])
                    p.op("sp", lambda h, cvout=cvout: h.dma_start(
                        cvout, zrow[:, 0:48]),
                        reads=[zrow[:, 0:48]], writes=[cval_d[s, :]],
                        queue="q_cvz")

                    def rsum3(out, tns, tag=""):
                        in3 = bass.AP(tns, 0, [[W, 128], [NE, NT], [1, NE]])
                        p.op("dve", lambda h, out=out, in3=in3: h.reduce_sum(
                            out, in3, axis=AX.X),
                            reads=[tns[:, 0:W]], writes=[out], tag=tag)

                    def rmax3(out, tns, tag=""):
                        in3 = bass.AP(tns, 0, [[W, 128], [NE, NT], [1, NE]])
                        p.op("dve", lambda h, out=out, in3=in3: h.reduce_max(
                            out, in3, axis=AX.X),
                            reads=[tns[:, 0:W]], writes=[out], tag=tag)

                    def bc0(t16):
                        return bass.AP(t16, 0, [[NT, 128], [1, NT], [0, NE]])

                    def v3(tns):
                        return bass.AP(tns, 0, [[W, 128], [NE, NT], [1, NE]])

                    def ttb(out, a3, b3, alu, rd, tag=""):
                        p.op("dve", lambda h, out=out, a3=a3, b3=b3, alu=alu:
                             h.tensor_tensor(out, a3, b3, alu),
                             reads=rd, writes=[out[:, 0:W] if hasattr(out, "tensor") is False else out],
                             tag=tag)

                    for t in range(NT):
                        for cc in range(8):
                            mm(psD[0:128, t * NE:(t + 1) * NE],
                               xn[:, cc * S + t * 128: cc * S + (t + 1) * 128],
                               wgt_s[:, cc * 8: cc * 8 + NE],
                               start=(cc == 0), stop=(cc == 7))
                    acopy(glA[:, 0:W], psD[0:128, 0:W])
                    rmax3(gmax[:, 0:NT], glA)
                    _o, _a, _b = v3(glA), v3(glA), bc0(gmax)
                    p.op("dve", lambda h, _o=_o, _a=_a, _b=_b: h.tensor_tensor(
                        _o, _a, _b, ALU.subtract),
                        reads=[glA[:, 0:W], gmax[:, 0:NT]], writes=[glA[:, 0:W]])
                    act(glE[:, 0:W], glA[:, 0:W], AF.Exp, tag="gexp")
                    rsum3(gsum[:, 0:NT], glE)
                    recipo(gsum[:, 0:NT], gsum[:, 0:NT])
                    _o, _a, _b = v3(gpA), v3(glE), bc0(gsum)
                    p.op("dve", lambda h, _o=_o, _a=_a, _b=_b: h.tensor_tensor(
                        _o, _a, _b, ALU.mult),
                        reads=[glE[:, 0:W], gsum[:, 0:NT]], writes=[gpA[:, 0:W]])
                    rmax3(gmx1[:, 0:NT], gpA)
                    _o, _a, _b = v3(m1kA), v3(gpA), bc0(gmx1)
                    p.op("dve", lambda h, _o=_o, _a=_a, _b=_b: h.tensor_tensor(
                        _o, _a, _b, ALU.is_ge),
                        reads=[gpA[:, 0:W], gmx1[:, 0:NT]], writes=[m1kA[:, 0:W]])
                    stt(pexA[:, 0:W], m1kA[:, 0:W], -1e30, gpA[:, 0:W],
                        ALU.mult, ALU.add)
                    rmax3(gmx2[:, 0:NT], pexA)
                    _o, _a, _b = v3(cmbA), v3(gpA), bc0(gmx2)
                    p.op("dve", lambda h, _o=_o, _a=_a, _b=_b: h.tensor_tensor(
                        _o, _a, _b, ALU.is_ge),
                        reads=[gpA[:, 0:W], gmx2[:, 0:NT]], writes=[cmbA[:, 0:W]])
                    tt(cmbA[:, 0:W], cmbA[:, 0:W], gpA[:, 0:W], ALU.mult)
                    tsc(mskA[:, 0:W], cmbA[:, 0:W], 0.0, None, ALU.is_gt)
                    tt(m2kA[:, 0:W], mskA[:, 0:W], m1kA[:, 0:W], ALU.subtract)
                    p.op("dve", lambda h, mbfA=mbfA, mskA=mskA:
                         h.tensor_copy(mbfA[:, 0:W], mskA[:, 0:W]),
                         reads=[mskA[:, 0:W]], writes=[mbfA[:, 0:W]])
                    for t in range(NT):
                        mm(psC[0:128, t * NE:(t + 1) * NE], triu[:, :],
                           mbfA[:, t * NE:(t + 1) * NE],
                           start=True, stop=True, tag="rank")
                    mm(psC[0:1, 128:128 + W], onesc_bf[:, 0:1], mbfA[:, 0:W],
                       start=True, stop=True, tag="tot")
                    acopy(rnkA[:, 0:W], psC[0:128, 0:W])
                    acopy(totsb[0:1, 0:W], psC[0:1, 128:128 + W])
                    p.op("dve", lambda h, cf=cf: h.memset(cf[0:1, 0:NE], 0.0),
                         writes=[cf[0:1, 0:NE]])
                    for t in range(1, NT):
                        tt(cf[0:1, t * NE:(t + 1) * NE],
                           cf[0:1, (t - 1) * NE:t * NE],
                           totsb[0:1, (t - 1) * NE:t * NE], ALU.add)
                    mm(psC[0:128, 256:256 + W], ones_f[0:1, 0:128], cf[0:1, 0:W],
                       start=True, stop=True, tag="carrybc")
                    tt(rnkF[:, 0:W], rnkA[:, 0:W], psC[0:128, 256:256 + W], ALU.add)
                    for k, mk in ((0, m1kA), (1, m2kA)):
                        tt(tmpA[:, 0:W], rnkF[:, 0:W], mk[:, 0:W], ALU.mult)
                        rsum3(rselA[:, k * NT:(k + 1) * NT], tmpA)
                        tt(tmpA[:, 0:W], lane8[:, 0:W], mk[:, 0:W], ALU.mult)
                        rsum3(eselA[:, k * NT:(k + 1) * NT], tmpA)
                        tt(tmpA[:, 0:W], cmbA[:, 0:W], mk[:, 0:W], ALU.mult)
                        rsum3(vselA[:, k * NT:(k + 1) * NT], tmpA)
                    tsc(ovfA[:, 0:2 * NT], rselA[:, 0:2 * NT], float(CAP), None,
                        ALU.is_ge)
                    tsc(flfA[:, 0:2 * NT], eselA[:, 0:2 * NT], float(CAP), None,
                        ALU.mult)
                    tt(flfA[:, 0:2 * NT], flfA[:, 0:2 * NT], rselA[:, 0:2 * NT],
                       ALU.add)
                    stt(flfA[:, 0:2 * NT], ovfA[:, 0:2 * NT], 1e9,
                        flfA[:, 0:2 * NT], ALU.mult, ALU.add)
                    p.op("dve", lambda h, fluA=fluA, flfA=flfA:
                         h.tensor_copy(fluA[:, 0:2 * NT], flfA[:, 0:2 * NT]),
                         reads=[flfA[:, 0:2 * NT]], writes=[fluA[:, 0:2 * NT]])
                    iout = bass.AP(idx_d.tensor, 0,
                                   [[1, n_seqs * NE * CAP], [1, 1]])
                    cvo = bass.AP(cval_d.tensor, 0,
                                  [[1, n_seqs * NE * CAP], [1, 1]])
                    ieoff = s * NE * CAP
                    for t in range(NT):
                        for k in range(2):
                            c = k * NT + t
                            p.op("pool", lambda h, t=t, c=c, iout=iout,
                                 ieoff=ieoff, fluA=fluA: h.indirect_dma_start(
                                     iout, IOA(ap=fluA[:, c:c + 1], axis=0),
                                     tokids[:, t:t + 1], None,
                                     element_offset=ieoff,
                                     bounds_check=rbC, oob_is_err=False),
                                reads=[fluA[:, c:c + 1], tokids[:, t:t + 1]],
                                writes=[idx_d[s, :]], queue=f"q_isc{k}")
                            p.op("pool", lambda h, c=c, cvo=cvo, ieoff=ieoff,
                                 fluA=fluA, vselA=vselA: h.indirect_dma_start(
                                     cvo, IOA(ap=fluA[:, c:c + 1], axis=0),
                                     vselA[:, c:c + 1], None,
                                     element_offset=ieoff,
                                     bounds_check=rbC, oob_is_err=False),
                                reads=[fluA[:, c:c + 1], vselA[:, c:c + 1]],
                                writes=[cval_d[s, :]], queue=f"q_vsc{k}")
                    p.barrier()

                hsb = b_es.enter_context(sb(f"hsb_{s}", [128, 11 * 1024], BF16))
                sgb = b_es.enter_context(sb(f"sgb_{s}", [128, 512], BF16))
                usb = b_es.enter_context(sb(f"usb_{s}", [128, 512], BF16))
                sgb2 = sgb
                usb2 = usb

                sgbs = [sgb, sgb2]
                usbs = [usb, usb2]
                for hf in range(NHALF):
                    h0 = hf * hw
                    for ic in range(11):
                        for j0 in range(0, hw, 512):
                            nw = min(512, hw - j0)
                            gb = (psC, psA0)[ic % 2]
                            ub = (psD, psA1)[ic % 2]
                            sg_ = sgbs[ic % 2]
                            us_ = usbs[ic % 2]
                            for cc in range(8):
                                mm(gb[0:128, 0:nw],
                                   w1b[:, cc * I_FF + ic * 128: cc * I_FF + ic * 128 + 128],
                                   xn[:, cc * S + h0 + j0: cc * S + h0 + j0 + nw],
                                   start=(cc == 0), stop=(cc == 7), tag="g")
                            for cc in range(8):
                                mm(ub[0:128, 0:nw],
                                   w2b[:, cc * I_FF + ic * 128: cc * I_FF + ic * 128 + 128],
                                   xn[:, cc * S + h0 + j0: cc * S + h0 + j0 + nw],
                                   start=(cc == 0), stop=(cc == 7), tag="u")
                            act(sg_[:, 0:nw], gb[0:128, 0:nw], AF.Silu)
                            acopy(us_[:, 0:nw], ub[0:128, 0:nw])
                            tt(hsb[:, ic * hw + j0: ic * hw + j0 + nw],
                               sg_[:, 0:nw], us_[:, 0:nw], ALU.mult, tag="h")
                    for cc in range(8):
                        for j0 in range(0, hw, 512):
                            nw = min(512, hw - j0)
                            pst = psB0 if (cc % 2 == 0) else psB1
                            for ic in range(11):
                                mm(pst[0:128, 0:nw],
                                   w3b[:, ic * 1024 + cc * 128: ic * 1024 + cc * 128 + 128],
                                   hsb[:, ic * hw + j0: ic * hw + j0 + nw],
                                   start=(ic == 0), stop=(ic == 10), tag="o")
                            tt(yf[:, cc * S + h0 + j0: cc * S + h0 + j0 + nw],
                               yf[:, cc * S + h0 + j0: cc * S + h0 + j0 + nw],
                               pst[0:128, 0:nw], ALU.add, tag="yacc")
                p.barrier()

                ost = [b_es.enter_context(sb(f"ost_{s}_{i}", [128, D], F32))
                       for i in range(2)]
                with contextlib.ExitStack() as fl_es:
                    xt2 = [fl_es.enter_context(sb(f"xt2_{s}_{i}", [128, D], F32))
                           for i in range(2)]
                    ob = ost
                    p.op("sp", lambda h, s=s: h.dma_start(
                        ysc_d[s, S:S + 128, :], zrow[:, :]),
                        reads=[zrow[:, :]], writes=[ysc_d[s, S:S + 128, :]],
                        queue="q_zr")
                    for t in range(NT):
                        xb = xt2[t % 2]
                        o = ob[t % 2]
                        p.op("sp", lambda h, t=t, xb=xb, s=s: h.dma_start(
                            xb[:, :], x_d[s, t * 128:(t + 1) * 128, :]),
                            writes=[xb[:, :]], queue=f"q_xt2{t % 2}")
                        for fo in range(8):
                            dst = (psA[:, fo * 128:(fo + 1) * 128] if fo < 4
                                   else psB[:, (fo - 4) * 128:(fo - 3) * 128])
                            transpose(dst,
                                      yf[:, fo * S + t * 128: fo * S + (t + 1) * 128],
                                      idf[:, :])
                        tt(o[:, 0:512], psA[:, 0:512], xb[:, 0:512], ALU.add)
                        tt(o[:, 512:1024], psB[:, 0:512], xb[:, 512:1024], ALU.add)
                        p.op("sp", lambda h, t=t, o=o, s=s: h.dma_start(
                            ysc_d[s, t * 128:(t + 1) * 128, :], o[:, :]),
                            reads=[o[:, :]], writes=[ysc_d[s, t * 128:(t + 1) * 128, :]],
                            queue=f"q_ob{t % 2}")
                    p.barrier()
                yf_es.close()

                xngt = [b_es.enter_context(sb(f"xngt_{s}_{i}", [128, D], BF16))
                        for i in range(2)]
                xngf = b_es.enter_context(sb(f"xngf_{s}", [128, 8 * CAP], BF16))
                ofm = b_es.enter_context(sb(f"ofm_{s}", [128, 8 * CAP], BF16))
                idxe = b_es.enter_context(sb(f"idxe_{s}", [128, NJT], U32))
                idxf = b_es.enter_context(sb(f"idxf_{s}", [128, NJT], F32))
                vld = b_es.enter_context(sb(f"vld_{s}", [128, NJT], F32))
                cscg = [b_es.enter_context(sb(f"csc_{s}_{i}", [128, NJT], F32))
                        for i in range(2)]
                sxf = b_es.enter_context(sb(f"sxf_{s}", [128, NJT], F32))
                sxug = [b_es.enter_context(sb(f"sxu_{s}_{i}", [128, NJT], U32))
                        for i in range(2)]
                pend_out = []
                t9 = b_es.enter_context(sb(f"t9_{s}", [128, NJT], F32))

                def gather_stage(e):
                    csc = cscg[e % 2]
                    sxu = sxug[e % 2]
                    iap = bass.AP(idx_d.tensor, (s * NE + e) * CAP,
                                  [[NJT, 128], [1, NJT]])
                    p.op("sp", lambda h, iap=iap, idxe=idxe: h.dma_start(
                        idxe[:, :], iap),
                        reads=[idx_d[s, :]], writes=[idxe[:, :]], queue="q_idxe")
                    cap_ = bass.AP(cval_d.tensor, (s * NE + e) * CAP,
                                   [[NJT, 128], [1, NJT]])
                    p.op("sp", lambda h, cap_=cap_, csc=csc: h.dma_start(
                        csc[:, :], cap_),
                        reads=[cval_d[s, :]], writes=[csc[:, :]], queue=f"q_csc{e % 2}")
                    for j in range(NJT):
                        gt_ = xngt[j % 2]
                        xnd_flat = bass.AP(xnd_d.tensor, 0,
                                           [[D, n_seqs * S], [1, D]])
                        p.op("pool", lambda h, gt_=gt_, s=s, j=j, idxe=idxe,
                             xnd_flat=xnd_flat: h.indirect_dma_start(
                                 gt_[:, :], None, xnd_flat,
                                 IOA(ap=idxe[:, j:j + 1], axis=0),
                                 element_offset=s * S * D,
                                 bounds_check=rbS, oob_is_err=False),
                             reads=[idxe[:, j:j + 1], xnd_d[s]],
                             writes=[gt_[:, :]], queue=f"q_xg{j % 2}")
                        p.op("dve", lambda h, j=j, idxe=idxe, idxf=idxf:
                             h.tensor_copy(idxf[:, j:j + 1], idxe[:, j:j + 1]),
                             reads=[idxe[:, j:j + 1]], writes=[idxf[:, j:j + 1]])
                        tsc(vld[:, j:j + 1], idxf[:, j:j + 1], float(S - 1), None,
                            ALU.is_le)
                        tsc(t9[:, j:j + 1], vld[:, j:j + 1], -1.0, 1.0,
                            ALU.mult, ALU.add)
                        tt(t9[:, j:j + 1], t9[:, j:j + 1], padid[:, 0:1], ALU.mult)
                        tt(sxf[:, j:j + 1], idxf[:, j:j + 1], vld[:, j:j + 1],
                           ALU.mult)
                        tt(sxf[:, j:j + 1], sxf[:, j:j + 1], t9[:, j:j + 1],
                           ALU.add)
                        p.op("dve", lambda h, j=j, sxu=sxu, sxf=sxf:
                             h.tensor_copy(sxu[:, j:j + 1], sxf[:, j:j + 1]),
                             reads=[sxf[:, j:j + 1]], writes=[sxu[:, j:j + 1]])
                        for cc in range(8):
                            transpose(psT[:, cc * 128:(cc + 1) * 128],
                                      gt_[:, cc * 128:(cc + 1) * 128], idb[:, :])
                        p.op("act", lambda h, j=j, xngf=xngf: h.copy(
                            bass.AP(xngf, j * 128, [[8 * CAP, 128], [CAP, 8], [1, 128]]),
                            psT[:, :]),
                            reads=[psT[:, :]],
                            writes=[bass.AP(xngf, j * 128,
                                            [[8 * CAP, 128], [CAP, 8], [1, 128]])],
                            tag="xg_evac")

                gather_stage(0)
                for e in range(n_experts):
                    csc = cscg[e % 2]
                    sxu = sxug[e % 2]
                    pairs = (chunk_pair(w1b, w1t_d, e * D * I_FF, 8, I_FF, I_FF),
                             chunk_pair(w2b, w2t_d, e * D * I_FF, 8, I_FF, I_FF),
                             chunk_pair(w3b, w3t_d, e * I_FF * D, 11, 1024, D))
                    for wbn, (wb_, (_d, _s)) in enumerate(zip((w1b, w2b, w3b), pairs)):
                        p.op("pool", lambda h, _d=_d, _s=_s: h.dma_start(_d, _s),
                             writes=[wb_[:, :]], queue=f"q_w{wbn}b")
                    if e > 0:
                        gather_stage(e)
                    for f_ in pend_out:
                        f_()
                    pend_out.clear()
                    for ic in range(11):
                        for j0 in range(0, CAP, 512):
                            nw = min(512, CAP - j0)
                            gb = (psC, psA0)[ic % 2]
                            ub = (psD, psA1)[ic % 2]
                            sg_ = sgbs[ic % 2]
                            us_ = usbs[ic % 2]
                            for cc in range(8):
                                mm(gb[0:128, 0:nw],
                                   w1b[:, cc * I_FF + ic * 128: cc * I_FF + ic * 128 + 128],
                                   xngf[:, cc * CAP + j0: cc * CAP + j0 + nw],
                                   start=(cc == 0), stop=(cc == 7), tag="g")
                            for cc in range(8):
                                mm(ub[0:128, 0:nw],
                                   w2b[:, cc * I_FF + ic * 128: cc * I_FF + ic * 128 + 128],
                                   xngf[:, cc * CAP + j0: cc * CAP + j0 + nw],
                                   start=(cc == 0), stop=(cc == 7), tag="u")
                            act(sg_[:, 0:nw], gb[0:128, 0:nw], AF.Silu)
                            acopy(us_[:, 0:nw], ub[0:128, 0:nw])
                            tt(hsb[:, ic * CAP + j0: ic * CAP + j0 + nw],
                               sg_[:, 0:nw], us_[:, 0:nw], ALU.mult, tag="h")
                    for cc in range(8):
                        for j0 in range(0, CAP, 512):
                            nw = min(512, CAP - j0)
                            pst = psB0 if (cc % 2 == 0) else psB1
                            for ic in range(11):
                                mm(pst[0:128, 0:nw],
                                   w3b[:, ic * 1024 + cc * 128: ic * 1024 + cc * 128 + 128],
                                   hsb[:, ic * CAP + j0: ic * CAP + j0 + nw],
                                   start=(ic == 0), stop=(ic == 10), tag="o")
                            acopy(ofm[:, cc * CAP + j0: cc * CAP + j0 + nw],
                                  pst[0:128, 0:nw])
                    def out_stage(s=s, e=e, csc2=csc, sxu2=sxu):
                        for j in range(NJT):
                            for cc in range(8):
                                transpose(psT[:, cc * 128:(cc + 1) * 128],
                                          ofm[:, cc * CAP + j * 128: cc * CAP + (j + 1) * 128],
                                          idb[:, :])
                            o_ = ost[j % 2]
                            p.op("dve", lambda h, o_=o_, j=j: h.tensor_scalar(
                                o_[:, :], psT[:, :], csc2[:, j:j + 1], None, ALU.mult),
                                reads=[psT[:, :], csc2[:, j:j + 1]], writes=[o_[:, :]],
                                tag="oscale")
                            ysc_flat = bass.AP(ysc_d.tensor, 0,
                                               [[D, n_seqs * (S + 128)], [1, D]])
                            p.op("pool", lambda h, o_=o_, s=s, j=j:
                                 h.indirect_dma_start(
                                     ysc_flat, IOA(ap=sxu2[:, j:j + 1], axis=0),
                                     o_[:, :], None,
                                     element_offset=s * (S + 128) * D,
                                     compute_op=ALU.add),
                                 reads=[sxu2[:, j:j + 1], o_[:, :]],
                                 writes=[ysc_d[s]], queue="q_ysc")
                    pend_out.append(out_stage)
                for f_ in pend_out:
                    f_()
                pend_out.clear()
                p.barrier()

            for q4 in range(4):
                r0 = q4 * (S // 4)
                r1 = (q4 + 1) * (S // 4)
                p.op("sp", lambda h, s=s, r0=r0, r1=r1: h.dma_start(
                    out_d[s, r0:r1, :], ysc_d[s, r0:r1, :]),
                    reads=[ysc_d[s, r0:r1, :]],
                    writes=[out_d[s, r0:r1, :]], queue="q_fin")
            p.barrier()

    p.finalize()
    es.close()
    return nc



def _rel_bucket(rel_pos, num_buckets=32, max_distance=128):
    nb = num_buckets // 2
    buckets = (rel_pos > 0).astype(np.int32) * nb
    rel = np.abs(rel_pos)
    max_exact = nb // 2
    is_small = rel < max_exact
    rel_if_large = max_exact + (
        np.log(np.maximum(rel, 1).astype(np.float32) / max_exact)
        / math.log(max_distance / max_exact) * (nb - max_exact)
    ).astype(np.int32)
    rel_if_large = np.minimum(rel_if_large, nb - 1)
    return buckets + np.where(is_small, rel, rel_if_large)


def make_bias_skew(rel_emb, S):
    d = np.arange(-(S - 1), S)
    buckets = _rel_bucket(d)
    master = np.asarray(rel_emb)[buckets, :]
    SKW = 2 * S - 128
    j = np.arange(SKW)
    pp = np.arange(128)
    idx = pp[:, None] + (2 * S - 129) - j[None, :]
    skw = master[idx, :]
    return np.ascontiguousarray(skw.transpose(2, 0, 1)).astype(np.float32)


def make_inputs(x_shard, Wq, Wk, Wv, Wo, rel_emb, Wg, W1, W2, W3,
                Ws1, Ws2, Ws3, S):
    c = np.ascontiguousarray
    f = np.float32
    return {
        "x": c(np.asarray(x_shard), dtype=f),
        "wq": c(np.asarray(Wq), dtype=f),
        "wk": c(np.asarray(Wk), dtype=f),
        "wv": c(np.asarray(Wv), dtype=f),
        "wo": c(np.asarray(Wo), dtype=f),
        "wgt": c(np.asarray(Wg).T, dtype=f),
        "w1t": c(np.asarray(W1).transpose(0, 2, 1), dtype=f),
        "w2t": c(np.asarray(W2).transpose(0, 2, 1), dtype=f),
        "w3t": c(np.asarray(W3).transpose(0, 2, 1), dtype=f),
        "ws1t": c(np.asarray(Ws1).T, dtype=f),
        "ws2t": c(np.asarray(Ws2).T, dtype=f),
        "ws3t": c(np.asarray(Ws3).T, dtype=f),
        "bsk": make_bias_skew(rel_emb, S),
        "idb": np.eye(128, dtype=ml_dtypes.bfloat16),
        "idf": np.eye(128, dtype=np.float32),
        "triu": np.triu(np.ones((128, 128)), 1).astype(ml_dtypes.bfloat16),
        "lane8": np.tile(np.arange(8, dtype=np.float32), (128, 16)).copy(),
        "tokids": (np.arange(128, dtype=np.uint32)[:, None]
                   + 128 * np.arange(16, dtype=np.uint32)[None, :]).copy(),
        "oobi": np.full(8 * 768, 10 ** 9, dtype=np.uint32),
        "rowid": np.arange(128, dtype=np.float32)[:, None].copy(),
    }


def kernel(L0_attn_hidden_states, L0_attn_Wq, L0_attn_Wk, L0_attn_Wv,
           L0_attn_Wo, L0_attn_rel_emb, L0_ffn_Wg, L0_ffn_W1, L0_ffn_W2,
           L0_ffn_W3, L0_ffn_Ws1, L0_ffn_Ws2, L0_ffn_Ws3):
    x = np.asarray(L0_attn_hidden_states)
    B, S, _ = x.shape
    per = B // NCORES
    nc = build_core(n_seqs=per, S=S)
    in_maps = []
    for c in range(NCORES):
        in_maps.append(make_inputs(
            x[c * per:(c + 1) * per], L0_attn_Wq, L0_attn_Wk, L0_attn_Wv,
            L0_attn_Wo, L0_attn_rel_emb, L0_ffn_Wg, L0_ffn_W1, L0_ffn_W2,
            L0_ffn_W3, L0_ffn_Ws1, L0_ffn_Ws2, L0_ffn_Ws3, S))
    res = run_bass_kernel_spmd(nc, in_maps, core_ids=list(range(NCORES)))
    return np.concatenate([r["out"] for r in res.results], axis=0)


# version 29
# speedup vs baseline: 1.0592x; 1.0379x over previous
import contextlib
import math

import numpy as np
import ml_dtypes

import concourse.bass as bass
import concourse.mybir as mybir
from concourse.alu_op_type import AluOpType as ALU
from concourse.bass_utils import run_bass_kernel_spmd

IOA = bass.IndirectOffsetOnAxis

F32 = mybir.dt.float32
BF16 = mybir.dt.bfloat16
U32 = mybir.dt.uint32
AF = mybir.ActivationFunctionType
AX = mybir.AxisListType

D = 1024
NH = 16
DH = 64
I_FF = 1408
NE = 8
NCORES = 8

ENGINES = ("pe", "act", "dve", "pool", "sp")


def cdiv(a, b):
    return (a + b - 1) // b


class Prog:

    def __init__(self, nc):
        self.nc = nc
        self.per_eng = {e: [] for e in ENGINES}
        self.q_counts = {}
        self.writes = {}
        self.reads = {}
        self.waited = {e: {} for e in ENGINES}
        self.sems = {}
        self.es = None
        self.n_ops = 0

    def make_sems(self, es):
        self.es = es
        for e in ENGINES:
            self.sems[e] = es.enter_context(self.nc.semaphore(f"sem_{e}"))

    def _channel(self, q):
        if q not in self.q_counts:
            self.q_counts[q] = 0
            self.sems[q] = self.es.enter_context(self.nc.semaphore(f"sem_{q}"))

    @staticmethod
    def _foot(ap):
        t = ap.tensor
        name = t.name
        shape = list(t.shape)
        pitch = 1
        for s in shape[1:]:
            pitch *= s
        if len(shape) == 1:
            pitch = shape[0]
        off = ap.offset
        p0 = off // pitch
        c0 = off % pitch
        pn = 1
        cspan = 0
        for step, count in list(ap.ap):
            if count <= 1:
                continue
            if step > 0 and step % pitch == 0:
                pn += (count - 1) * (step // pitch)
            else:
                cspan += (count - 1) * abs(step)
        p1 = p0 + pn
        c1 = c0 + cspan + 1
        if "PSum" in type(t).__name__:
            esz = 2 if t.dtype == BF16 else 4
            bank = 2048 // esz
            c0 = (c0 // bank) * bank
            c1 = cdiv(c1, bank) * bank
        return (name, p0, p1, c0, c1)

    @staticmethod
    def _overlap(a, b):
        return a[1] < b[2] and b[1] < a[2] and a[3] < b[4] and b[3] < a[4]

    @staticmethod
    def _covers(a, b):
        return a[1] <= b[1] and a[2] >= b[2] and a[3] <= b[3] and a[4] >= b[4]

    def op(self, eng, fn, reads=(), writes=(), queue=None, tag=""):
        rf = [self._foot(a) for a in reads]
        wf = [self._foot(a) for a in writes]
        owner = queue if queue else eng
        edges = {}

        def add_edge(src, idx):
            if src == "pe" and owner == "pe":
                return
            if self.waited[eng].get(src, -1) >= idx:
                return
            if edges.get(src, -1) >= idx:
                return
            edges[src] = idx

        for f in rf:
            for w in self.writes.get(f[0], ()):
                if self._overlap(f, w["r"]):
                    add_edge(w["eng"], w["idx"])
        for f in wf:
            for w in self.writes.get(f[0], ()):
                if self._overlap(f, w["r"]):
                    add_edge(w["eng"], w["idx"])
            for r in self.reads.get(f[0], ()):
                if self._overlap(f, r["r"]):
                    add_edge(r["eng"], r["idx"])

        if queue:
            self._channel(queue)
            idx = self.q_counts[queue]
            self.q_counts[queue] += 1
        else:
            idx = self.n_ops
        self.n_ops += 1

        rec = {"fn": fn, "edges": edges, "queue": queue, "idx": idx,
               "inc": queue is not None, "tag": tag}
        self.per_eng[eng].append(rec)

        wd = self.waited[eng]
        for sKey, i in edges.items():
            if wd.get(sKey, -1) < i:
                wd[sKey] = i

        rec_eng = queue if queue else eng
        for f in wf:
            lst = self.writes.setdefault(f[0], [])
            lst[:] = [w for w in lst if not self._covers(f, w["r"])]
            lst.append({"r": f, "eng": rec_eng, "idx": idx})
            rl = self.reads.get(f[0])
            if rl:
                rl[:] = [r for r in rl if not self._covers(f, r["r"])]
        for f in rf:
            self.reads.setdefault(f[0], []).append(
                {"r": f, "eng": rec_eng, "idx": idx})
        return rec

    def barrier(self):
        lasts = {}
        for e in ENGINES:
            for rec in reversed(self.per_eng[e]):
                if rec["queue"] is None and rec["fn"] is not None:
                    lasts[e] = rec["idx"]
                    break
        for q, n in self.q_counts.items():
            if n:
                lasts[q] = n - 1
        for e in ENGINES:
            edges = {}
            for sKey, i in lasts.items():
                if sKey == e:
                    continue
                if self.waited[e].get(sKey, -1) >= i:
                    continue
                edges[sKey] = i
            if edges:
                self.per_eng[e].append({"fn": None, "edges": edges,
                                        "queue": None, "idx": None,
                                        "inc": False, "tag": "barrier"})
                for sKey, i in edges.items():
                    self.waited[e][sKey] = i

    def finalize(self):
        needed = {e: set() for e in ENGINES}
        for e in ENGINES:
            for rec in self.per_eng[e]:
                for sKey, i in rec["edges"].items():
                    if sKey in ENGINES:
                        needed[sKey].add(i)
        vals = {}
        for e in ENGINES:
            acc = 0
            v = {}
            for rec in self.per_eng[e]:
                if rec["queue"] is None and rec["fn"] is not None:
                    if rec["idx"] in needed[e]:
                        rec["inc"] = True
                        acc += 1
                    v[rec["idx"]] = acc
            vals[e] = v
        prog = self

        def emit(handle, eng):
            for rec in prog.per_eng[eng]:
                for sKey in sorted(rec["edges"], key=str):
                    i = rec["edges"][sKey]
                    if sKey in ENGINES:
                        handle.wait_ge(prog.sems[sKey], vals[sKey][i])
                    else:
                        handle.wait_ge(prog.sems[sKey], 16 * (i + 1))
                if rec["fn"] is None:
                    continue
                ins = rec["fn"](handle)
                if rec["queue"]:
                    ins.then_inc(prog.sems[rec["queue"]], 16)
                elif rec["inc"]:
                    if ins is None:
                        ins = handle.engine_nop() if hasattr(handle, "engine_nop") else handle.nop()
                    ins.then_inc(prog.sems[eng], 1)
            if eng == "sp":
                for q, n in sorted(prog.q_counts.items()):
                    if n:
                        handle.wait_ge(prog.sems[q], 16 * n)

        with prog.nc.Block() as block:
            @block.tensor
            def _(h):
                emit(h, "pe")

            @block.scalar
            def _(h):
                emit(h, "act")

            @block.vector
            def _(h):
                emit(h, "dve")

            @block.gpsimd
            def _(h):
                emit(h, "pool")

            @block.sync
            def _(h):
                emit(h, "sp")



def build_core(n_seqs=4, S=2048, n_experts=NE):
    KC = S // 128
    CAP = 768
    NJT = CAP // 128
    QHW = min(1024, S)
    NQH = S // QHW
    QGW = min(512, QHW)
    NQG = QHW // QGW
    NT = S // 128
    SKW = 2 * S - 128

    nc = bass.Bass("TRN2", target_bir_lowering=False)

    x_d = nc.dram_tensor("x", [n_seqs, S, D], F32, kind="ExternalInput").ap()
    wq_d = nc.dram_tensor("wq", [D, D], F32, kind="ExternalInput").ap()
    wk_d = nc.dram_tensor("wk", [D, D], F32, kind="ExternalInput").ap()
    wv_d = nc.dram_tensor("wv", [D, D], F32, kind="ExternalInput").ap()
    wo_d = nc.dram_tensor("wo", [D, D], F32, kind="ExternalInput").ap()
    wgt_d = nc.dram_tensor("wgt", [D, NE], F32, kind="ExternalInput").ap()
    w1t_d = nc.dram_tensor("w1t", [n_experts, D, I_FF], F32, kind="ExternalInput").ap()
    w2t_d = nc.dram_tensor("w2t", [n_experts, D, I_FF], F32, kind="ExternalInput").ap()
    w3t_d = nc.dram_tensor("w3t", [n_experts, I_FF, D], F32, kind="ExternalInput").ap()
    ws1t_d = nc.dram_tensor("ws1t", [D, I_FF], F32, kind="ExternalInput").ap()
    ws2t_d = nc.dram_tensor("ws2t", [D, I_FF], F32, kind="ExternalInput").ap()
    ws3t_d = nc.dram_tensor("ws3t", [I_FF, D], F32, kind="ExternalInput").ap()
    bsk_d = nc.dram_tensor("bsk", [NH, 128, SKW], F32, kind="ExternalInput").ap()
    triu_d = nc.dram_tensor("triu", [128, 128], BF16, kind="ExternalInput").ap()
    lane8_d = nc.dram_tensor("lane8", [128, 128], F32, kind="ExternalInput").ap()
    tokid_d = nc.dram_tensor("tokids", [128, 16], U32, kind="ExternalInput").ap()
    oobi_d = nc.dram_tensor("oobi", [NE * CAP], U32, kind="ExternalInput").ap()
    rowid_d = nc.dram_tensor("rowid", [128, 1], F32, kind="ExternalInput").ap()
    idb_d = nc.dram_tensor("idb", [128, 128], BF16, kind="ExternalInput").ap()
    idf_d = nc.dram_tensor("idf", [128, 128], F32, kind="ExternalInput").ap()
    out_d = nc.dram_tensor("out", [n_seqs, S, D], F32, kind="ExternalOutput").ap()
    xnd_d = nc.dram_tensor("xnd", [n_seqs, S, D], BF16).ap()
    cval_d = nc.dram_tensor("cvald", [n_seqs, NE * CAP], F32).ap()
    idx_d = nc.dram_tensor("idxd", [n_seqs, NE * CAP], U32).ap()
    ysc_d = nc.dram_tensor("yscd", [n_seqs, S + 128, D], F32).ap()

    es = contextlib.ExitStack()
    p = Prog(nc)
    p.make_sems(es)

    def sb(name, shape, dt, side=None):
        if side:
            return nc.sbuf_tensor(name, shape, dt, side=side)
        return nc.sbuf_tensor(name, shape, dt)

    idb = es.enter_context(sb("idb_s", [128, 128], BF16, side="right"))
    idf = es.enter_context(sb("idf_s", [128, 128], F32, side="right"))
    ones_f = es.enter_context(sb("ones_f", [1, 128], F32, side="right"))
    wgt_s = es.enter_context(sb("wgt_s", [128, NE * 8], BF16, side="right"))
    xn = es.enter_context(sb("xn", [128, 8 * S], BF16, side="right"))
    psA = es.enter_context(nc.psum_tensor("psA", [128, 1024], F32))
    psB = es.enter_context(nc.psum_tensor("psB", [128, 1024], F32))
    psC2 = es.enter_context(nc.psum_tensor("psC2", [128, 1024], F32))
    psE = es.enter_context(nc.psum_tensor("psE", [128, 512], F32))
    psT = es.enter_context(nc.psum_tensor("psT", [128, 1024], BF16))

    class BankView:

        def __init__(self, t, c0):
            self.t, self.c0 = t, c0

        def __getitem__(self, idx):
            r, c = idx
            cs = self.c0 + (c.start or 0)
            ce = self.c0 + c.stop
            return self.t[r, cs:ce]

    psC = BankView(psC2, 0)
    psD = BankView(psC2, 512)
    psA0 = BankView(psA, 0)
    psA1 = BankView(psA, 512)
    psB0 = BankView(psB, 0)
    psB1 = BankView(psB, 512)

    p.op("sp", lambda h: h.dma_start(idb[:, :], idb_d), writes=[idb[:, :]], queue="q_idb")
    p.op("sp", lambda h: h.dma_start(idf[:, :], idf_d), writes=[idf[:, :]], queue="q_idf")
    p.op("dve", lambda h: h.memset(ones_f[:, :], 1.0), writes=[ones_f[:, :]])

    def chunk_pair(dst, src_ap, base_off, n_ch, inner, src_row_pitch):
        t = src_ap.tensor
        d = bass.AP(dst, 0, [[n_ch * inner, 128], [inner, n_ch], [1, inner]])
        s_ = bass.AP(t, base_off,
                     [[src_row_pitch, 128], [128 * src_row_pitch, n_ch], [1, inner]])
        return d, s_

    def mm(out, lhsT, rhs, start, stop, tag=""):
        p.op("pe", lambda h: h.matmul(out, lhsT, rhs, start=start, stop=stop),
             reads=[lhsT, rhs], writes=[out], tag=tag)

    def transpose(out, in_, ident, tag=""):
        p.op("pe", lambda h: h.transpose(out, in_, ident),
             reads=[in_, ident], writes=[out], tag=tag)

    def act(out, in_, func, bias=None, scale=None, accum=None, tag=""):
        kw = {}
        if bias is not None:
            kw["bias"] = bias
        if scale is not None:
            kw["scale"] = scale
        if accum is not None:
            kw["accum_out"] = accum
        reads = [in_] + ([bias] if isinstance(bias, bass.AP) else [])
        wr = [out] + ([accum] if accum is not None else [])
        p.op("act", lambda h: h.activation(out, in_, func, **kw),
             reads=reads, writes=wr, tag=tag)

    def acopy(out, in_, tag=""):
        p.op("act", lambda h: h.copy(out, in_), reads=[in_], writes=[out], tag=tag)

    def tt(out, a, b, alu, tag=""):
        p.op("dve", lambda h: h.tensor_tensor(out, a, b, alu),
             reads=[a, b], writes=[out], tag=tag)

    def tsc(out, in_, s1, s2, op0, op1=None, tag=""):
        reads = [in_] + [q for q in (s1, s2) if isinstance(q, bass.AP)]
        if op1 is None:
            p.op("dve", lambda h: h.tensor_scalar(out, in_, s1, None, op0),
                 reads=reads, writes=[out], tag=tag)
        else:
            p.op("dve", lambda h: h.tensor_scalar(out, in_, s1, s2, op0, op1),
                 reads=reads, writes=[out], tag=tag)

    def stt(out, in0, scalar, in1, op0, op1, tag=""):
        p.op("dve", lambda h: h.scalar_tensor_tensor(out, in0, scalar, in1, op0, op1),
             reads=[in0, in1], writes=[out], tag=tag)

    def rmax(out, in_, tag=""):
        p.op("dve", lambda h: h.reduce_max(out, in_, axis=AX.X),
             reads=[in_], writes=[out], tag=tag)

    def recipo(out, in_, tag=""):
        p.op("dve", lambda h: h.reciprocal(out, in_),
             reads=[in_], writes=[out], tag=tag)

    _wgt_dst, _wgt_src = chunk_pair(wgt_s, wgt_d, 0, 8, NE, NE)
    p.op("pool", lambda h: h.dma_start(_wgt_dst, _wgt_src),
        writes=[wgt_s[:, :]], queue="q_wgt")
    triu = es.enter_context(sb("triu_s", [128, 128], BF16, side="right"))
    tokids = es.enter_context(sb("tokids_s", [128, 16], U32, side="right"))
    rowid = es.enter_context(sb("rowid_s", [128, 1], F32, side="right"))
    zrow = es.enter_context(sb("zrow_s", [128, D], F32, side="right"))
    p.op("sp", lambda h: h.dma_start(triu[:, :], triu_d), writes=[triu[:, :]], queue="q_triu")
    p.op("sp", lambda h: h.dma_start(tokids[:, :], tokid_d), writes=[tokids[:, :]], queue="q_tokids")
    p.op("sp", lambda h: h.dma_start(rowid[:, :], rowid_d),
         writes=[rowid[:, :]], queue="q_rowid")
    onesc_bf = es.enter_context(sb("onesc_bf", [128, 1], BF16, side="right"))
    padid = es.enter_context(sb("padid_s", [128, 1], F32, side="right"))
    p.op("dve", lambda h: h.memset(zrow[:, :], 0.0), writes=[zrow[:, :]])
    p.op("dve", lambda h: h.memset(onesc_bf[:, :], 1.0), writes=[onesc_bf[:, :]])
    tsc(padid[:, 0:1], rowid[:, 0:1], float(S), None, ALU.add)
    rbS = nc.gpsimd.alloc_register("rbS")
    rbC = nc.gpsimd.alloc_register("rbC")
    p.op("pool", lambda h: h.reg_mov(rbS, S - 1))
    p.op("pool", lambda h: h.reg_mov(rbC, NE * CAP - 1))

    for s in range(n_seqs):
        with contextlib.ExitStack() as s_es:
            with contextlib.ExitStack() as ln_es:
                xst = [ln_es.enter_context(sb(f"xst{s}_{i}", [128, D], F32))
                       for i in range(2)]
                xnt = [ln_es.enter_context(sb(f"xnt{s}_{i}", [128, D], BF16))
                       for i in range(2)]
                bn6 = ln_es.enter_context(sb(f"bn6_{s}", [128, 12], F32))
                bn2 = ln_es.enter_context(sb(f"bn2_{s}", [128, 2], F32))
                nmean = ln_es.enter_context(sb(f"nmean_{s}", [128, 1], F32))
                rstd = ln_es.enter_context(sb(f"rstd_{s}", [128, 1], F32))
                for t in range(NT):
                    xb = xst[t % 2]
                    nb = xnt[t % 2]
                    p.op("sp", lambda h, t=t, xb=xb, s=s: h.dma_start(
                        xb[:, :], x_d[s, t * 128:(t + 1) * 128, :]),
                        writes=[xb[:, :]], queue=f"q_xst{t % 2}")
                    p.op("dve", lambda h, xb=xb, bn6=bn6: h.bn_stats(bn6[:, 0:6], xb[:, 0:512]),
                         reads=[xb[:, 0:512]], writes=[bn6[:, 0:6]])
                    p.op("dve", lambda h, xb=xb, bn6=bn6: h.bn_stats(bn6[:, 6:12], xb[:, 512:1024]),
                         reads=[xb[:, 512:1024]], writes=[bn6[:, 6:12]])
                    p.op("dve", lambda h, bn2=bn2, bn6=bn6: h.bn_aggr(bn2[:, :], bn6[:, :]),
                         reads=[bn6[:, :]], writes=[bn2[:, :]])
                    tsc(nmean[:, :], bn2[:, 0:1], -1.0, None, ALU.mult)
                    tsc(rstd[:, :], bn2[:, 1:2], 1e-5, None, ALU.add)
                    act(rstd[:, :], rstd[:, :], AF.Ln)
                    tsc(rstd[:, :], rstd[:, :], -0.5, None, ALU.mult)
                    act(rstd[:, :], rstd[:, :], AF.Exp)
                    tsc(nb[:, :], xb[:, :], nmean[:, 0:1], rstd[:, 0:1],
                        ALU.add, ALU.mult)
                    p.op("sp", lambda h, t=t, nb=nb, s=s: h.dma_start(
                        xnd_d[s, t * 128:(t + 1) * 128, :], nb[:, :]),
                        reads=[nb[:, :]], writes=[xnd_d[s, t * 128:(t + 1) * 128, :]],
                        queue=f"q_xnd{t % 2}")
                    for cc in range(8):
                        transpose(psT[:, cc * 128:(cc + 1) * 128],
                                  nb[:, cc * 128:(cc + 1) * 128], idb[:, :])
                    p.op("act", lambda h, t=t: h.copy(
                        bass.AP(xn, t * 128, [[8 * S, 128], [S, 8], [1, 128]]),
                        psT[:, :]),
                        reads=[psT[:, :]],
                        writes=[bass.AP(xn, t * 128, [[8 * S, 128], [S, 8], [1, 128]])],
                        tag="xn_evac")
                p.barrier()

            ab_es = s_es.enter_context(contextlib.ExitStack())
            ao = ab_es.enter_context(sb(f"ao_{s}", [128, 8 * S], BF16))
            with contextlib.ExitStack() as at_es:
                qf = at_es.enter_context(sb(f"qf_{s}", [128, 4 * S], BF16))
                kf = at_es.enter_context(sb(f"kf_{s}", [128, 4 * S], BF16))
                vt = at_es.enter_context(sb(f"vt_{s}", [128, KC * 520], BF16))
                prb = at_es.enter_context(sb(f"prb_{s}", [128, KC * QHW], BF16))
                skw = [at_es.enter_context(sb(f"skw_{s}_{i}", [128, SKW], BF16))
                       for i in range(2)]
                wpr = [at_es.enter_context(sb(f"wpr_{s}_{i}", [128, 8 * 512], BF16))
                       for i in range(2)]
                recip = at_es.enter_context(sb(f"recip_{s}", [1, 1024], F32))
                rbc = at_es.enter_context(sb(f"rbc_{s}", [64, 1024], F32))
                aosum = at_es.enter_context(sb(f"aosum_{s}", [128, 1024], F32))

                p.op("dve", lambda h, vt=vt: h.memset(vt[:, :], 1.0), writes=[vt[:, :]])

                pending = []
                for g in range(2):
                    for wi, (wd, dst) in enumerate(((wk_d, kf), (wq_d, qf))):
                        wb = wpr[wi]
                        _d, _s = chunk_pair(wb, wd, g * 512, 8, 512, D)
                        p.op("pool", lambda h, _d=_d, _s=_s: h.dma_start(_d, _s),
                            writes=[wb[:, :]], queue=f"q_wpr{wi}")
                        for fl in range(4):
                            for n0 in range(0, S, 512):
                                nw = min(512, S - n0)
                                pst = psC if (fl % 2 == 0) else psD
                                for cc in range(8):
                                    mm(pst[0:128, 0:nw],
                                       wb[:, cc * 512 + fl * 128: cc * 512 + fl * 128 + 128],
                                       xn[:, cc * S + n0: cc * S + n0 + nw],
                                       start=(cc == 0), stop=(cc == 7))
                                acopy(dst[:, fl * S + n0: fl * S + n0 + nw],
                                      pst[0:128, 0:nw])
                    wb = wpr[0]
                    _d, _s = chunk_pair(wb, wv_d, g * 512, 8, 512, D)
                    p.op("pool", lambda h, _d=_d, _s=_s: h.dma_start(_d, _s),
                        writes=[wb[:, :]], queue="q_wpr0")
                    for t in range(NT):
                        pst = psC if (t % 2 == 0) else psD
                        for cc in range(8):
                            mm(pst[:, 0:512],
                               xn[:, cc * S + t * 128: cc * S + (t + 1) * 128],
                               wb[:, cc * 512:(cc + 1) * 512],
                               start=(cc == 0), stop=(cc == 7))
                        vap = bass.AP(vt, t * 520, [[KC * 520, 128], [65, 8], [1, 64]])
                        p.op("act", lambda h, pst=pst, vap=vap: h.copy(vap, pst[:, 0:512]),
                             reads=[pst[:, 0:512]],
                             writes=[vt[:, t * 520:(t + 1) * 520]])

                    for hl in range(8):
                        hh = g * 8 + hl
                        sk = skw[hh % 2]
                        p.op("pool", lambda h, hh=hh, sk=sk: h.dma_start(
                            sk[:, :], bsk_d[hh, :, :]),
                            writes=[sk[:, :]], queue=f"q_skw{hh % 2}")
                        pbase = 64 * (hl % 2)
                        fcol = (hl // 2) * S
                        aocol = (hh // 2) * S
                        for qh in range(NQH):
                            q0 = qh * QHW

                            PVG = KC

                            def pv_ap(qg, r0, r1, gw=QGW):
                                if qg == 0:
                                    return psE[r0:r1, 0:gw]
                                return psT[r0:r1, 0:2 * gw].bitcast(F32)

                            def pv_evac(qg, first):
                                dst = aosum[0:65, qg * QGW:(qg + 1) * QGW]
                                srcp = pv_ap(qg, 0, 65)
                                if first:
                                    p.op("dve", lambda h, dst=dst, srcp=srcp:
                                         h.tensor_copy(dst, srcp),
                                         reads=[srcp], writes=[dst], tag="pvev")
                                else:
                                    tt(dst, dst, srcp, ALU.add, tag="pvev")

                            def pv_step(kc):
                                for qg in range(NQG):
                                    mm(pv_ap(qg, 0, 65),
                                       vt[:, kc * 520 + hl * 65: kc * 520 + hl * 65 + 65],
                                       prb[:, kc * QHW + qg * QGW: kc * QHW + (qg + 1) * QGW],
                                       start=(kc % PVG == 0),
                                       stop=(kc % PVG == PVG - 1 or kc == KC - 1),
                                       tag="pv")
                                if kc % PVG == PVG - 1 or kc == KC - 1:
                                    for qg in range(NQG):
                                        pv_evac(qg, kc < PVG)

                            flush_at = min(4, KC - 1)
                            for kc in range(KC):
                                if kc == flush_at:
                                    for f_ in pending:
                                        f_()
                                    pending.clear()
                                sc = (psA, psB, psC2)[kc % 3]
                                boff = q0 + (S - 128) - kc * 128
                                for j in range(0, QHW, 512):
                                    nw = min(512, QHW - j)
                                    mm(sc[:, j:j + nw], idb[:, :],
                                       sk[:, boff + j: boff + j + nw],
                                       start=True, stop=False, tag="biasmm")
                                for j in range(0, QHW, 512):
                                    nw = min(512, QHW - j)
                                    mm(sc[:, j:j + nw],
                                       kf[pbase:pbase + 64,
                                          fcol + kc * 128: fcol + (kc + 1) * 128],
                                       qf[pbase:pbase + 64,
                                          fcol + q0 + j: fcol + q0 + j + nw],
                                       start=False, stop=True, tag="scores")
                                act(prb[:, kc * QHW: (kc + 1) * QHW],
                                    sc[:, 0:QHW], AF.Exp, tag="exp")
                                if kc > 0:
                                    pv_step(kc - 1)
                            pv_step(KC - 1)

                            for qg in range(NQG):
                                qs = qg * QGW
                                p.op("dve", lambda h, qs=qs, recip=recip, aosum=aosum: h.reciprocal(
                                    recip[0:1, qs:qs + QGW],
                                    aosum[64:65, qs:qs + QGW]),
                                    reads=[aosum[64:65, qs:qs + QGW]],
                                    writes=[recip[0:1, qs:qs + QGW]], tag="recip")

                                def norm_part2(qs=qs, pbase=pbase, aocol=aocol,
                                               q0=q0, recip=recip, aosum=aosum,
                                               rbc=rbc):
                                    bk = psC2[0:64, qs:qs + QGW]
                                    mm(bk, ones_f[0:1, 0:64],
                                       recip[0:1, qs:qs + QGW],
                                       start=True, stop=True, tag="bcast")
                                    acopy(rbc[0:64, qs:qs + QGW], bk)
                                    tt(ao[pbase:pbase + 64,
                                          aocol + q0 + qs: aocol + q0 + qs + QGW],
                                       aosum[0:64, qs:qs + QGW],
                                       rbc[0:64, qs:qs + QGW],
                                       ALU.mult, tag="ao")
                                pending.append(norm_part2)
                for f_ in pending:
                    f_()
                pending.clear()
                p.barrier()

            yf_es = s_es.enter_context(contextlib.ExitStack())
            yf = yf_es.enter_context(sb(f"yf_{s}", [128, 8 * S], F32, side="right"))
            with contextlib.ExitStack() as op_es:
                wob = op_es.enter_context(sb(f"wob_{s}", [128, 8 * 1024], BF16))
                _d, _s = chunk_pair(wob, wo_d, 0, 8, 1024, D)
                p.op("pool", lambda h, _d=_d, _s=_s: h.dma_start(_d, _s),
                    writes=[wob[:, :]], queue="q_wob")
                for fo in range(8):
                    for n0 in range(0, S, 512):
                        nw = min(512, S - n0)
                        pst = psC if (fo % 2 == 0) else psD
                        for cc in range(8):
                            mm(pst[:, 0:nw],
                               wob[:, cc * 1024 + fo * 128: cc * 1024 + fo * 128 + 128],
                               ao[:, cc * S + n0: cc * S + n0 + nw],
                               start=(cc == 0), stop=(cc == 7))
                        acopy(yf[:, fo * S + n0: fo * S + n0 + nw], pst[:, 0:nw])
                p.barrier()
            ab_es.close()

            with contextlib.ExitStack() as b_es:
                hw = min(1024, S // 2)
                NHALF = S // hw
                w1b = b_es.enter_context(sb(f"w1b_{s}", [128, 8 * I_FF], BF16))
                w2b = b_es.enter_context(sb(f"w2b_{s}", [128, 8 * I_FF], BF16))
                w3b = b_es.enter_context(sb(f"w3b_{s}", [128, 11 * 1024], BF16))
                for wbn, (wb_, wd_, nch, inner) in enumerate(
                        ((w1b, ws1t_d, 8, I_FF), (w2b, ws2t_d, 8, I_FF),
                         (w3b, ws3t_d, 11, 1024))):
                    for hn, (c0, c1) in enumerate(((0, nch // 2),
                                                   (nch // 2, nch))):
                        _d = bass.AP(wb_, c0 * inner,
                                     [[nch * inner, 128],
                                      [inner, c1 - c0], [1, inner]])
                        _s = bass.AP(wd_.tensor, c0 * 128 * inner,
                                     [[inner, 128],
                                      [128 * inner, c1 - c0], [1, inner]])
                        p.op("pool", lambda h, _d=_d, _s=_s: h.dma_start(_d, _s),
                             writes=[wb_[:, c0 * inner:c1 * inner]],
                             queue=f"q_w{wbn}{'bc'[hn]}")
                with contextlib.ExitStack() as gt_es:
                    W = NT * NE
                    glA = gt_es.enter_context(sb(f"glA_{s}", [128, W], F32))
                    glE = gt_es.enter_context(sb(f"glE_{s}", [128, W], F32))
                    gpA = gt_es.enter_context(sb(f"gpA_{s}", [128, W], F32))
                    m1kA = gt_es.enter_context(sb(f"m1kA_{s}", [128, W], F32))
                    m2kA = gt_es.enter_context(sb(f"m2kA_{s}", [128, W], F32))
                    pexA = gt_es.enter_context(sb(f"pexA_{s}", [128, W], F32))
                    cmbA = gt_es.enter_context(sb(f"cmbA_{s}", [128, W], F32))
                    mskA = gt_es.enter_context(sb(f"mskA_{s}", [128, W], F32))
                    mbfA = gt_es.enter_context(sb(f"mbfA_{s}", [128, W], BF16))
                    tmpA = gt_es.enter_context(sb(f"tmpA_{s}", [128, W], F32))
                    rnkA = gt_es.enter_context(sb(f"rnkA_{s}", [128, W], F32))
                    rnkF = gt_es.enter_context(sb(f"rnkF_{s}", [128, W], F32))
                    gmax = gt_es.enter_context(sb(f"gmax_{s}", [128, NT], F32))
                    gsum = gt_es.enter_context(sb(f"gsum_{s}", [128, NT], F32))
                    gmx1 = gt_es.enter_context(sb(f"gmx1_{s}", [128, NT], F32))
                    gmx2 = gt_es.enter_context(sb(f"gmx2_{s}", [128, NT], F32))
                    totsb = gt_es.enter_context(sb(f"totsb_{s}", [1, W], F32))
                    cf = gt_es.enter_context(sb(f"cf_{s}", [1, W], F32))
                    rselA = gt_es.enter_context(sb(f"rselA_{s}", [128, 2 * NT], F32))
                    eselA = gt_es.enter_context(sb(f"eselA_{s}", [128, 2 * NT], F32))
                    vselA = gt_es.enter_context(sb(f"vselA_{s}", [128, 2 * NT], F32))
                    ovfA = gt_es.enter_context(sb(f"ovfA_{s}", [128, 2 * NT], F32))
                    flfA = gt_es.enter_context(sb(f"flfA_{s}", [128, 2 * NT], F32))
                    fluA = gt_es.enter_context(sb(f"fluA_{s}", [128, 2 * NT], U32))
                    lane8 = gt_es.enter_context(sb(f"lane8_{s}", [128, W], F32))
                    lsrc = bass.AP(lane8_d.tensor, 0, [[128, 128], [1, W]])
                    p.op("sp", lambda h, lane8=lane8, lsrc=lsrc: h.dma_start(
                        lane8[:, :], lsrc),
                        writes=[lane8[:, :]], queue="q_lane8")

                    p.op("sp", lambda h, s=s: h.dma_start(
                        idx_d[s, :], oobi_d),
                        writes=[idx_d[s, :]], queue="q_oobi")
                    cvout = bass.AP(cval_d.tensor, s * NE * CAP,
                                    [[48, 128], [1, 48]])
                    p.op("sp", lambda h, cvout=cvout: h.dma_start(
                        cvout, zrow[:, 0:48]),
                        reads=[zrow[:, 0:48]], writes=[cval_d[s, :]],
                        queue="q_cvz")

                    def rsum3(out, tns, tag=""):
                        in3 = bass.AP(tns, 0, [[W, 128], [NE, NT], [1, NE]])
                        p.op("dve", lambda h, out=out, in3=in3: h.reduce_sum(
                            out, in3, axis=AX.X),
                            reads=[tns[:, 0:W]], writes=[out], tag=tag)

                    def rmax3(out, tns, tag=""):
                        in3 = bass.AP(tns, 0, [[W, 128], [NE, NT], [1, NE]])
                        p.op("dve", lambda h, out=out, in3=in3: h.reduce_max(
                            out, in3, axis=AX.X),
                            reads=[tns[:, 0:W]], writes=[out], tag=tag)

                    def bc0(t16):
                        return bass.AP(t16, 0, [[NT, 128], [1, NT], [0, NE]])

                    def v3(tns):
                        return bass.AP(tns, 0, [[W, 128], [NE, NT], [1, NE]])

                    def ttb(out, a3, b3, alu, rd, tag=""):
                        p.op("dve", lambda h, out=out, a3=a3, b3=b3, alu=alu:
                             h.tensor_tensor(out, a3, b3, alu),
                             reads=rd, writes=[out[:, 0:W] if hasattr(out, "tensor") is False else out],
                             tag=tag)

                    for t in range(NT):
                        for cc in range(8):
                            mm(psD[0:128, t * NE:(t + 1) * NE],
                               xn[:, cc * S + t * 128: cc * S + (t + 1) * 128],
                               wgt_s[:, cc * 8: cc * 8 + NE],
                               start=(cc == 0), stop=(cc == 7))
                    acopy(glA[:, 0:W], psD[0:128, 0:W])
                    rmax3(gmax[:, 0:NT], glA)
                    _o, _a, _b = v3(glA), v3(glA), bc0(gmax)
                    p.op("dve", lambda h, _o=_o, _a=_a, _b=_b: h.tensor_tensor(
                        _o, _a, _b, ALU.subtract),
                        reads=[glA[:, 0:W], gmax[:, 0:NT]], writes=[glA[:, 0:W]])
                    act(glE[:, 0:W], glA[:, 0:W], AF.Exp, tag="gexp")
                    rsum3(gsum[:, 0:NT], glE)
                    recipo(gsum[:, 0:NT], gsum[:, 0:NT])
                    _o, _a, _b = v3(gpA), v3(glE), bc0(gsum)
                    p.op("dve", lambda h, _o=_o, _a=_a, _b=_b: h.tensor_tensor(
                        _o, _a, _b, ALU.mult),
                        reads=[glE[:, 0:W], gsum[:, 0:NT]], writes=[gpA[:, 0:W]])
                    rmax3(gmx1[:, 0:NT], gpA)
                    _o, _a, _b = v3(m1kA), v3(gpA), bc0(gmx1)
                    p.op("dve", lambda h, _o=_o, _a=_a, _b=_b: h.tensor_tensor(
                        _o, _a, _b, ALU.is_ge),
                        reads=[gpA[:, 0:W], gmx1[:, 0:NT]], writes=[m1kA[:, 0:W]])
                    stt(pexA[:, 0:W], m1kA[:, 0:W], -1e30, gpA[:, 0:W],
                        ALU.mult, ALU.add)
                    rmax3(gmx2[:, 0:NT], pexA)
                    _o, _a, _b = v3(cmbA), v3(gpA), bc0(gmx2)
                    p.op("dve", lambda h, _o=_o, _a=_a, _b=_b: h.tensor_tensor(
                        _o, _a, _b, ALU.is_ge),
                        reads=[gpA[:, 0:W], gmx2[:, 0:NT]], writes=[cmbA[:, 0:W]])
                    tt(cmbA[:, 0:W], cmbA[:, 0:W], gpA[:, 0:W], ALU.mult)
                    tsc(mskA[:, 0:W], cmbA[:, 0:W], 0.0, None, ALU.is_gt)
                    tt(m2kA[:, 0:W], mskA[:, 0:W], m1kA[:, 0:W], ALU.subtract)
                    p.op("dve", lambda h, mbfA=mbfA, mskA=mskA:
                         h.tensor_copy(mbfA[:, 0:W], mskA[:, 0:W]),
                         reads=[mskA[:, 0:W]], writes=[mbfA[:, 0:W]])
                    for t in range(NT):
                        mm(psC[0:128, t * NE:(t + 1) * NE], triu[:, :],
                           mbfA[:, t * NE:(t + 1) * NE],
                           start=True, stop=True, tag="rank")
                    mm(psC[0:1, 128:128 + W], onesc_bf[:, 0:1], mbfA[:, 0:W],
                       start=True, stop=True, tag="tot")
                    acopy(rnkA[:, 0:W], psC[0:128, 0:W])
                    acopy(totsb[0:1, 0:W], psC[0:1, 128:128 + W])
                    p.op("dve", lambda h, cf=cf: h.memset(cf[0:1, 0:NE], 0.0),
                         writes=[cf[0:1, 0:NE]])
                    for t in range(1, NT):
                        tt(cf[0:1, t * NE:(t + 1) * NE],
                           cf[0:1, (t - 1) * NE:t * NE],
                           totsb[0:1, (t - 1) * NE:t * NE], ALU.add)
                    mm(psC[0:128, 256:256 + W], ones_f[0:1, 0:128], cf[0:1, 0:W],
                       start=True, stop=True, tag="carrybc")
                    tt(rnkF[:, 0:W], rnkA[:, 0:W], psC[0:128, 256:256 + W], ALU.add)
                    for k, mk in ((0, m1kA), (1, m2kA)):
                        tt(tmpA[:, 0:W], rnkF[:, 0:W], mk[:, 0:W], ALU.mult)
                        rsum3(rselA[:, k * NT:(k + 1) * NT], tmpA)
                        tt(tmpA[:, 0:W], lane8[:, 0:W], mk[:, 0:W], ALU.mult)
                        rsum3(eselA[:, k * NT:(k + 1) * NT], tmpA)
                        tt(tmpA[:, 0:W], cmbA[:, 0:W], mk[:, 0:W], ALU.mult)
                        rsum3(vselA[:, k * NT:(k + 1) * NT], tmpA)
                    tsc(ovfA[:, 0:2 * NT], rselA[:, 0:2 * NT], float(CAP), None,
                        ALU.is_ge)
                    tsc(flfA[:, 0:2 * NT], eselA[:, 0:2 * NT], float(CAP), None,
                        ALU.mult)
                    tt(flfA[:, 0:2 * NT], flfA[:, 0:2 * NT], rselA[:, 0:2 * NT],
                       ALU.add)
                    stt(flfA[:, 0:2 * NT], ovfA[:, 0:2 * NT], 1e9,
                        flfA[:, 0:2 * NT], ALU.mult, ALU.add)
                    p.op("dve", lambda h, fluA=fluA, flfA=flfA:
                         h.tensor_copy(fluA[:, 0:2 * NT], flfA[:, 0:2 * NT]),
                         reads=[flfA[:, 0:2 * NT]], writes=[fluA[:, 0:2 * NT]])
                    iout = bass.AP(idx_d.tensor, 0,
                                   [[1, n_seqs * NE * CAP], [1, 1]])
                    cvo = bass.AP(cval_d.tensor, 0,
                                  [[1, n_seqs * NE * CAP], [1, 1]])
                    ieoff = s * NE * CAP
                    for t in range(NT):
                        for k in range(2):
                            c = k * NT + t
                            p.op("pool", lambda h, t=t, c=c, iout=iout,
                                 ieoff=ieoff, fluA=fluA: h.indirect_dma_start(
                                     iout, IOA(ap=fluA[:, c:c + 1], axis=0),
                                     tokids[:, t:t + 1], None,
                                     element_offset=ieoff,
                                     bounds_check=rbC, oob_is_err=False),
                                reads=[fluA[:, c:c + 1], tokids[:, t:t + 1]],
                                writes=[idx_d[s, :]], queue=f"q_isc{k}")
                            p.op("pool", lambda h, c=c, cvo=cvo, ieoff=ieoff,
                                 fluA=fluA, vselA=vselA: h.indirect_dma_start(
                                     cvo, IOA(ap=fluA[:, c:c + 1], axis=0),
                                     vselA[:, c:c + 1], None,
                                     element_offset=ieoff,
                                     bounds_check=rbC, oob_is_err=False),
                                reads=[fluA[:, c:c + 1], vselA[:, c:c + 1]],
                                writes=[cval_d[s, :]], queue=f"q_vsc{k}")
                    p.barrier()

                hsb = b_es.enter_context(sb(f"hsb_{s}", [128, 11 * 1024], BF16))
                sgb = b_es.enter_context(sb(f"sgb_{s}", [128, 512], BF16))
                usb = b_es.enter_context(sb(f"usb_{s}", [128, 512], BF16))
                sgb2 = sgb
                usb2 = usb

                sgbs = [sgb, sgb2]
                usbs = [usb, usb2]
                for hf in range(NHALF):
                    h0 = hf * hw
                    for ic in range(11):
                        for j0 in range(0, hw, 512):
                            nw = min(512, hw - j0)
                            gb = (psC, psA0)[ic % 2]
                            ub = (psD, psA1)[ic % 2]
                            sg_ = sgbs[ic % 2]
                            us_ = usbs[ic % 2]
                            for cc in range(8):
                                mm(gb[0:128, 0:nw],
                                   w1b[:, cc * I_FF + ic * 128: cc * I_FF + ic * 128 + 128],
                                   xn[:, cc * S + h0 + j0: cc * S + h0 + j0 + nw],
                                   start=(cc == 0), stop=(cc == 7), tag="g")
                            for cc in range(8):
                                mm(ub[0:128, 0:nw],
                                   w2b[:, cc * I_FF + ic * 128: cc * I_FF + ic * 128 + 128],
                                   xn[:, cc * S + h0 + j0: cc * S + h0 + j0 + nw],
                                   start=(cc == 0), stop=(cc == 7), tag="u")
                            act(sg_[:, 0:nw], gb[0:128, 0:nw], AF.Silu)
                            acopy(us_[:, 0:nw], ub[0:128, 0:nw])
                            tt(hsb[:, ic * hw + j0: ic * hw + j0 + nw],
                               sg_[:, 0:nw], us_[:, 0:nw], ALU.mult, tag="h")
                    for cc in range(8):
                        for j0 in range(0, hw, 512):
                            nw = min(512, hw - j0)
                            pst = psB0 if (cc % 2 == 0) else psB1
                            for ic in range(11):
                                mm(pst[0:128, 0:nw],
                                   w3b[:, ic * 1024 + cc * 128: ic * 1024 + cc * 128 + 128],
                                   hsb[:, ic * hw + j0: ic * hw + j0 + nw],
                                   start=(ic == 0), stop=(ic == 10), tag="o")
                            tt(yf[:, cc * S + h0 + j0: cc * S + h0 + j0 + nw],
                               yf[:, cc * S + h0 + j0: cc * S + h0 + j0 + nw],
                               pst[0:128, 0:nw], ALU.add, tag="yacc")
                p.barrier()

                ost = [b_es.enter_context(sb(f"ost_{s}_{i}", [128, D], F32))
                       for i in range(2)]
                with contextlib.ExitStack() as fl_es:
                    xt2 = [fl_es.enter_context(sb(f"xt2_{s}_{i}", [128, D], F32))
                           for i in range(2)]
                    ob = ost
                    p.op("sp", lambda h, s=s: h.dma_start(
                        ysc_d[s, S:S + 128, :], zrow[:, :]),
                        reads=[zrow[:, :]], writes=[ysc_d[s, S:S + 128, :]],
                        queue="q_zr")
                    for t in range(NT):
                        xb = xt2[t % 2]
                        o = ob[t % 2]
                        p.op("sp", lambda h, t=t, xb=xb, s=s: h.dma_start(
                            xb[:, :], x_d[s, t * 128:(t + 1) * 128, :]),
                            writes=[xb[:, :]], queue=f"q_xt2{t % 2}")
                        for fo in range(8):
                            dst = (psA[:, fo * 128:(fo + 1) * 128] if fo < 4
                                   else psB[:, (fo - 4) * 128:(fo - 3) * 128])
                            transpose(dst,
                                      yf[:, fo * S + t * 128: fo * S + (t + 1) * 128],
                                      idf[:, :])
                        tt(o[:, 0:512], psA[:, 0:512], xb[:, 0:512], ALU.add)
                        tt(o[:, 512:1024], psB[:, 0:512], xb[:, 512:1024], ALU.add)
                        p.op("sp", lambda h, t=t, o=o, s=s: h.dma_start(
                            ysc_d[s, t * 128:(t + 1) * 128, :], o[:, :]),
                            reads=[o[:, :]], writes=[ysc_d[s, t * 128:(t + 1) * 128, :]],
                            queue=f"q_ob{t % 2}")
                    p.barrier()
                yf_es.close()

                xngt = [b_es.enter_context(sb(f"xngt_{s}_{i}", [128, D], BF16))
                        for i in range(2)]
                xngf = b_es.enter_context(sb(f"xngf_{s}", [128, 8 * CAP], BF16))
                ofm = b_es.enter_context(sb(f"ofm_{s}", [128, 8 * CAP], BF16))
                idxe = b_es.enter_context(sb(f"idxe_{s}", [128, NJT], U32))
                idxf = b_es.enter_context(sb(f"idxf_{s}", [128, NJT], F32))
                vld = b_es.enter_context(sb(f"vld_{s}", [128, NJT], F32))
                cscg = [b_es.enter_context(sb(f"csc_{s}_{i}", [128, NJT], F32))
                        for i in range(2)]
                sxf = b_es.enter_context(sb(f"sxf_{s}", [128, NJT], F32))
                sxug = [b_es.enter_context(sb(f"sxu_{s}_{i}", [128, NJT], U32))
                        for i in range(2)]
                pend_out = []
                t9 = b_es.enter_context(sb(f"t9_{s}", [128, NJT], F32))

                def gather_stage(e):
                    csc = cscg[e % 2]
                    sxu = sxug[e % 2]
                    iap = bass.AP(idx_d.tensor, (s * NE + e) * CAP,
                                  [[NJT, 128], [1, NJT]])
                    p.op("sp", lambda h, iap=iap, idxe=idxe: h.dma_start(
                        idxe[:, :], iap),
                        reads=[idx_d[s, :]], writes=[idxe[:, :]], queue="q_idxe")
                    cap_ = bass.AP(cval_d.tensor, (s * NE + e) * CAP,
                                   [[NJT, 128], [1, NJT]])
                    p.op("sp", lambda h, cap_=cap_, csc=csc: h.dma_start(
                        csc[:, :], cap_),
                        reads=[cval_d[s, :]], writes=[csc[:, :]], queue=f"q_csc{e % 2}")
                    for j in range(NJT):
                        gt_ = xngt[j % 2]
                        xnd_flat = bass.AP(xnd_d.tensor, 0,
                                           [[D, n_seqs * S], [1, D]])
                        p.op("pool", lambda h, gt_=gt_, s=s, j=j, idxe=idxe,
                             xnd_flat=xnd_flat: h.indirect_dma_start(
                                 gt_[:, :], None, xnd_flat,
                                 IOA(ap=idxe[:, j:j + 1], axis=0),
                                 element_offset=s * S * D,
                                 bounds_check=rbS, oob_is_err=False),
                             reads=[idxe[:, j:j + 1], xnd_d[s]],
                             writes=[gt_[:, :]], queue=f"q_xg{j % 2}")
                        p.op("dve", lambda h, j=j, idxe=idxe, idxf=idxf:
                             h.tensor_copy(idxf[:, j:j + 1], idxe[:, j:j + 1]),
                             reads=[idxe[:, j:j + 1]], writes=[idxf[:, j:j + 1]])
                        tsc(vld[:, j:j + 1], idxf[:, j:j + 1], float(S - 1), None,
                            ALU.is_le)
                        tsc(t9[:, j:j + 1], vld[:, j:j + 1], -1.0, 1.0,
                            ALU.mult, ALU.add)
                        tt(t9[:, j:j + 1], t9[:, j:j + 1], padid[:, 0:1], ALU.mult)
                        tt(sxf[:, j:j + 1], idxf[:, j:j + 1], vld[:, j:j + 1],
                           ALU.mult)
                        tt(sxf[:, j:j + 1], sxf[:, j:j + 1], t9[:, j:j + 1],
                           ALU.add)
                        p.op("dve", lambda h, j=j, sxu=sxu, sxf=sxf:
                             h.tensor_copy(sxu[:, j:j + 1], sxf[:, j:j + 1]),
                             reads=[sxf[:, j:j + 1]], writes=[sxu[:, j:j + 1]])
                        for cc in range(8):
                            transpose(psT[:, cc * 128:(cc + 1) * 128],
                                      gt_[:, cc * 128:(cc + 1) * 128], idb[:, :])
                        p.op("act", lambda h, j=j, xngf=xngf: h.copy(
                            bass.AP(xngf, j * 128, [[8 * CAP, 128], [CAP, 8], [1, 128]]),
                            psT[:, :]),
                            reads=[psT[:, :]],
                            writes=[bass.AP(xngf, j * 128,
                                            [[8 * CAP, 128], [CAP, 8], [1, 128]])],
                            tag="xg_evac")

                gather_stage(0)
                for e in range(n_experts):
                    csc = cscg[e % 2]
                    sxu = sxug[e % 2]
                    pairs = (chunk_pair(w1b, w1t_d, e * D * I_FF, 8, I_FF, I_FF),
                             chunk_pair(w2b, w2t_d, e * D * I_FF, 8, I_FF, I_FF),
                             chunk_pair(w3b, w3t_d, e * I_FF * D, 11, 1024, D))
                    for wbn, (wb_, (_d, _s)) in enumerate(zip((w1b, w2b, w3b), pairs)):
                        p.op("pool", lambda h, _d=_d, _s=_s: h.dma_start(_d, _s),
                             writes=[wb_[:, :]], queue=f"q_w{wbn}b")
                    if e > 0:
                        gather_stage(e)
                    for f_ in pend_out:
                        f_()
                    pend_out.clear()
                    for ic in range(11):
                        for j0 in range(0, CAP, 512):
                            nw = min(512, CAP - j0)
                            gb = (psC, psA0)[ic % 2]
                            ub = (psD, psA1)[ic % 2]
                            sg_ = sgbs[ic % 2]
                            us_ = usbs[ic % 2]
                            for cc in range(8):
                                mm(gb[0:128, 0:nw],
                                   w1b[:, cc * I_FF + ic * 128: cc * I_FF + ic * 128 + 128],
                                   xngf[:, cc * CAP + j0: cc * CAP + j0 + nw],
                                   start=(cc == 0), stop=(cc == 7), tag="g")
                            for cc in range(8):
                                mm(ub[0:128, 0:nw],
                                   w2b[:, cc * I_FF + ic * 128: cc * I_FF + ic * 128 + 128],
                                   xngf[:, cc * CAP + j0: cc * CAP + j0 + nw],
                                   start=(cc == 0), stop=(cc == 7), tag="u")
                            act(sg_[:, 0:nw], gb[0:128, 0:nw], AF.Silu)
                            acopy(us_[:, 0:nw], ub[0:128, 0:nw])
                            tt(hsb[:, ic * CAP + j0: ic * CAP + j0 + nw],
                               sg_[:, 0:nw], us_[:, 0:nw], ALU.mult, tag="h")
                    for cc in range(8):
                        for j0 in range(0, CAP, 512):
                            nw = min(512, CAP - j0)
                            pst = psB0 if (cc % 2 == 0) else psB1
                            for ic in range(11):
                                mm(pst[0:128, 0:nw],
                                   w3b[:, ic * 1024 + cc * 128: ic * 1024 + cc * 128 + 128],
                                   hsb[:, ic * CAP + j0: ic * CAP + j0 + nw],
                                   start=(ic == 0), stop=(ic == 10), tag="o")
                            acopy(ofm[:, cc * CAP + j0: cc * CAP + j0 + nw],
                                  pst[0:128, 0:nw])
                    def out_stage(s=s, e=e, csc2=csc, sxu2=sxu):
                        for j in range(NJT):
                            for cc in range(8):
                                transpose(psT[:, cc * 128:(cc + 1) * 128],
                                          ofm[:, cc * CAP + j * 128: cc * CAP + (j + 1) * 128],
                                          idb[:, :])
                            o_ = ost[j % 2]
                            p.op("dve", lambda h, o_=o_, j=j: h.tensor_scalar(
                                o_[:, :], psT[:, :], csc2[:, j:j + 1], None, ALU.mult),
                                reads=[psT[:, :], csc2[:, j:j + 1]], writes=[o_[:, :]],
                                tag="oscale")
                            ysc_flat = bass.AP(ysc_d.tensor, 0,
                                               [[D, n_seqs * (S + 128)], [1, D]])
                            p.op("pool", lambda h, o_=o_, s=s, j=j:
                                 h.indirect_dma_start(
                                     ysc_flat, IOA(ap=sxu2[:, j:j + 1], axis=0),
                                     o_[:, :], None,
                                     element_offset=s * (S + 128) * D,
                                     compute_op=ALU.add),
                                 reads=[sxu2[:, j:j + 1], o_[:, :]],
                                 writes=[ysc_d[s]], queue="q_ysc")
                    pend_out.append(out_stage)
                for f_ in pend_out:
                    f_()
                pend_out.clear()
                p.barrier()

            for q4 in range(4):
                r0 = q4 * (S // 4)
                r1 = (q4 + 1) * (S // 4)
                p.op("sp", lambda h, s=s, r0=r0, r1=r1: h.dma_start(
                    out_d[s, r0:r1, :], ysc_d[s, r0:r1, :]),
                    reads=[ysc_d[s, r0:r1, :]],
                    writes=[out_d[s, r0:r1, :]], queue="q_fin")
            p.barrier()

    p.finalize()
    es.close()
    return nc



def _rel_bucket(rel_pos, num_buckets=32, max_distance=128):
    nb = num_buckets // 2
    buckets = (rel_pos > 0).astype(np.int32) * nb
    rel = np.abs(rel_pos)
    max_exact = nb // 2
    is_small = rel < max_exact
    rel_if_large = max_exact + (
        np.log(np.maximum(rel, 1).astype(np.float32) / max_exact)
        / math.log(max_distance / max_exact) * (nb - max_exact)
    ).astype(np.int32)
    rel_if_large = np.minimum(rel_if_large, nb - 1)
    return buckets + np.where(is_small, rel, rel_if_large)


def make_bias_skew(rel_emb, S):
    d = np.arange(-(S - 1), S)
    buckets = _rel_bucket(d)
    master = np.asarray(rel_emb)[buckets, :]
    SKW = 2 * S - 128
    j = np.arange(SKW)
    pp = np.arange(128)
    idx = pp[:, None] + (2 * S - 129) - j[None, :]
    skw = master[idx, :]
    return np.ascontiguousarray(skw.transpose(2, 0, 1)).astype(np.float32)


def make_inputs(x_shard, Wq, Wk, Wv, Wo, rel_emb, Wg, W1, W2, W3,
                Ws1, Ws2, Ws3, S):
    c = np.ascontiguousarray
    f = np.float32
    return {
        "x": c(np.asarray(x_shard), dtype=f),
        "wq": c(np.asarray(Wq), dtype=f),
        "wk": c(np.asarray(Wk), dtype=f),
        "wv": c(np.asarray(Wv), dtype=f),
        "wo": c(np.asarray(Wo), dtype=f),
        "wgt": c(np.asarray(Wg).T, dtype=f),
        "w1t": c(np.asarray(W1).transpose(0, 2, 1), dtype=f),
        "w2t": c(np.asarray(W2).transpose(0, 2, 1), dtype=f),
        "w3t": c(np.asarray(W3).transpose(0, 2, 1), dtype=f),
        "ws1t": c(np.asarray(Ws1).T, dtype=f),
        "ws2t": c(np.asarray(Ws2).T, dtype=f),
        "ws3t": c(np.asarray(Ws3).T, dtype=f),
        "bsk": make_bias_skew(rel_emb, S),
        "idb": np.eye(128, dtype=ml_dtypes.bfloat16),
        "idf": np.eye(128, dtype=np.float32),
        "triu": np.triu(np.ones((128, 128)), 1).astype(ml_dtypes.bfloat16),
        "lane8": np.tile(np.arange(8, dtype=np.float32), (128, 16)).copy(),
        "tokids": (np.arange(128, dtype=np.uint32)[:, None]
                   + 128 * np.arange(16, dtype=np.uint32)[None, :]).copy(),
        "oobi": np.full(8 * 768, 10 ** 9, dtype=np.uint32),
        "rowid": np.arange(128, dtype=np.float32)[:, None].copy(),
    }


def kernel(L0_attn_hidden_states, L0_attn_Wq, L0_attn_Wk, L0_attn_Wv,
           L0_attn_Wo, L0_attn_rel_emb, L0_ffn_Wg, L0_ffn_W1, L0_ffn_W2,
           L0_ffn_W3, L0_ffn_Ws1, L0_ffn_Ws2, L0_ffn_Ws3):
    x = np.asarray(L0_attn_hidden_states)
    B, S, _ = x.shape
    per = B // NCORES
    nc = build_core(n_seqs=per, S=S)
    in_maps = []
    for c in range(NCORES):
        in_maps.append(make_inputs(
            x[c * per:(c + 1) * per], L0_attn_Wq, L0_attn_Wk, L0_attn_Wv,
            L0_attn_Wo, L0_attn_rel_emb, L0_ffn_Wg, L0_ffn_W1, L0_ffn_W2,
            L0_ffn_W3, L0_ffn_Ws1, L0_ffn_Ws2, L0_ffn_Ws3, S))
    res = run_bass_kernel_spmd(nc, in_maps, core_ids=list(range(NCORES)))
    return np.concatenate([r["out"] for r in res.results], axis=0)
